# version 09
# baseline (speedup 1.0000x reference)
import numpy as np
from contextlib import ExitStack

import concourse.bass as bass
import concourse.mybir as mybir
from concourse.bass_utils import run_bass_kernel_spmd

F32 = mybir.dt.float32
I32 = mybir.dt.int32
U32 = mybir.dt.uint32
AF = mybir.ActivationFunctionType
ALU = mybir.AluOpType
AX = mybir.AxisListType

NCORES = 8
T = 2048
NTOK = 2 * T
D = 1024
TT = 512
NTILE = NTOK // TT
NH, DK, DV, CH = 4, 128, 256, 64
H0, HKV, HD = 16, 4, 64
HID, BLK, BANK = 4096, 64, 8



def _prep(inputs):
    g = {}
    c = np.ascontiguousarray
    g["wx1T"] = c(inputs["L0_attn_Wx1"].T)
    g["w2nT"] = c(np.transpose(np.reshape(np.asarray(inputs["L0_attn_Wx2"]), (D, 5, 32)), (1, 2, 0)))
    g["mu_x"] = c(np.asarray(inputs["L0_attn_mu_x"]).reshape(D, 1))
    g["x_biasT"] = c(np.asarray(inputs["L0_attn_x_bias"]).T)
    g["wrT"] = c(inputs["L0_attn_Wr"].T)
    g["wkT"] = c(inputs["L0_attn_Wk"].T)
    g["wvT"] = c(inputs["L0_attn_Wv"].T)
    g["wgT"] = c(inputs["L0_attn_Wg"].T)
    g["ww1T"] = c(inputs["L0_attn_Ww1"].T)
    g["ww2T"] = c(inputs["L0_attn_Ww2"].T)
    g["bw2"] = c(np.asarray(inputs["L0_attn_bw2"]).reshape(512, 1))
    g["bonusT"] = c(np.asarray(inputs["L0_attn_bonus"]).T)
    g["ln_w"] = c(np.asarray(inputs["L0_attn_ln_w"]).reshape(1, 256))
    g["ln_b"] = c(np.asarray(inputs["L0_attn_ln_b"]).reshape(1, 256))
    g["woT"] = c(inputs["L0_attn_Wo"].T)
    P2H = [0, 4, 1, 5, 2, 6, 3, 7, 8, 12, 9, 13, 10, 14, 11, 15]
    wq = np.asarray(inputs["L0_ffn_wq"])
    wqT = wq.T.reshape(D, H0, HD)
    g["wqT2"] = c(wqT[:, P2H, :].reshape(D, D))
    g["wkT2"] = c(inputs["L0_ffn_wk"].T)
    g["wvT2"] = c(inputs["L0_ffn_wv"].T)
    woT = np.asarray(inputs["L0_ffn_wo"]).T.reshape(H0, HD, D)
    g["woT2"] = c(woT[P2H].reshape(D, D))
    g["q_gain_row"] = c(np.asarray(inputs["L0_ffn_q_gain"])[P2H].reshape(1, H0))
    g["k_gain_row"] = c(np.asarray(inputs["L0_ffn_k_gain"]).reshape(1, HKV))
    tpos = np.arange(T, dtype=np.float64)
    inv = 1.0 / (10000.0 ** (np.arange(0, HD, 2, dtype=np.float64) / HD))
    fr = np.outer(tpos, inv)
    g["rope_cos"] = c(np.cos(fr).astype(np.float32))
    g["rope_sin"] = c(np.sin(fr).astype(np.float32))
    g["tri_neg"] = c(np.triu(np.full((128, 128), -1e30, np.float32), 1))
    def mix_perm(n_in):
        idx = np.empty(n_in * BLK, np.int64)
        out = []
        for e in range(BANK):
            for gg in range(n_in):
                for cc in range(BLK):
                    idx[gg * BLK + cc] = cc * n_in + (gg + e) % n_in
            out.append(idx.copy())
        return out
    fcp = mix_perm(16)
    pjp = mix_perm(64)
    fc_mix = np.asarray(inputs["L0_ffn_fc_mix"])
    pj_mix = np.asarray(inputs["L0_ffn_pj_mix"])
    g["fc_mixB"] = c(np.stack([fc_mix[e][:, fcp[e]].reshape(32, 128, D) for e in range(BANK)]))
    g["pj_mixTB"] = c(np.stack([np.ascontiguousarray(pj_mix[e][:, pjp[e]].T).reshape(32, 128, D)
                                for e in range(BANK)]))
    g["fc_s1f"] = c(np.asarray(inputs["L0_ffn_fc_s1"]).reshape(BANK * D, BLK))
    g["fc_s2f"] = c(np.asarray(inputs["L0_ffn_fc_s2"]).reshape(BANK * HID, BLK))
    g["pj_s1f"] = c(np.asarray(inputs["L0_ffn_pj_s1"]).reshape(BANK * HID, BLK))
    g["pj_s2f"] = c(np.asarray(inputs["L0_ffn_pj_s2"]).reshape(BANK * D, BLK))
    g["fc_logits_row"] = c(np.asarray(inputs["L0_ffn_fc_logits"]).reshape(1, BANK))
    g["pj_logits_row"] = c(np.asarray(inputs["L0_ffn_pj_logits"]).reshape(1, BANK))
    for part in ("R", "I"):
        G1 = np.asarray(inputs[f"L1_ffn_G1_{part}"])
        G2 = np.asarray(inputs[f"L1_ffn_G2_{part}"])
        Ab = np.stack([np.broadcast_to(G1[0, :, cc, :].T[:, None, :, None],
                                       (32, 16, 32, 16)).reshape(512, 512) for cc in range(4)])
        Bb = np.stack([np.broadcast_to(G2[cc, :, 0, :].T[None, :, None, :],
                                       (32, 16, 32, 16)).reshape(512, 512) for cc in range(4)])
        g[f"Abig{part}"] = c(Ab)
        g[f"Bbig{part}"] = c(Bb)
    g["bias_ri_row"] = c(np.concatenate([np.asarray(inputs["L1_ffn_bias_real"]),
                                         np.asarray(inputs["L1_ffn_bias_imag"])]).reshape(1, D))
    g["ident"] = np.eye(128, dtype=np.float32)
    g["mask64"] = c(np.triu(np.ones((CH, CH), np.float32), 1))
    for k in g:
        g[k] = np.asarray(g[k], dtype=np.float32)
    return g



class Sem:
    def __init__(self, ctx, nc, name):
        self.h = ctx.enter_context(nc.semaphore(name))
        self.n = 0

    def bump(self, k=1):
        self.n += k
        return self.n


ENGS = ("sync", "tensor", "vector", "scalar", "gpsimd")


class H:
    def __init__(self, nc, ctx, block):
        self.nc = nc
        self.ctx = ctx
        self.block = block
        self.pfx = ""
        self.ld = Sem(ctx, nc, "ld")
        self.st = Sem(ctx, nc, "st")
        self.gd = Sem(ctx, nc, "gd")
        self.esem = {e: Sem(ctx, nc, "es_" + e) for e in ENGS}
        self.waited = {e: {} for e in ENGS}

    def tok(self):
        t = {e: self.esem[e].n for e in ENGS}
        t["ld"] = self.ld.n
        t["st"] = self.st.n
        t["gd"] = self.gd.n
        return t

    def sb(self, name, shape, dt=F32):
        return self.ctx.enter_context(self.nc.sbuf_tensor("s_" + self.pfx + name, shape, dt))

    def ps(self, name, shape, dt=F32):
        return self.ctx.enter_context(self.nc.psum_tensor("p_" + self.pfx + name, shape, dt))


def ST(h, eng, body, deps=None):
    if deps is None:
        tgt = h.tok()
        if eng != "sync":
            tgt.pop("st", None)
    else:
        tgt = {}
        for d in deps:
            for k, v in d.items():
                if v > tgt.get(k, 0):
                    tgt[k] = v
    w = h.waited[eng]
    sems = dict(h.esem)
    sems["ld"] = h.ld
    sems["st"] = h.st
    sems["gd"] = h.gd
    holder = {}

    def _f(e):
        for k, v in tgt.items():
            if v <= 0:
                continue
            if v > w.get(k, 0):
                e.wait_ge(sems[k].h, v)
                w[k] = v
        holder["last"] = body(e)

    getattr(h.block, eng)(_f)
    last = holder.get("last")
    if last is not None:
        last.then_inc(h.esem[eng].h, 1)
        h.esem[eng].bump()
    return h.tok()


def tm_to_fm(h, tm_tiles, fm, pA, pB, ident, nchunk=8, fm_stride=TT, sub0=0):
    for s in range(len(tm_tiles)):
        def tbody(t, s=s):
            last = None
            for c in range(nchunk):
                bank = pA if c < 4 else pB
                last = t.transpose(bank[:, (c % 4) * 128:(c % 4) * 128 + 128],
                                   tm_tiles[s][:, c * 128:(c + 1) * 128], ident[:])
            return last
        ST(h, "tensor", tbody)

        def vbody(v, s=s):
            if fm_stride == 128 and nchunk == 8:
                col0 = (sub0 + s) * 128
                v.tensor_copy(out=fm[:, col0:col0 + 512], in_=pA[:, 0:512])
                return v.tensor_copy(out=fm[:, col0 + 512:col0 + 1024], in_=pB[:, 0:512])
            last = None
            for c in range(nchunk):
                bank = pA if c < 4 else pB
                col = c * fm_stride + (sub0 + s) * 128
                last = v.tensor_copy(out=fm[:, col:col + 128],
                                     in_=bank[:, (c % 4) * 128:(c % 4) * 128 + 128])
            return last
        ST(h, "vector", vbody)


def emit_ln(h, xres, out_ap, trash, nfeat=1024, eps=1e-5, stats=None, dep=None):
    s_ = stats
    dd = None
    if dep is not None:
        dd = [dep] + ([h._ln_last] if getattr(h, "_ln_last", None) else [])
    t1 = ST(h, "vector", lambda v: v.tensor_scalar(out=trash, in0=xres, scalar1=-1.0 / nfeat,
                                                   scalar2=None, op0=ALU.mult, op1=ALU.add,
                                                   accum_out=s_[:, 1:2]), deps=dd)
    t2 = ST(h, "scalar", lambda sc: sc.activation(out=trash, in_=xres, func=AF.Square,
                                                  bias=s_[:, 1:2], scale=1.0, accum_out=s_[:, 2:3]),
            deps=[t1] if dep is not None else None)
    t3 = ST(h, "vector", lambda v: v.tensor_scalar(out=s_[:, 3:4], in0=s_[:, 2:3],
                                                   scalar1=1.0 / nfeat, scalar2=float(eps),
                                                   op0=ALU.mult, op1=ALU.add),
            deps=[t2] if dep is not None else None)
    t4 = ST(h, "scalar", lambda sc: sc.activation(out=s_[:, 4:5], in_=s_[:, 3:4], func=AF.Sqrt),
            deps=[t3] if dep is not None else None)
    t5 = ST(h, "vector", lambda v: v.reciprocal(out=s_[:, 5:6], in_=s_[:, 4:5]),
            deps=[t4] if dep is not None else None)
    t6 = ST(h, "vector", lambda v: v.tensor_scalar(out=out_ap, in0=xres, scalar1=s_[:, 1:2],
                                                   scalar2=s_[:, 5:6], op0=ALU.add, op1=ALU.mult),
            deps=[t5] if dep is not None else None)
    if dep is not None:
        h._ln_last = t6
    return t6



def emit_p0b(h, io, debug=False):
    woT = h.sb("b_woT", [128, 8 * 1024])
    bonusT = h.sb("b_bonusT", [128, 4])
    ident = h.sb("b_ident", [128, 128])
    mask64 = h.sb("b_mask64", [64, 64])
    ones1 = h.sb("b_ones1", [1, 128])
    ones128 = h.sb("b_ones128", [128, 1])
    lnrow = h.sb("b_lnrow", [1, 512])
    lnw_bc = h.sb("b_lnwbc", [128, 256])
    lnb_bc = h.sb("b_lnbbc", [128, 256])

    rfm = h.sb("b_rfm", [128, 4 * TT])
    kfm = h.sb("b_kfm", [128, 4 * TT])
    wfm = h.sb("b_wfm", [128, 4 * TT])
    x0tm = [h.sb(f"b_x0tm{s}", [128, D]) for s in range(4)]
    vch = h.sb("b_vch", [64, D])
    gch = h.sb("b_gch", [64, D])
    Sst = [h.sb(f"b_S{hh}", [128, DV]) for hh in range(NH)]
    otm = h.sb("b_otm", [128, 4 * D])
    ofm = h.sb("b_ofm", [128, 8 * TT])
    ewb4_ = [h.sb("b_ewb4a", [128, 4 * CH]), h.sb("b_ewb4b", [128, 4 * CH])]
    cumb4_ = [h.sb("b_cumb4a", [128, 4 * CH]), h.sb("b_cumb4b", [128, 4 * CH])]
    aexp4_ = [h.sb("b_aexp4a", [128, 4 * CH]), h.sb("b_aexp4b", [128, 4 * CH])]
    bexp4_ = [h.sb("b_bexp4a", [128, 4 * CH]), h.sb("b_bexp4b", [128, 4 * CH])]
    pc4_ = [h.sb("b_pc4a", [128, 4]), h.sb("b_pc4b", [128, 4])]
    rt4_ = [h.sb("b_rt4a", [128, 4 * CH]), h.sb("b_rt4b", [128, 4 * CH])]
    kt4_ = [h.sb("b_kt4a", [128, 4 * CH]), h.sb("b_kt4b", [128, 4 * CH])]
    rbk4_ = [h.sb("b_rbk4a", [128, 4 * CH]), h.sb("b_rbk4b", [128, 4 * CH])]
    vch2 = h.sb("b_vch2", [64, D])
    gch2 = h.sb("b_gch2", [64, D])
    msb4 = h.sb("b_msb4", [64, 4 * CH])
    ktT4 = h.sb("b_ktT4", [64, 4 * 128])
    odb4 = h.sb("b_odb4", [64, 4 * DV])
    osb4 = h.sb("b_osb4", [64, 4 * DV])
    onb4 = h.sb("b_onb4", [64, 4 * DV])
    sig4 = h.sb("b_sig4", [64, 4 * DV])
    tr644 = h.sb("b_tr644", [64, 4 * DV])
    tr128 = h.sb("b_tr128", [128, D])
    xres = h.sb("b_xres", [128, D])
    stats = h.sb("b_stats", [128, 8])
    dbgb = h.sb("b_dbgb", [128, 1024]) if debug else None
    st644 = h.sb("b_st644", [64, 32])
    stmp4 = h.sb("b_stmp4", [128, 4 * DV])

    pPool = h.ps("b_pPool", [128, 4096])
    pT = pPool[:, 0:512]
    pP = pPool[:, 512:1024]
    pW0 = pPool[:, 1024:1536]
    pW1 = pPool[:, 1536:2048]

    def loads(sync):
        for c in range(8):
            sync.dma_start(out=woT[:, c * 1024:(c + 1) * 1024], in_=io["woT"][c * 128:(c + 1) * 128, :]).then_inc(h.ld.h, 16)
            h.ld.bump(16)
        for nm, dst in [("bonusT", bonusT), ("ident", ident), ("mask64", mask64)]:
            sync.dma_start(out=dst[:], in_=io[nm][:]).then_inc(h.ld.h, 16)
            h.ld.bump(16)
        sync.dma_start(out=lnrow[:, 0:256], in_=io["ln_w"][:]).then_inc(h.ld.h, 16)
        sync.dma_start(out=lnrow[:, 256:512], in_=io["ln_b"][:]).then_inc(h.ld.h, 16)
        h.ld.bump(32)
        return None
    ST(h, "sync", loads)

    def init(v):
        v.memset(ones1[:], 1.0)
        v.memset(ones128[:], 1.0)
        return v.engine_nop()
    ST(h, "vector", init)

    def bc(t):
        t.matmul(pT[:, 0:256], ones1[:], lnrow[:, 0:256], start=True, stop=True)
        return t.matmul(pT[:, 256:512], ones1[:], lnrow[:, 256:512], start=True, stop=True)
    ST(h, "tensor", bc)

    def bc2(v):
        v.tensor_copy(out=lnw_bc[:], in_=pT[:, 0:256])
        return v.tensor_copy(out=lnb_bc[:], in_=pT[:, 256:512])
    ST(h, "vector", bc2)

    for seq in range(2):
        def zs(v):
            last = None
            for hh in range(NH):
                last = v.memset(Sst[hh][:], 0.0)
            return last
        ST(h, "vector", zs)

        for it in range(T // TT):
            t0 = seq * T + it * TT

            def ldt(sync, t0=t0):
                for hh in range(4):
                    sync.dma_start(out=rfm[:, hh * TT:(hh + 1) * TT], in_=io["rsc"][hh * 128:(hh + 1) * 128, t0:t0 + TT]).then_inc(h.ld.h, 16)
                    sync.dma_start(out=kfm[:, hh * TT:(hh + 1) * TT], in_=io["ksc"][hh * 128:(hh + 1) * 128, t0:t0 + TT]).then_inc(h.ld.h, 16)
                    sync.dma_start(out=wfm[:, hh * TT:(hh + 1) * TT], in_=io["wsc"][hh * 128:(hh + 1) * 128, t0:t0 + TT]).then_inc(h.ld.h, 16)
                    h.ld.bump(48)
                for s in range(4):
                    sync.dma_start(out=x0tm[s][:], in_=io["x0"][t0 + s * 128:t0 + (s + 1) * 128, :]).then_inc(h.ld.h, 16)
                    h.ld.bump(16)
                return None
            ST(h, "sync", ldt)

            ctok0 = h.tok()
            t_fr, t_bk, t_ldc = {}, {}, {}

            def chunk_front(ch, t0=t0):
                p = ch % 2
                tl = ch * CH
                ewb4, cumb4, aexp4, bexp4 = ewb4_[p], cumb4_[p], aexp4_[p], bexp4_[p]
                pc4, rt4, kt4, rbk4 = pc4_[p], rt4_[p], kt4_[p], rbk4_[p]
                vchp = vch if p == 0 else vch2
                gchp = gch if p == 0 else gch2
                rsl = [rfm[:, hh * TT + tl:hh * TT + tl + CH] for hh in range(NH)]
                ksl = [kfm[:, hh * TT + tl:hh * TT + tl + CH] for hh in range(NH)]
                wsl = [wfm[:, hh * TT + tl:hh * TT + tl + CH] for hh in range(NH)]
                guard = t_bk.get(ch - 2, ctok0)

                def ldc(sync, t0=t0, tl=tl, vchp=vchp, gchp=gchp):
                    sync.dma_start(out=vchp[:], in_=io["vsc"][t0 + tl:t0 + tl + CH, :]).then_inc(h.ld.h, 16)
                    sync.dma_start(out=gchp[:], in_=io["gsc"][t0 + tl:t0 + tl + CH, :]).then_inc(h.ld.h, 16)
                    h.ld.bump(32)
                    return None
                ST(h, "sync", ldc, deps=[guard] + ([t_ldc[ch - 1]] if ch >= 1 else []))
                tld = h.tok()
                t_ldc[ch] = tld

                def s1(sc):
                    last = None
                    for hh in range(NH):
                        last = sc.activation(out=ewb4[:, hh * CH:(hh + 1) * CH], in_=wsl[hh], func=AF.Exp)
                    return last
                ta = ST(h, "scalar", s1, deps=[guard, tld])

                def s2(v):
                    last = None
                    for hh in range(NH):
                        last = v.tensor_tensor_scan(out=cumb4[:, hh * CH:(hh + 1) * CH],
                                                    data0=ewb4[:, hh * CH:(hh + 1) * CH],
                                                    data1=ewb4[:, hh * CH:(hh + 1) * CH],
                                                    initial=0.0, op0=ALU.add, op1=ALU.bypass)
                    return last
                tb = ST(h, "vector", s2, deps=[ta])
                tc = ST(h, "vector", lambda v: v.tensor_tensor(out=aexp4[:], in0=ewb4[:], in1=cumb4[:],
                                                               op=ALU.subtract), deps=[tb])

                def s4(sc):
                    sc.activation(out=aexp4[:], in_=aexp4[:], func=AF.Exp)
                    sc.activation(out=bexp4[:], in_=cumb4[:], func=AF.Exp)
                    last = None
                    for hh in range(NH):
                        last = sc.activation(out=pc4[:, hh:hh + 1],
                                             in_=cumb4[:, hh * CH + CH - 1:hh * CH + CH], func=AF.Exp,
                                             scale=-1.0)
                    return last
                td = ST(h, "scalar", s4, deps=[tc])

                def s5(v):
                    last = None
                    for hh in range(NH):
                        v.tensor_tensor(out=rt4[:, hh * CH:(hh + 1) * CH], in0=rsl[hh],
                                        in1=aexp4[:, hh * CH:(hh + 1) * CH], op=ALU.mult)
                        v.tensor_tensor(out=kt4[:, hh * CH:(hh + 1) * CH], in0=ksl[hh],
                                        in1=bexp4[:, hh * CH:(hh + 1) * CH], op=ALU.mult)
                        last = v.scalar_tensor_tensor(out=rbk4[:, hh * CH:(hh + 1) * CH], in0=rsl[hh],
                                                      scalar=bonusT[:, hh:hh + 1], in1=ksl[hh],
                                                      op0=ALU.mult, op1=ALU.mult)
                    return last
                t_fr[ch] = ST(h, "vector", s5, deps=[td])

            def chunk_back(ch):
                p = ch % 2
                base = (ch % 2) * 64
                sub = ch // 2
                pc4, rt4, kt4, rbk4 = pc4_[p], rt4_[p], kt4_[p], rbk4_[p]
                vchp = vch if p == 0 else vch2
                gchp = gch if p == 0 else gch2
                vsl = [vchp[:, hh * DV:(hh + 1) * DV] for hh in range(NH)]
                pSm_ = [pPool[0:64, hh * 512:(hh + 1) * 512] for hh in range(NH)]
                pO_ = [pPool[(hh % 2) * 64:(hh % 2) * 64 + 64,
                             2048 + (hh // 2) * 256:2048 + (hh // 2) * 256 + 256] for hh in range(NH)]
                pKV_ = [pPool[:, 2560 + hh * 256:2560 + (hh + 1) * 256] for hh in range(NH)]
                prev = t_bk.get(ch - 1, ctok0)

                def s6(t):
                    last = None
                    for hh in range(NH):
                        t.matmul(pSm_[hh][:, 0:64], kt4[:, hh * CH:(hh + 1) * CH],
                                 rt4[:, hh * CH:(hh + 1) * CH], start=True, stop=True)
                        t.matmul(pSm_[hh][:, 64:65], rbk4[:, hh * CH:(hh + 1) * CH],
                                 ones128[:], start=True, stop=True)
                        last = t.transpose(pSm_[hh][:, 128:256], kt4[:, hh * CH:(hh + 1) * CH], ident[:])
                    return last
                t6 = ST(h, "tensor", s6, deps=[t_fr[ch], prev])

                def s7(v):
                    last = None
                    for hh in range(NH):
                        v.tensor_tensor(out=msb4[:, hh * CH:(hh + 1) * CH], in0=pSm_[hh][0:64, 0:64],
                                        in1=mask64[:], op=ALU.mult)
                        v.tensor_copy(out=ktT4[:, hh * 128:(hh + 1) * 128], in_=pSm_[hh][0:64, 128:256])
                        last = v.tensor_scalar(out=odb4[:, hh * DV:(hh + 1) * DV], in0=vsl[hh],
                                               scalar1=pSm_[hh][0:64, 64:65], scalar2=None, op0=ALU.mult)
                    return last
                t7 = ST(h, "vector", s7, deps=[t6])

                def s8(t):
                    last = None
                    for hh in range(NH):
                        t.matmul(pO_[hh], rt4[:, hh * CH:(hh + 1) * CH], Sst[hh][:], start=True, stop=False)
                        t.matmul(pO_[hh], msb4[:, hh * CH:(hh + 1) * CH], vsl[hh], start=False, stop=True)
                        last = t.matmul(pKV_[hh], ktT4[:, hh * 128:(hh + 1) * 128], vsl[hh],
                                        start=True, stop=True)
                    return last
                t8 = ST(h, "tensor", s8, deps=[t7])

                def s9(v):
                    last = None
                    for hh in range(NH):
                        last = v.tensor_tensor(out=osb4[:, hh * DV:(hh + 1) * DV], in0=pO_[hh],
                                               in1=odb4[:, hh * DV:(hh + 1) * DV], op=ALU.add)
                    return last
                t9 = ST(h, "vector", s9, deps=[t8])

                def s10(v):
                    last = None
                    for hh in range(NH):
                        last = v.tensor_scalar(out=tr644[:, hh * DV:(hh + 1) * DV],
                                               in0=osb4[:, hh * DV:(hh + 1) * DV],
                                               scalar1=-1.0 / DV, scalar2=None, op0=ALU.mult, op1=ALU.add,
                                               accum_out=st644[:, hh * 8 + 1:hh * 8 + 2])
                    return last
                t10 = ST(h, "vector", s10, deps=[t9])

                def s11(sc):
                    last = None
                    for hh in range(NH):
                        last = sc.activation(out=tr644[:, hh * DV:(hh + 1) * DV],
                                             in_=osb4[:, hh * DV:(hh + 1) * DV], func=AF.Square,
                                             bias=st644[:, hh * 8 + 1:hh * 8 + 2], scale=1.0,
                                             accum_out=st644[:, hh * 8 + 2:hh * 8 + 3])
                    return last
                t11 = ST(h, "scalar", s11, deps=[t10])

                def s12(v):
                    last = None
                    for hh in range(NH):
                        last = v.tensor_scalar(out=st644[:, hh * 8 + 3:hh * 8 + 4],
                                               in0=st644[:, hh * 8 + 2:hh * 8 + 3],
                                               scalar1=1.0 / DV, scalar2=1e-5, op0=ALU.mult, op1=ALU.add)
                    return last
                t12 = ST(h, "vector", s12, deps=[t11])

                def s13(sc, gchp=gchp):
                    sc.activation(out=sig4[:], in_=gchp[0:64, :], func=AF.Exp, scale=-1.0)
                    last = None
                    for hh in range(NH):
                        last = sc.activation(out=st644[:, hh * 8 + 4:hh * 8 + 5],
                                             in_=st644[:, hh * 8 + 3:hh * 8 + 4], func=AF.Sqrt)
                    return last
                t13 = ST(h, "scalar", s13, deps=[t12])

                def s14(v):
                    v.tensor_scalar(out=sig4[:], in0=sig4[:], scalar1=1.0, scalar2=None, op0=ALU.add)
                    last = None
                    for hh in range(NH):
                        last = v.reciprocal(out=st644[:, hh * 8 + 5:hh * 8 + 6],
                                            in_=st644[:, hh * 8 + 4:hh * 8 + 5])
                    return last
                t14 = ST(h, "vector", s14, deps=[t13])

                def s15(v):
                    v.reciprocal(out=sig4[:], in_=sig4[:])
                    last = None
                    for hh in range(NH):
                        last = v.tensor_scalar(out=onb4[:, hh * DV:(hh + 1) * DV],
                                               in0=osb4[:, hh * DV:(hh + 1) * DV],
                                               scalar1=st644[:, hh * 8 + 1:hh * 8 + 2],
                                               scalar2=st644[:, hh * 8 + 5:hh * 8 + 6],
                                               op0=ALU.add, op1=ALU.mult)
                    return last
                t15 = ST(h, "vector", s15, deps=[t14])

                def s16(v, gchp=gchp):
                    lnw_b4 = lnw_bc[0:64, :].unsqueeze(1).broadcast_to([64, 4, 256])
                    onv = onb4[:].rearrange("p (h d) -> p h d", d=256)
                    v.tensor_tensor(out=onv, in0=onv, in1=lnw_b4, op=ALU.mult)
                    return v.tensor_tensor(out=sig4[:], in0=sig4[:], in1=gchp[0:64, :], op=ALU.mult)
                t16 = ST(h, "vector", s16, deps=[t15])

                t17 = ST(h, "vector", lambda v: v.tensor_tensor(
                    out=onb4[:].rearrange("p (h d) -> p h d", d=256),
                    in0=onb4[:].rearrange("p (h d) -> p h d", d=256),
                    in1=lnb_bc[0:64, :].unsqueeze(1).broadcast_to([64, 4, 256]), op=ALU.add), deps=[t16])

                t18 = ST(h, "vector", lambda v, base=base, sub=sub: v.tensor_tensor(
                    out=otm[base:base + 64, sub * D:(sub + 1) * D],
                    in0=onb4[:], in1=sig4[:], op=ALU.mult), deps=[t17])

                def s19(v):
                    last = None
                    for hh in range(NH):
                        last = v.tensor_tensor(out=stmp4[:, hh * DV:(hh + 1) * DV], in0=Sst[hh][:],
                                               in1=pKV_[hh], op=ALU.add)
                    return last
                t19 = ST(h, "vector", s19, deps=[t8])

                def s20(v, pc4=pc4):
                    last = None
                    for hh in range(NH):
                        last = v.tensor_scalar(out=Sst[hh][:], in0=stmp4[:, hh * DV:(hh + 1) * DV],
                                               scalar1=pc4[:, hh:hh + 1], scalar2=None, op0=ALU.mult)
                    return last
                t20 = ST(h, "vector", s20, deps=[t19])
                t_bk[ch] = t20

            for step in range(TT // CH + 1):
                if step < TT // CH:
                    chunk_front(step)
                if step >= 1:
                    chunk_back(step - 1)

            if debug and seq == 0 and it == 0:
                def dbg2(sync):
                    for s in range(4):
                        sync.dma_start(out=io["d_otm"][s * 128:(s + 1) * 128, :],
                                       in_=otm[:, s * D:(s + 1) * D]).then_inc(h.st.h, 16)
                        h.st.bump(16)
                    return None
                ST(h, "sync", dbg2)
            tm_to_fm(h, [otm[:, s * D:(s + 1) * D] for s in range(4)], ofm, pT, pP, ident, fm_stride=TT)
            for s in range(4):
                def wmm(t, s=s):
                    last = None
                    for half, bank in [(0, pW0), (1, pW1)]:
                        for c in range(8):
                            last = t.matmul(bank[:], ofm[:, c * TT + s * 128:c * TT + (s + 1) * 128],
                                            woT[:, c * 1024 + half * 512:c * 1024 + half * 512 + 512],
                                            start=(c == 0), stop=(c == 7))
                    return last
                ST(h, "tensor", wmm)

                def resid(v, s=s):
                    v.tensor_tensor(out=xres[:, 0:512], in0=pW0[:], in1=x0tm[s][:, 0:512], op=ALU.add)
                    return v.tensor_tensor(out=xres[:, 512:1024], in0=pW1[:], in1=x0tm[s][:, 512:1024], op=ALU.add)
                ST(h, "vector", resid)

                emit_ln(h, xres[:], xres[:], tr128[:], nfeat=D, eps=1e-5, stats=stats)

                def sto(sync, t0=t0, s=s):
                    sync.dma_start(out=io["s1"][t0 + s * 128:t0 + (s + 1) * 128, :], in_=xres[:]).then_inc(h.st.h, 16)
                    h.st.bump(16)
                    sync.wait_ge(h.st.h, h.st.n)
                    return sync.nop()
                ST(h, "sync", sto)

    if debug:
        def dbgout(sync):
            sync.dma_start(out=io["d_buf"][:], in_=dbgb[:]).then_inc(h.st.h, 16)
            h.st.bump(16)
            return None
        ST(h, "sync", dbgout)
    ST(h, "sync", lambda sync: sync.wait_ge(h.st.h, h.st.n) and None)



def emit_p0a(h, io, variant):
    ns = [0, 1, 2] if variant == "A" else [3, 4]

    wx1T = h.sb("wx1T", [128, 8 * 160])
    w2nT = h.sb("w2nT", [32, 5 * 1024])
    mu_x = h.sb("mu_x", [128, 8])
    x_bias = h.sb("x_bias", [128, 8 * 5])
    ident = h.sb("identP0", [128, 128])
    if variant == "A":
        wrT = h.sb("wrT", [128, 8 * 512])
        wkT = h.sb("wkT", [128, 8 * 512])
        ww1T = h.sb("ww1T", [128, 8 * 64])
        ww2T = h.sb("ww2T", [64, 512])
        bw2 = h.sb("bw2", [128, 4])
    else:
        wvT = h.sb("wvT", [128, 8 * 1024])
        wgT = h.sb("wgT", [128, 8 * 1024])

    xtm = [h.sb(f"xtm{s}", [128, D]) for s in range(4)]
    xfm = h.sb("xfm", [128, 8 * TT])
    delta = h.sb("delta", [128, 8 * TT])
    stash = h.sb("stash", [128, 8])
    xmod = h.sb("xmod", [128, 8 * TT])
    musb = h.sb("musb", [128, TT])
    musbB = h.sb("musbB", [128, TT])
    xp = h.sb("xp", [32, 5 * TT])
    ubuf = h.sb("ubuf", [64, TT])
    ofm = h.sb("ofm", [128, 4 * 1024])

    pA = h.ps("pA", [128, 512])
    pB = h.ps("pB", [128, 512])
    pX = h.ps("pX", [32, TT])
    pM = h.ps("pM", [128, TT])
    pM2 = h.ps("pM2", [128, TT])
    pU = h.ps("pU", [64, TT])

    def loads(sync):
        last = None
        for c in range(8):
            last = sync.dma_start(out=wx1T[:, c * 160:(c + 1) * 160], in_=io["wx1T"][c * 128:(c + 1) * 128, :])
            last.then_inc(h.ld.h, 16); h.ld.bump(16)
        for n_ in range(5):
            last = sync.dma_start(out=w2nT[:, n_ * 1024:(n_ + 1) * 1024], in_=io["w2nT"][n_, :, :])
            last.then_inc(h.ld.h, 16); h.ld.bump(16)
        for c in range(8):
            last = sync.dma_start(out=mu_x[:, c:c + 1], in_=io["mu_x"][c * 128:(c + 1) * 128, :])
            last.then_inc(h.ld.h, 16); h.ld.bump(16)
            last = sync.dma_start(out=x_bias[:, c * 5:(c + 1) * 5], in_=io["x_biasT"][c * 128:(c + 1) * 128, :])
            last.then_inc(h.ld.h, 16); h.ld.bump(16)
        last = sync.dma_start(out=ident[:], in_=io["ident"][:])
        last.then_inc(h.ld.h, 16); h.ld.bump(16)
        if variant == "A":
            for c in range(8):
                for dst, src, wdt in [(wrT, io["wrT"], 512), (wkT, io["wkT"], 512), (ww1T, io["ww1T"], 64)]:
                    last = sync.dma_start(out=dst[:, c * wdt:(c + 1) * wdt], in_=src[c * 128:(c + 1) * 128, :])
                    last.then_inc(h.ld.h, 16); h.ld.bump(16)
            last = sync.dma_start(out=ww2T[:], in_=io["ww2T"][:])
            last.then_inc(h.ld.h, 16); h.ld.bump(16)
            for o in range(4):
                last = sync.dma_start(out=bw2[:, o:o + 1], in_=io["bw2"][o * 128:(o + 1) * 128, :])
                last.then_inc(h.ld.h, 16); h.ld.bump(16)
        else:
            for c in range(8):
                for dst, src in [(wvT, io["wvT"]), (wgT, io["wgT"])]:
                    last = sync.dma_start(out=dst[:, c * 1024:(c + 1) * 1024], in_=src[c * 128:(c + 1) * 128, :])
                    last.then_inc(h.ld.h, 16); h.ld.bump(16)
        return None
    ST(h, "sync", loads)
    ST(h, "vector", lambda v: v.memset(stash[:], 0.0))

    for i in range(NTILE):
        t0 = i * TT

        def ldx(sync, t0=t0):
            for s in range(4):
                sync.dma_start(out=xtm[s][:], in_=io["x0"][t0 + s * 128:t0 + (s + 1) * 128, :]).then_inc(h.ld.h, 16)
                h.ld.bump(16)
            return None
        ST(h, "sync", ldx)

        tm_to_fm(h, xtm, xfm, pA, pB, ident)

        def dstage1(v, i=i):
            last = None
            for c in range(8):
                xc = xfm[:, c * TT:(c + 1) * TT]
                dc = delta[:, c * TT:(c + 1) * TT]
                v.tensor_tensor(out=dc[:, 1:TT], in0=xc[:, 0:TT - 1], in1=xc[:, 1:TT], op=ALU.subtract)
                if i % (T // TT) == 0:
                    last = v.tensor_scalar(out=dc[:, 0:1], in0=xc[:, 0:1], scalar1=-1.0, scalar2=None, op0=ALU.mult)
                else:
                    last = v.tensor_tensor(out=dc[:, 0:1], in0=stash[:, c:c + 1], in1=xc[:, 0:1], op=ALU.subtract)
            return last
        ST(h, "vector", dstage1)

        def dstage2(v):
            last = None
            for c in range(8):
                last = v.tensor_copy(out=stash[:, c:c + 1], in_=xfm[:, c * TT + TT - 1:c * TT + TT])
            return last
        ST(h, "vector", dstage2)

        def dstage3(v):
            last = None
            for c in range(8):
                last = v.scalar_tensor_tensor(out=xmod[:, c * TT:(c + 1) * TT], in0=delta[:, c * TT:(c + 1) * TT],
                                              scalar=mu_x[:, c:c + 1], in1=xfm[:, c * TT:(c + 1) * TT],
                                              op0=ALU.mult, op1=ALU.add)
            return last
        ST(h, "vector", dstage3)

        base0 = h.tok()
        t_xp, t_tanh = {}, {}
        prev_n = None
        ofm_guard = base0
        for idx_, n_ in enumerate(ns):
            def xpmm(t, n_=n_):
                last = None
                for c in range(8):
                    last = t.matmul(pX[:], wx1T[:, c * 160 + n_ * 32:c * 160 + (n_ + 1) * 32],
                                    xmod[:, c * TT:(c + 1) * TT], start=(c == 0), stop=(c == 7))
                return last
            dd = [base0] + ([t_tanh[ns[idx_ - 1]]] if idx_ > 0 else [])
            t_xp[n_] = ST(h, "tensor", xpmm, deps=dd)
            t_tanh[n_] = ST(h, "scalar", lambda s, n_=n_: s.activation(
                out=xp[:, n_ * TT:(n_ + 1) * TT], in_=pX[:], func=AF.Tanh), deps=[t_xp[n_]])
        xp_done = t_xp[ns[-1]]

        for n_ in ns:
            mb = [pM, pM2]
            mub = [musb, musbB]
            tmm, tst, txm = {}, {}, {}
            xmod_guard = prev_n if prev_n is not None else xp_done

            def em_musmm(c, n_=n_):
                deps = [t_tanh[n_]] + ([tst[c - 2]] if c >= 2 else [xmod_guard])
                tmm[c] = ST(h, "tensor", lambda t, n_=n_, c=c: t.matmul(
                    mb[c % 2][:], w2nT[:, n_ * 1024 + c * 128:n_ * 1024 + (c + 1) * 128],
                    xp[:, n_ * TT:(n_ + 1) * TT], start=True, stop=True), deps=deps)

            def em_stt(c, n_=n_):
                deps = [tmm[c]] + ([txm[c - 2]] if c >= 2 else [])
                tst[c] = ST(h, "vector", lambda v, n_=n_, c=c: v.scalar_tensor_tensor(
                    out=mub[c % 2][:], in0=mb[c % 2][:], scalar=x_bias[:, c * 5 + n_:c * 5 + n_ + 1],
                    in1=delta[:, c * TT:(c + 1) * TT], op0=ALU.add, op1=ALU.mult), deps=deps)
                deps2 = [tst[c], xmod_guard]
                txm[c] = ST(h, "vector", lambda v, c=c: v.tensor_tensor(
                    out=xmod[:, c * TT:(c + 1) * TT], in0=mub[c % 2][:],
                    in1=xfm[:, c * TT:(c + 1) * TT], op=ALU.add), deps=deps2)

            for c in range(9):
                if c < 8:
                    em_musmm(c)
                if c >= 1:
                    em_stt(c - 1)
            xm_all = txm[7]

            if variant == "A" and n_ in (0, 2):
                wT = wrT if n_ == 0 else wkT
                dst = io["rsc"] if n_ == 0 else io["ksc"]
                pb = [pA, pB]
                tpm, tpe = {}, {}

                def em_pmm(o, wT=wT):
                    def pmm(t, o=o, wT=wT):
                        last = None
                        for c in range(8):
                            last = t.matmul(pb[o % 2][:], wT[:, c * 512 + o * 128:c * 512 + (o + 1) * 128],
                                            xmod[:, c * TT:(c + 1) * TT], start=(c == 0), stop=(c == 7))
                        return last
                    deps = [xm_all] + ([tpe[o - 2]] if o >= 2 else [])
                    tpm[o] = ST(h, "tensor", pmm, deps=deps)

                def em_pev(o):
                    tpe[o] = ST(h, "vector", lambda v, o=o: v.tensor_copy(
                        out=ofm[:, o * 1024:o * 1024 + TT], in_=pb[o % 2][:]),
                        deps=[tpm[o], ofm_guard])

                for o in range(5):
                    if o < 4:
                        em_pmm(o)
                    if o >= 1:
                        em_pev(o - 1)
                prev_n = tpe[3]

                def stores(sync, dst=dst, t0=t0):
                    for o in range(4):
                        sync.dma_start(out=dst[o * 128:(o + 1) * 128, t0:t0 + TT],
                                       in_=ofm[:, o * 1024:o * 1024 + TT]).then_inc(h.st.h, 16)
                        h.st.bump(16)
                    sync.wait_ge(h.st.h, h.st.n)
                    return sync.nop()
                ofm_guard = ST(h, "sync", stores, deps=[tpe[3]])
            elif variant == "A":
                def wmm(t):
                    last = None
                    for c in range(8):
                        last = t.matmul(pU[:], ww1T[:, c * 64:(c + 1) * 64],
                                        xmod[:, c * TT:(c + 1) * TT], start=(c == 0), stop=(c == 7))
                    return last
                twm = ST(h, "tensor", wmm, deps=[xm_all])
                ttn = ST(h, "scalar", lambda s: s.activation(out=ubuf[:], in_=pU[:], func=AF.Tanh),
                         deps=[twm])
                tpe3 = None
                bank_guard = prev_n if prev_n is not None else base0
                for o in range(4):
                    tw2 = ST(h, "tensor", lambda t, o=o: t.matmul(
                        pA[:] if o % 2 == 0 else pB[:], ww2T[:, o * 128:(o + 1) * 128],
                        ubuf[:], start=True, stop=True),
                        deps=[ttn, bank_guard] + ([tpe3] if tpe3 is not None and o >= 2 else []))
                    tpe3 = ST(h, "vector", lambda v, o=o: v.tensor_scalar(
                        out=ofm[:, o * 1024:o * 1024 + TT], in0=pA[:] if o % 2 == 0 else pB[:],
                        scalar1=bw2[:, o:o + 1], scalar2=None, op0=ALU.add), deps=[tw2, ofm_guard])
                prev_n = tpe3

                def storew(sync, t0=t0):
                    for o in range(4):
                        sync.dma_start(out=io["wsc"][o * 128:(o + 1) * 128, t0:t0 + TT],
                                       in_=ofm[:, o * 1024:o * 1024 + TT]).then_inc(h.st.h, 16)
                        h.st.bump(16)
                    sync.wait_ge(h.st.h, h.st.n)
                    return sync.nop()
                ofm_guard = ST(h, "sync", storew, deps=[tpe3])
            else:
                wT = wvT if n_ == 3 else wgT
                dst = io["vsc"] if n_ == 3 else io["gsc"]
                pairs = [(pA, pB), (pM, pM2)]
                tvm, tve = {}, {}

                def em_vmm(s, wT=wT):
                    def vmm(t, s=s, wT=wT):
                        last = None
                        for half in range(2):
                            bank = pairs[s % 2][half]
                            for c in range(8):
                                last = t.matmul(bank[:],
                                                xmod[:, c * TT + s * 128:c * TT + (s + 1) * 128],
                                                wT[:, c * 1024 + half * 512:c * 1024 + half * 512 + 512],
                                                start=(c == 0), stop=(c == 7))
                        return last
                    deps = [xm_all] + ([tve[s - 2]] if s >= 2 else [])
                    if prev_n is not None and s < 2:
                        deps.append(prev_n)
                    tvm[s] = ST(h, "tensor", vmm, deps=deps)

                def em_vev(s):
                    def vevac(v, s=s):
                        v.tensor_copy(out=ofm[:, s * 1024:s * 1024 + 512], in_=pairs[s % 2][0][:])
                        return v.tensor_copy(out=ofm[:, s * 1024 + 512:(s + 1) * 1024], in_=pairs[s % 2][1][:])
                    tve[s] = ST(h, "vector", vevac, deps=[tvm[s], ofm_guard])

                for s in range(5):
                    if s < 4:
                        em_vmm(s)
                    if s >= 1:
                        em_vev(s - 1)
                prev_n = tve[3]

                def storev(sync, dst=dst, t0=t0):
                    for s in range(4):
                        sync.dma_start(out=dst[t0 + s * 128:t0 + (s + 1) * 128, :],
                                       in_=ofm[:, s * 1024:(s + 1) * 1024]).then_inc(h.st.h, 16)
                        h.st.bump(16)
                    sync.wait_ge(h.st.h, h.st.n)
                    return sync.nop()
                ofm_guard = ST(h, "sync", storev, deps=[tve[3]])

    ST(h, "sync", lambda sync: sync.wait_ge(h.st.h, h.st.n) and None)



def emit_rms(h, xin, out_ap, trash, stats, nfeat=1024, eps=1e-6):
    s_ = stats
    ST(h, "scalar", lambda sc: sc.activation(out=trash, in_=xin, func=AF.Square, accum_out=s_[:, 2:3]))
    ST(h, "vector", lambda v: v.tensor_scalar(out=s_[:, 3:4], in0=s_[:, 2:3],
                                              scalar1=1.0 / nfeat, scalar2=float(eps), op0=ALU.mult, op1=ALU.add))
    ST(h, "scalar", lambda sc: sc.activation(out=s_[:, 4:5], in_=s_[:, 3:4], func=AF.Sqrt))
    ST(h, "vector", lambda v: v.reciprocal(out=s_[:, 5:6], in_=s_[:, 4:5]))
    if out_ap is not None:
        ST(h, "vector", lambda v: v.tensor_scalar(out=out_ap, in0=xin, scalar1=s_[:, 5:6],
                                                  scalar2=None, op0=ALU.mult))


def emit_p1(h, io):
    wqT = h.sb("q_wqT", [128, 8 * D])
    wkT = h.sb("q_wkT", [128, 8 * 256])
    wvT = h.sb("q_wvT", [128, 8 * 256])
    woT = h.sb("q_woT", [128, 8 * D])
    ident = h.sb("q_ident", [128, 128])
    trineg = h.sb("q_trineg", [128, 128])
    cosf = h.sb("q_cos", [128, 16 * 32])
    sinf = h.sb("q_sin", [128, 16 * 32])
    grow = h.sb("q_grow", [1, 32])
    gq = h.sb("q_gq", [128, H0])
    gk = h.sb("q_gk", [128, HKV])
    ones1 = h.sb("q_ones1", [1, 128])

    x1t = h.sb("q_x1t", [128, D])
    xn = h.sb("q_xn", [128, D])
    xfm = h.sb("q_xfm", [128, 8 * 128])
    qtm = h.sb("q_qtm", [128, D])
    qrp = h.sb("q_qrp", [128, D])
    Ab = h.sb("q_Ab", [128, D])
    Bb = h.sb("q_Bb", [128, D])
    qfm = h.sb("q_qfm", [128, 8 * 128])
    ktm = h.sb("q_ktm", [128, 256])
    krp = h.sb("q_krp", [128, 256])
    sqb = h.sb("q_sqb", [128, D])
    red = h.sb("q_red", [128, H0])
    rstdh = h.sb("q_rstdh", [128, H0])
    kfm = h.sb("q_kfm", [128, 2 * T])
    vseq = h.sb("q_vseq", [128, 16 * 256])
    scb2_ = [h.sb("q_scbA", [128, T]), h.sb("q_scbB", [128, T])]
    ptb2_ = [h.sb("q_ptbA", [128, 16 * 128]), h.sb("q_ptbB", [128, 16 * 128])]
    otm = h.sb("q_otm", [128, D])
    ofm = h.sb("q_ofm", [128, 8 * 128])
    stats = h.sb("q_stats", [128, 8])
    trash = h.sb("q_trash", [128, D])

    pBig = h.ps("q_pBig", [128, 2048])
    pAV = h.ps("q_pAV", [128, 128])
    pW0 = h.ps("q_pW0", [128, 512])
    pW1 = h.ps("q_pW1", [128, 512])

    def loads(sync):
        for c in range(8):
            sync.dma_start(out=wqT[:, c * D:(c + 1) * D], in_=io["wqT2"][c * 128:(c + 1) * 128, :]).then_inc(h.ld.h, 16)
            sync.dma_start(out=wkT[:, c * 256:(c + 1) * 256], in_=io["wkT2"][c * 128:(c + 1) * 128, :]).then_inc(h.ld.h, 16)
            sync.dma_start(out=wvT[:, c * 256:(c + 1) * 256], in_=io["wvT2"][c * 128:(c + 1) * 128, :]).then_inc(h.ld.h, 16)
            sync.dma_start(out=woT[:, c * D:(c + 1) * D], in_=io["woT2"][c * 128:(c + 1) * 128, :]).then_inc(h.ld.h, 16)
            h.ld.bump(64)
        for i in range(16):
            sync.dma_start(out=cosf[:, i * 32:(i + 1) * 32], in_=io["rope_cos"][i * 128:(i + 1) * 128, :]).then_inc(h.ld.h, 16)
            sync.dma_start(out=sinf[:, i * 32:(i + 1) * 32], in_=io["rope_sin"][i * 128:(i + 1) * 128, :]).then_inc(h.ld.h, 16)
            h.ld.bump(32)
        for nm, sl in [("ident", ident[:]), ("tri_neg", trineg[:])]:
            sync.dma_start(out=sl, in_=io[nm][:]).then_inc(h.ld.h, 16)
            h.ld.bump(16)
        sync.dma_start(out=grow[:, 0:16], in_=io["q_gain_row"][:]).then_inc(h.ld.h, 16)
        sync.dma_start(out=grow[:, 16:20], in_=io["k_gain_row"][:]).then_inc(h.ld.h, 16)
        h.ld.bump(32)
        return None
    ST(h, "sync", loads)
    ST(h, "vector", lambda v: v.memset(ones1[:], 1.0))

    def gbc(t):
        t.matmul(pW0[:, 0:16], ones1[:], grow[:, 0:16], start=True, stop=True)
        return t.matmul(pW0[:, 16:20], ones1[:], grow[:, 16:20], start=True, stop=True)
    ST(h, "tensor", gbc)

    def gbc2(v):
        v.tensor_scalar(out=gq[:], in0=pW0[:, 0:16], scalar1=0.125, scalar2=None, op0=ALU.mult)
        return v.tensor_copy(out=gk[:], in_=pW0[:, 16:20])
    ST(h, "vector", gbc2)

    def head_rms_rope(src, dst, nh, gain, ti):
        ST(h, "scalar", lambda sc: sc.activation(out=sqb[:, 0:nh * 64], in_=src, func=AF.Square))
        ST(h, "vector", lambda v: v.tensor_reduce(
            out=red[:, 0:nh], in_=sqb[:, 0:nh * 64].rearrange("p (h d) -> p h d", d=64),
            axis=AX.X, op=ALU.add))
        ST(h, "vector", lambda v: v.tensor_scalar(out=red[:, 0:nh], in0=red[:, 0:nh],
                                                  scalar1=1.0 / 64, scalar2=1e-6, op0=ALU.mult, op1=ALU.add))
        ST(h, "scalar", lambda sc: sc.activation(out=red[:, 0:nh], in_=red[:, 0:nh], func=AF.Sqrt))
        ST(h, "vector", lambda v: v.reciprocal(out=rstdh[:, 0:nh], in_=red[:, 0:nh]))

        def norm(v):
            last = None
            for hh in range(nh):
                last = v.tensor_scalar(out=src[:, hh * 64:(hh + 1) * 64], in0=src[:, hh * 64:(hh + 1) * 64],
                                       scalar1=rstdh[:, hh:hh + 1], scalar2=gain[:, hh:hh + 1],
                                       op0=ALU.mult, op1=ALU.mult)
            return last
        ST(h, "vector", norm)

        cb = cosf[:, ti * 32:(ti + 1) * 32].unsqueeze(1).broadcast_to([128, nh, 32])
        sb_ = sinf[:, ti * 32:(ti + 1) * 32].unsqueeze(1).broadcast_to([128, nh, 32])
        sv = src.rearrange("p (h d) -> p h d", d=64)
        Av = Ab[:, 0:nh * 64].rearrange("p (h d) -> p h d", d=64)
        Bv = Bb[:, 0:nh * 64].rearrange("p (h d) -> p h d", d=64)
        dv = dst.rearrange("p (h d) -> p h d", d=64)

        def ropeAB(v):
            v.tensor_tensor(out=Av[:, :, 0:32], in0=sv[:, :, 0:32], in1=cb, op=ALU.mult)
            v.tensor_tensor(out=Av[:, :, 32:64], in0=sv[:, :, 32:64], in1=cb, op=ALU.mult)
            v.tensor_tensor(out=Bv[:, :, 0:32], in0=sv[:, :, 0:32], in1=sb_, op=ALU.mult)
            return v.tensor_tensor(out=Bv[:, :, 32:64], in0=sv[:, :, 32:64], in1=sb_, op=ALU.mult)
        ST(h, "vector", ropeAB)

        def ropeC(v):
            v.tensor_tensor(out=dv[:, :, 0:32], in0=Av[:, :, 0:32], in1=Bv[:, :, 32:64], op=ALU.add)
            return v.tensor_tensor(out=dv[:, :, 32:64], in0=Av[:, :, 32:64], in1=Bv[:, :, 0:32], op=ALU.subtract)
        ST(h, "vector", ropeC)

    for seq in range(2):
        base_t = seq * T
        for ti in range(16):
            t0 = base_t + ti * 128

            def ldx(sync, t0=t0):
                sync.dma_start(out=x1t[:], in_=io["s1"][t0:t0 + 128, :]).then_inc(h.ld.h, 16)
                h.ld.bump(16)
                return None
            ST(h, "sync", ldx)

            emit_rms(h, x1t[:], xn[:], trash[:], stats)
            tm_to_fm(h, [xn], xfm, pBig[:, 0:512], pBig[:, 512:1024], ident, fm_stride=128, sub0=0)

            def kvmm(t):
                last = None
                for c in range(8):
                    t.matmul(pBig[:, 1024:1280], xfm[:, c * 128:(c + 1) * 128],
                             wkT[:, c * 256:(c + 1) * 256], start=(c == 0), stop=(c == 7))
                for c in range(8):
                    last = t.matmul(pBig[:, 1280:1536], xfm[:, c * 128:(c + 1) * 128],
                                    wvT[:, c * 256:(c + 1) * 256], start=(c == 0), stop=(c == 7))
                return last
            ST(h, "tensor", kvmm)

            def kvevac(v, ti=ti):
                v.tensor_copy(out=ktm[:], in_=pBig[:, 1024:1280])
                return v.tensor_copy(out=vseq[:, ti * 256:(ti + 1) * 256], in_=pBig[:, 1280:1536])
            ST(h, "vector", kvevac)

            head_rms_rope(ktm[:], krp[:], HKV, gk, ti)

            def ktr(t, ti=ti):
                t.transpose(pBig[:, 1024:1152], krp[:, 0:128], ident[:])
                return t.transpose(pBig[:, 1152:1280], krp[:, 128:256], ident[:])
            ST(h, "tensor", ktr)

            def kev(v, ti=ti):
                v.tensor_copy(out=kfm[:, ti * 128:(ti + 1) * 128], in_=pBig[:, 1024:1152])
                return v.tensor_copy(out=kfm[:, T + ti * 128:T + (ti + 1) * 128], in_=pBig[:, 1152:1280])
            ST(h, "vector", kev)

        for qi in range(16):
            t0 = base_t + qi * 128
            klen = (qi + 1) * 128

            def ldq(sync, t0=t0):
                sync.dma_start(out=x1t[:], in_=io["s1"][t0:t0 + 128, :]).then_inc(h.ld.h, 16)
                h.ld.bump(16)
                return None
            ST(h, "sync", ldq)

            emit_rms(h, x1t[:], xn[:], trash[:], stats)
            tm_to_fm(h, [xn], xfm, pBig[:, 0:512], pBig[:, 512:1024], ident, fm_stride=128, sub0=0)

            def qmm(t):
                last = None
                for half, lo in [(0, 1024), (1, 1536)]:
                    for c in range(8):
                        last = t.matmul(pBig[:, lo:lo + 512], xfm[:, c * 128:(c + 1) * 128],
                                        wqT[:, c * D + half * 512:c * D + half * 512 + 512],
                                        start=(c == 0), stop=(c == 7))
                return last
            ST(h, "tensor", qmm)

            def qevac(v):
                v.tensor_copy(out=qtm[:, 0:512], in_=pBig[:, 1024:1536])
                return v.tensor_copy(out=qtm[:, 512:1024], in_=pBig[:, 1536:2048])
            ST(h, "vector", qevac)

            head_rms_rope(qtm[:], qrp[:], H0, gq, qi)

            def qtr(t):
                last = None
                for c in range(8):
                    last = t.transpose(pBig[:, c * 128:c * 128 + 128],
                                       qrp[:, c * 128:(c + 1) * 128], ident[:])
                return last
            ST(h, "tensor", qtr)

            def qev(v):
                v.tensor_copy(out=qfm[:, 0:512], in_=pBig[:, 0:512])
                return v.tensor_copy(out=qfm[:, 512:1024], in_=pBig[:, 512:1024])
            ST(h, "vector", qev)

            P2H = [0, 4, 1, 5, 2, 6, 3, 7, 8, 12, 9, 13, 10, 14, 11, 15]
            tok0 = h.tok()
            nrnd = (klen + 1023) // 1024
            t_se_last, t_exp, t_rcp, t_pe_last, t_av, t_oev = {}, {}, {}, {}, {}, {}

            def front(hh, qi=qi, klen=klen):
                p = hh % 2
                kvh = P2H[hh] // 4
                qbase = (hh % 2) * 64
                kchunk = kvh // 2
                kbase = (kvh % 2) * 64
                q_sl = qfm[qbase:qbase + 64, (hh // 2) * 128:(hh // 2) * 128 + 128]
                pair = pBig[:, p * 1024:(p + 1) * 1024]
                t_prev = None
                for r in range(nrnd):
                    lo = r * 1024
                    w_ = min(1024, klen - lo)

                    def smm(t, q_sl=q_sl, kchunk=kchunk, kbase=kbase, lo=lo, w_=w_, pair=pair):
                        last = None
                        for kb in range(0, w_, 512):
                            ww = min(512, w_ - kb)
                            last = t.matmul(pair[:, kb:kb + ww], q_sl,
                                            kfm[kbase:kbase + 64,
                                                kchunk * T + lo + kb:kchunk * T + lo + kb + ww],
                                            start=True, stop=True)
                        return last
                    deps = [t_prev] if t_prev is not None else (
                        [t_pe_last[hh - 2]] if hh >= 2 else [tok0])
                    tsc = ST(h, "tensor", smm, deps=deps)

                    def sev(v, lo=lo, w_=w_, qi=qi, p=p, pair=pair):
                        last = None
                        dlo = qi * 128
                        for kb in range(0, w_, 512):
                            ww = min(512, w_ - kb)
                            a0 = lo + kb
                            a1 = a0 + ww
                            if dlo >= a0 and dlo < a1:
                                if dlo > a0:
                                    v.tensor_copy(out=scb2_[p][:, a0:dlo], in_=pair[:, kb:kb + dlo - a0])
                                last = v.tensor_tensor(out=scb2_[p][:, dlo:dlo + 128],
                                                       in0=pair[:, dlo - lo:dlo - lo + 128],
                                                       in1=trineg[:], op=ALU.add)
                            else:
                                last = v.tensor_copy(out=scb2_[p][:, a0:a1], in_=pair[:, kb:kb + ww])
                        return last
                    t_prev = ST(h, "vector", sev, deps=[tsc])
                t_se_last[hh] = t_prev
                tmax = ST(h, "vector", lambda v, p=p, klen=klen: v.tensor_reduce(
                    out=stats[:, 0 + p:1 + p], in_=scb2_[p][:, 0:klen], axis=AX.X,
                    op=ALU.max, negate=True), deps=[t_prev])
                t_exp[hh] = ST(h, "scalar", lambda sc, p=p, klen=klen: sc.activation(
                    out=scb2_[p][:, 0:klen], in_=scb2_[p][:, 0:klen], func=AF.Exp,
                    bias=stats[:, 0 + p:1 + p], accum_out=stats[:, 2 + p:3 + p]), deps=[tmax])
                t_rcp[hh] = ST(h, "vector", lambda v, p=p: v.reciprocal(
                    out=stats[:, 4 + p:5 + p], in_=stats[:, 2 + p:3 + p]), deps=[t_exp[hh]])

            def back(hh, qi=qi):
                p = hh % 2
                kvh = P2H[hh] // 4
                pair = pBig[:, p * 1024:(p + 1) * 1024]
                nks = qi + 1
                t_prev = t_exp[hh]
                for r in range(0, nks, 8):
                    nblk = min(8, nks - r)

                    def ptr(t, r=r, nblk=nblk, p=p, pair=pair):
                        last = None
                        for b_ in range(nblk):
                            ks = r + b_
                            last = t.transpose(pair[:, b_ * 128:(b_ + 1) * 128],
                                               scb2_[p][:, ks * 128:(ks + 1) * 128], ident[:])
                        return last
                    ttr = ST(h, "tensor", ptr, deps=[t_prev])

                    def pev(sc, r=r, nblk=nblk, p=p, pair=pair):
                        last = None
                        for b0 in range(0, nblk, 4):
                            w = min(4, nblk - b0) * 128
                            last = sc.activation(out=ptb2_[p][:, (r + b0) * 128:(r + b0) * 128 + w],
                                                 in_=pair[:, b0 * 128:b0 * 128 + w], func=AF.Copy)
                        return last
                    depsv = [ttr] + ([t_av[hh - 2]] if hh >= 2 and r == 0 else [])
                    t_prev = ST(h, "scalar", pev, deps=depsv)
                t_pe_last[hh] = t_prev

                def avmm(t, qi=qi, kvh=kvh, p=p):
                    last = None
                    for ks in range(qi + 1):
                        last = t.matmul(pAV[:, p * 64:(p + 1) * 64],
                                        ptb2_[p][:, ks * 128:(ks + 1) * 128],
                                        vseq[:, ks * 256 + kvh * 64:ks * 256 + (kvh + 1) * 64],
                                        start=(ks == 0), stop=(ks == qi))
                    return last
                depsav = [t_prev, t_rcp[hh]] + ([t_oev[hh - 2]] if hh >= 2 else [])
                t_av[hh] = ST(h, "tensor", avmm, deps=depsav)
                t_oev[hh] = ST(h, "vector", lambda v, hh=hh, p=p: v.tensor_scalar(
                    out=otm[:, hh * 64:(hh + 1) * 64], in0=pAV[:, p * 64:(p + 1) * 64],
                    scalar1=stats[:, 4 + p:5 + p], scalar2=None, op0=ALU.mult), deps=[t_av[hh]])

            for step in range(H0 + 1):
                if step < H0:
                    front(step)
                if step >= 1:
                    back(step - 1)

            tm_to_fm(h, [otm], ofm, pBig[:, 0:512], pBig[:, 512:1024], ident, fm_stride=128, sub0=0)

            def omm(t):
                last = None
                for half, bank in [(0, pW0), (1, pW1)]:
                    for c in range(8):
                        last = t.matmul(bank[:], ofm[:, c * 128:(c + 1) * 128],
                                        woT[:, c * D + half * 512:c * D + half * 512 + 512],
                                        start=(c == 0), stop=(c == 7))
                return last
            ST(h, "tensor", omm)

            def resid(v):
                v.tensor_tensor(out=xn[:, 0:512], in0=pW0[:], in1=x1t[:, 0:512], op=ALU.add)
                return v.tensor_tensor(out=xn[:, 512:1024], in0=pW1[:], in1=x1t[:, 512:1024], op=ALU.add)
            ST(h, "vector", resid)

            def sxa(sync, t0=t0):
                sync.dma_start(out=io["xa"][t0:t0 + 128, :], in_=xn[:]).then_inc(h.st.h, 16)
                h.st.bump(16)
                sync.wait_ge(h.st.h, h.st.n)
                return sync.nop()
            ST(h, "sync", sxa)

    ST(h, "sync", lambda sync: sync.wait_ge(h.st.h, h.st.n) and None)



def emit_p2(h, io):
    ident = h.sb("m_ident", [128, 128])
    ones1 = h.sb("m_ones1", [1, 128])
    iota8 = h.sb("m_iota8", [1, 8], I32)
    iotaf = h.sb("m_iotaf", [1, 8])
    lrow = h.sb("m_lrow", [1, 16])
    tks = h.sb("m_tks", [1, 16])
    coefb = h.sb("m_coefb", [128, 4])
    ebase_i = h.sb("m_ebasei", [1, 4], I32)

    fcs1 = h.sb("m_fcs1", [128, 8 * 64])
    fcs1b = h.sb("m_fcs1b", [128, 8 * 64])
    fcs2 = h.sb("m_fcs2", [128, 32 * 64])
    fcs2b = h.sb("m_fcs2b", [128, 32 * 64])
    pjs1 = h.sb("m_pjs1", [128, 32 * 64])
    pjs1b = h.sb("m_pjs1b", [128, 32 * 64])
    pjs2 = h.sb("m_pjs2", [128, 8 * 64])
    pjs2b = h.sb("m_pjs2b", [128, 8 * 64])

    xat = [h.sb(f"m_xat{s}", [128, D]) for s in range(4)]
    h0tm = h.sb("m_h0tm", [128, 4 * D])
    h0fm = h.sb("m_h0fm", [128, 8 * TT])
    h1 = h.sb("m_h1", [128, 8 * TT])
    h2 = h.sb("m_h2", [128, 16 * TT])
    t1 = h.sb("m_t1", [128, 16 * TT])
    pmix = h.sb("m_pmix", [128, 8 * TT])
    gbuf2_ = [h.sb("m_gbufA", [128, 1024]), h.sb("m_gbufB", [128, 1024])]
    lbuf2_ = [h.sb("m_lbufA", [128, 8 * 128]), h.sb("m_lbufB", [128, 8 * 128])]
    strip2_ = [h.sb("m_stripA", [128, TT]), h.sb("m_stripB", [128, TT])]
    gbuf = gbuf2_[0]
    lbuf = lbuf2_[0]
    strip = strip2_[0]
    stats = h.sb("m_stats", [128, 8])
    trash = gbuf2_[0][:]

    pF = h.ps("m_pF", [128, 2048])
    pMix = h.ps("m_pMix", [128, 512])
    pTr = h.ps("m_pTr", [128, 512])
    pB2 = h.ps("m_pB2", [128, 512])
    p8 = h.ps("m_p8", [128, 512])
    pjBanks = [pF[:, 0:512], pF[:, 512:1024], pF[:, 1024:1536], pF[:, 1536:2048],
               pMix[:], pTr[:], pB2[:], p8[:]]

    rgF0 = h.ctx.enter_context(h.nc.gpsimd.register("rgF0"))
    rgF1 = h.ctx.enter_context(h.nc.gpsimd.register("rgF1"))
    rgP0 = h.ctx.enter_context(h.nc.gpsimd.register("rgP0"))
    rgP1 = h.ctx.enter_context(h.nc.gpsimd.register("rgP1"))
    rgT = h.ctx.enter_context(h.nc.gpsimd.register("rgT"))

    def loads(sync):
        sync.dma_start(out=ident[:], in_=io["ident"][:]).then_inc(h.ld.h, 16)
        sync.dma_start(out=lrow[:, 0:8], in_=io["fc_logits_row"][:]).then_inc(h.ld.h, 16)
        sync.dma_start(out=lrow[:, 8:16], in_=io["pj_logits_row"][:]).then_inc(h.ld.h, 16)
        h.ld.bump(48)
        return None
    ST(h, "sync", loads)

    def init(v):
        v.memset(ones1[:], 1.0)
        return v.memset(tks[:], 0.0)
    ST(h, "vector", init)
    ST(h, "gpsimd", lambda gp: gp.iota(iota8[:], base=0, pattern=[[1, 8]], channel_multiplier=0))
    ST(h, "vector", lambda v: v.tensor_copy(out=iotaf[:], in_=iota8[:]))

    for gi, lo in [(0, 0), (1, 8)]:
        lsl = lrow[:, lo:lo + 8]
        b = gi * 8
        ST(h, "vector", lambda v, lsl=lsl, b=b: v.tensor_reduce(
            out=tks[:, b + 0:b + 1], in_=lsl, axis=AX.X, op=ALU.max))
        ST(h, "vector", lambda v, lsl=lsl, b=b: v.tensor_scalar(
            out=trash[0:1, 0:8], in0=lsl, scalar1=tks[:, b + 0:b + 1], scalar2=None, op0=ALU.is_ge))
        ST(h, "vector", lambda v, b=b: v.tensor_tensor(
            out=trash[0:1, 8:16], in0=trash[0:1, 0:8], in1=iotaf[:], op=ALU.mult))
        ST(h, "vector", lambda v, b=b: v.tensor_reduce(
            out=tks[:, b + 2:b + 3], in_=trash[0:1, 8:16], axis=AX.X, op=ALU.add))
        ST(h, "vector", lambda v, lsl=lsl, b=b: v.scalar_tensor_tensor(
            out=trash[0:1, 0:8], in0=trash[0:1, 0:8], scalar=-1e30, in1=lsl,
            op0=ALU.mult, op1=ALU.add))
        ST(h, "vector", lambda v, b=b: v.tensor_reduce(
            out=tks[:, b + 1:b + 2], in_=trash[0:1, 0:8], axis=AX.X, op=ALU.max))
        ST(h, "vector", lambda v, b=b: v.tensor_scalar(
            out=trash[0:1, 16:24], in0=trash[0:1, 0:8], scalar1=tks[:, b + 1:b + 2],
            scalar2=None, op0=ALU.is_ge))
        ST(h, "vector", lambda v, b=b: v.tensor_tensor(
            out=trash[0:1, 24:32], in0=trash[0:1, 16:24], in1=iotaf[:], op=ALU.mult))
        ST(h, "vector", lambda v, b=b: v.tensor_reduce(
            out=tks[:, b + 3:b + 4], in_=trash[0:1, 24:32], axis=AX.X, op=ALU.add))
        ST(h, "vector", lambda v, b=b: v.tensor_tensor(
            out=tks[:, b + 7:b + 8], in0=tks[:, b + 1:b + 2], in1=tks[:, b + 0:b + 1], op=ALU.subtract))
        ST(h, "scalar", lambda sc, b=b: sc.activation(
            out=tks[:, b + 4:b + 5], in_=tks[:, b + 7:b + 8], func=AF.Exp))
        ST(h, "vector", lambda v, b=b: v.tensor_scalar(
            out=tks[:, b + 5:b + 6], in0=tks[:, b + 4:b + 5], scalar1=1.0, scalar2=None, op0=ALU.add))
        ST(h, "vector", lambda v, b=b: v.reciprocal(out=tks[:, b + 5:b + 6], in_=tks[:, b + 5:b + 6]))
        ST(h, "vector", lambda v, b=b: v.tensor_tensor(
            out=tks[:, b + 6:b + 7], in0=tks[:, b + 4:b + 5], in1=tks[:, b + 5:b + 6], op=ALU.mult))

    def cbc(t):
        t.matmul(pMix[:, 0:1], ones1[:], tks[:, 5:6], start=True, stop=True)
        t.matmul(pMix[:, 1:2], ones1[:], tks[:, 6:7], start=True, stop=True)
        t.matmul(pMix[:, 2:3], ones1[:], tks[:, 13:14], start=True, stop=True)
        return t.matmul(pMix[:, 3:4], ones1[:], tks[:, 14:15], start=True, stop=True)
    ST(h, "tensor", cbc)
    ST(h, "vector", lambda v: v.tensor_copy(out=coefb[:], in_=pMix[:, 0:4]))

    def packi(v):
        v.tensor_copy(out=trash[0:1, 0:1], in_=tks[:, 2:3])
        v.tensor_copy(out=trash[0:1, 1:2], in_=tks[:, 3:4])
        v.tensor_copy(out=trash[0:1, 2:3], in_=tks[:, 10:11])
        return v.tensor_copy(out=trash[0:1, 3:4], in_=tks[:, 11:12])
    ST(h, "vector", packi)
    ST(h, "vector", lambda v: v.tensor_copy(out=ebase_i[:], in_=trash[0:1, 0:4]))

    def gsetup(gp):
        gp.reg_load(rgF0, ebase_i[:1, 0:1])
        gp.reg_load(rgF1, ebase_i[:1, 1:2])
        gp.reg_load(rgP0, ebase_i[:1, 2:3])
        gp.reg_load(rgP1, ebase_i[:1, 3:4])
        for slot, rg, src_, dst in [(0, rgF0, io["fc_mixB"], io["fc_sel"]),
                                    (1, rgF1, io["fc_mixB"], io["fc_sel"]),
                                    (0, rgP0, io["pj_mixTB"], io["pj_sel"]),
                                    (1, rgP1, io["pj_mixTB"], io["pj_sel"])]:
            gp.reg_mov(rgT, 0)
            gp.reg_add(rgT, rgT, rg)
            gp.reg_mul(rgT, rgT, 32 * 128 * D)
            gp.dma_start(out=dst[slot * 4096:(slot + 1) * 4096, :],
                         in_=bass.AP(src_.tensor, rgT, [[32768, 128], [1, 1], [1, 32768]])
                         ).then_inc(h.gd.h, 16)
            h.gd.bump(16)
        for rg, srcs in [(rgF0, [(io["fc_s1f"], fcs1, 8, D), (io["fc_s2f"], fcs2, 32, HID)]),
                         (rgF1, [(io["fc_s1f"], fcs1b, 8, D), (io["fc_s2f"], fcs2b, 32, HID)]),
                         (rgP0, [(io["pj_s1f"], pjs1, 32, HID), (io["pj_s2f"], pjs2, 8, D)]),
                         (rgP1, [(io["pj_s1f"], pjs1b, 32, HID), (io["pj_s2f"], pjs2b, 8, D)])]:
            for src_t, dst_sb, ntile, rows in srcs:
                gp.reg_mov(rgT, 0)
                gp.reg_add(rgT, rgT, rg)
                gp.reg_mul(rgT, rgT, rows * 64)
                gp.dma_start(out=dst_sb[:, 0:ntile * 64],
                             in_=bass.AP(src_t.tensor, rgT,
                                         [[64, 128], [8192, ntile], [1, 64]])
                             ).then_inc(h.gd.h, 16)
                h.gd.bump(16)
        return None
    ST(h, "gpsimd", gsetup)

    for mt in range(NTILE):
        t0 = mt * TT

        def ldxa(sync, t0=t0):
            for s in range(4):
                sync.dma_start(out=xat[s][:], in_=io["xa"][t0 + s * 128:t0 + (s + 1) * 128, :]).then_inc(h.ld.h, 16)
                h.ld.bump(16)
            return None
        ST(h, "sync", ldxa)

        for s in range(4):
            emit_rms(h, xat[s][:], h0tm[:, s * D:(s + 1) * D], trash, stats)
        tm_to_fm(h, [h0tm[:, s * D:(s + 1) * D] for s in range(4)],
                 h0fm, pF[:, 0:512], pF[:, 512:1024], ident, fm_stride=TT)

        def yz(v):
            last = None
            for s in range(4):
                last = v.memset(h0tm[:, s * D:(s + 1) * D], 0.0)
            return last
        ST(h, "vector", yz)

        for half in range(2):
            jlo = half * 16
            for eslot in range(2):
                s1b = fcs1 if eslot == 0 else fcs1b
                s2b = fcs2 if eslot == 0 else fcs2b
                rg = rgF0 if eslot == 0 else rgF1

                for jg in range(2):
                    def bd1mm(t, jg=jg, s1b=s1b):
                        last = None
                        for jj in range(4):
                            ch = jg * 4 + jj
                            for gpar in range(2):
                                g = ch * 2 + gpar
                                base = gpar * 64
                                last = t.matmul(pF[base:base + 64, jj * 512:(jj + 1) * 512],
                                                s1b[base:base + 64, (g // 2) * 64:(g // 2) * 64 + 64],
                                                h0fm[base:base + 64, ch * TT:(ch + 1) * TT],
                                                start=True, stop=True)
                        return last
                    ST(h, "tensor", bd1mm)

                    def bd1ev(v, jg=jg):
                        last = None
                        for jj in range(4):
                            ch = jg * 4 + jj
                            last = v.tensor_copy(out=h1[:, ch * TT:(ch + 1) * TT],
                                                 in_=pF[:, jj * 512:(jj + 1) * 512])
                        return last
                    ST(h, "vector", bd1ev)

                tok0 = h.tok()
                tg, ttr, tev, tmm, tse, tb2, tcm = {}, {}, {}, {}, {}, {}, {}
                trpair = [pF[:, 0:1024], pF[:, 1024:2048]]
                mixb = [pMix, p8]
                b2b = [pTr, pB2]

                def em_gth(jj):
                    p = jj % 2
                    j = jlo + jj

                    def gth(sync, j=j, p=p, eslot=eslot):
                        r0 = eslot * 4096 + j * 128
                        sync.dma_start(out=gbuf2_[p][:], in_=io["fc_sel"][r0:r0 + 128, :]).then_inc(h.ld.h, 16)
                        h.ld.bump(16)
                        return None
                    deps = ([ttr[jj - 2]] if jj >= 2 else [tok0]) + ([tg[jj - 1]] if jj >= 1 else [])
                    ST(h, "sync", gth, deps=deps)
                    tg[jj] = h.tok()

                def em_gtr(jj):
                    p = jj % 2

                    def gtr(t, p=p):
                        last = None
                        for c in range(8):
                            last = t.transpose(trpair[p][:, c * 128:(c + 1) * 128],
                                               gbuf2_[p][:, c * 128:(c + 1) * 128], ident[:])
                        return last
                    deps = [tg[jj]] + ([tev[jj - 2]] if jj >= 2 else [tok0])
                    ttr[jj] = ST(h, "tensor", gtr, deps=deps)

                def em_gev(jj):
                    p = jj % 2

                    def gev(sc, p=p):
                        sc.activation(out=lbuf2_[p][:, 0:512], in_=trpair[p][:, 0:512],
                                      func=AF.Copy)
                        return sc.activation(out=lbuf2_[p][:, 512:1024], in_=trpair[p][:, 512:1024],
                                             func=AF.Copy)
                    deps = [ttr[jj]] + ([tmm[jj - 2]] if jj >= 2 else [tok0])
                    tev[jj] = ST(h, "scalar", gev, deps=deps)

                def em_mix(jj):
                    p = jj % 2

                    def mixmm(t, p=p):
                        last = None
                        for c in range(8):
                            last = t.matmul(mixb[p][:], lbuf2_[p][:, c * 128:(c + 1) * 128],
                                            h1[:, c * TT:(c + 1) * TT], start=(c == 0), stop=(c == 7))
                        return last
                    deps = [tev[jj]] + ([tse[jj - 2]] if jj >= 2 else [tok0])
                    tmm[jj] = ST(h, "tensor", mixmm, deps=deps)

                def em_sev(jj):
                    p = jj % 2
                    deps = [tmm[jj]] + ([tb2[jj - 2]] if jj >= 2 else [tok0])
                    tse[jj] = ST(h, "vector",
                                 lambda v, p=p: v.tensor_copy(out=strip2_[p][:], in_=mixb[p][:]),
                                 deps=deps)

                def em_b2(jj):
                    p = jj % 2
                    j = jlo + jj

                    def b2mm(t, j=j, p=p, s2b=s2b):
                        last = None
                        for gpar in range(2):
                            g = j * 2 + gpar
                            base = gpar * 64
                            last = t.matmul(b2b[p][base:base + 64, 0:TT],
                                            s2b[base:base + 64, (g // 2) * 64:(g // 2) * 64 + 64],
                                            strip2_[p][base:base + 64, :], start=True, stop=True)
                        return last
                    deps = [tse[jj]] + ([tcm[jj - 2]] if jj >= 2 else [tok0])
                    tb2[jj] = ST(h, "tensor", b2mm, deps=deps)

                def em_comb(jj):
                    p = jj % 2
                    j = jlo + jj
                    if eslot == 0:
                        tcm[jj] = ST(h, "vector", lambda v, j=j, p=p, half=half: v.tensor_scalar(
                            out=h2[:, (j - half * 16) * TT:(j - half * 16 + 1) * TT], in0=b2b[p][:, 0:TT],
                            scalar1=coefb[:, 0:1], scalar2=None, op0=ALU.mult), deps=[tb2[jj]])
                    else:
                        tcm[jj] = ST(h, "vector", lambda v, j=j, p=p, half=half: v.scalar_tensor_tensor(
                            out=h2[:, (j - half * 16) * TT:(j - half * 16 + 1) * TT], in0=b2b[p][:, 0:TT],
                            scalar=coefb[:, 1:2],
                            in1=h2[:, (j - half * 16) * TT:(j - half * 16 + 1) * TT],
                            op0=ALU.mult, op1=ALU.add), deps=[tb2[jj]])

                for jj in range(18):
                    if jj < 16:
                        em_gth(jj)
                        em_gtr(jj)
                    if 1 <= jj <= 16:
                        em_gev(jj - 1)
                        em_mix(jj - 1)
                    if jj >= 2:
                        em_sev(jj - 2)
                        em_b2(jj - 2)
                        em_comb(jj - 2)

            def relu_(v):
                last = None
                for c in range(16):
                    last = v.tensor_scalar(out=h2[:, c * TT:(c + 1) * TT], in0=h2[:, c * TT:(c + 1) * TT],
                                           scalar1=0.0, scalar2=None, op0=ALU.max)
                return last
            ST(h, "vector", relu_)

            def sq_(sc):
                last = None
                for c in range(16):
                    last = sc.activation(out=h2[:, c * TT:(c + 1) * TT], in_=h2[:, c * TT:(c + 1) * TT],
                                         func=AF.Square)
                return last
            ST(h, "scalar", sq_)

            for eslot in range(2):
                s1b = pjs1 if eslot == 0 else pjs1b
                s2b = pjs2 if eslot == 0 else pjs2b
                rg = rgP0 if eslot == 0 else rgP1
                dcol = 2 + eslot

                for jg in range(4):
                    def pbd1(t, jg=jg, s1b=s1b, half=half):
                        last = None
                        for jj in range(4):
                            ch = jg * 4 + jj
                            gch = half * 16 + ch
                            for gpar in range(2):
                                g = gch * 2 + gpar
                                base = gpar * 64
                                last = t.matmul(pF[base:base + 64, jj * 512:(jj + 1) * 512],
                                                s1b[base:base + 64, (g // 2) * 64:(g // 2) * 64 + 64],
                                                h2[base:base + 64, ch * TT:(ch + 1) * TT],
                                                start=True, stop=True)
                        return last
                    ST(h, "tensor", pbd1)

                    def pbd1e(v, jg=jg):
                        last = None
                        for jj in range(4):
                            ch = jg * 4 + jj
                            last = v.tensor_copy(out=t1[:, ch * TT:(ch + 1) * TT],
                                                 in_=pF[:, jj * 512:(jj + 1) * 512])
                        return last
                    ST(h, "vector", pbd1e)

                ptok0 = h.tok()
                tpg, tpm = {}, {}

                def em_pgth(ii):
                    p = ii % 2
                    gi_ = half * 16 + ii

                    def pgth(sync, gi_=gi_, p=p, eslot=eslot):
                        r0 = eslot * 4096 + gi_ * 128
                        sync.dma_start(out=gbuf2_[p][:], in_=io["pj_sel"][r0:r0 + 128, :]).then_inc(h.ld.h, 16)
                        h.ld.bump(16)
                        return None
                    deps = ([tpm[ii - 2]] if ii >= 2 else [ptok0]) + ([tpg[ii - 1]] if ii >= 1 else [])
                    ST(h, "sync", pgth, deps=deps)
                    tpg[ii] = h.tok()

                def em_pmm(ii):
                    p = ii % 2

                    def pmm(t, ii=ii, p=p):
                        last = None
                        for o in range(8):
                            last = t.matmul(pjBanks[o], gbuf2_[p][:, o * 128:(o + 1) * 128],
                                            t1[:, ii * TT:(ii + 1) * TT],
                                            start=(ii == 0), stop=(ii == 15))
                        return last
                    deps = [tpg[ii]] + ([ptok0] if ii < 2 else [])
                    tpm[ii] = ST(h, "tensor", pmm, deps=deps)

                for ii in range(17):
                    if ii < 16:
                        em_pgth(ii)
                    if ii >= 1:
                        em_pmm(ii - 1)

                def pev(v):
                    last = None
                    for o in range(8):
                        last = v.tensor_copy(out=pmix[:, o * TT:(o + 1) * TT], in_=pjBanks[o])
                    return last
                ST(h, "vector", pev)

                for sub in range(4):
                    def pb2(t, sub=sub, s2b=s2b):
                        last = None
                        for g in range(16):
                            base = (g % 2) * 64
                            bank = pB2 if g % 2 == 0 else pTr
                            last = t.matmul(bank[:, (g // 2) * 64:(g // 2) * 64 + 64],
                                            pmix[base:base + 64, (g // 2) * TT + sub * 128:(g // 2) * TT + sub * 128 + 128],
                                            s2b[base:base + 64, (g // 2) * 64:(g // 2) * 64 + 64],
                                            start=True, stop=True)
                        return last
                    ST(h, "tensor", pb2)

                    def pacc(v, sub=sub, dcol=dcol):
                        yv = h0tm[:, sub * D:(sub + 1) * D].rearrange("p (a b k) -> p a b k", b=2, k=64)
                        ev = yv[:, :, 0:1, :]
                        ov = yv[:, :, 1:2, :]
                        pbv = pB2[:].rearrange("p (a k) -> p a k", k=64).unsqueeze(2)
                        ptv = pTr[:].rearrange("p (a k) -> p a k", k=64).unsqueeze(2)
                        v.scalar_tensor_tensor(out=ev, in0=pbv, scalar=coefb[:, dcol:dcol + 1],
                                               in1=ev, op0=ALU.mult, op1=ALU.add)
                        return v.scalar_tensor_tensor(out=ov, in0=ptv, scalar=coefb[:, dcol:dcol + 1],
                                                      in1=ov, op0=ALU.mult, op1=ALU.add)
                    ST(h, "vector", pacc)

        def ldx1(sync, t0=t0):
            for s in range(4):
                sync.dma_start(out=h0fm[:, s * D:(s + 1) * D],
                               in_=io["s1"][t0 + s * 128:t0 + (s + 1) * 128, :]).then_inc(h.ld.h, 16)
                h.ld.bump(16)
            return None
        ST(h, "sync", ldx1)
        ep0 = h.tok()
        st_toks = []
        for s in range(4):
            tr2a = ST(h, "vector", lambda v, s=s: v.tensor_tensor(
                out=h0tm[:, s * D:(s + 1) * D], in0=h0tm[:, s * D:(s + 1) * D],
                in1=xat[s][:], op=ALU.add), deps=[ep0])
            tr2 = ST(h, "vector", lambda v, s=s: v.tensor_tensor(
                out=h0tm[:, s * D:(s + 1) * D], in0=h0tm[:, s * D:(s + 1) * D],
                in1=h0fm[:, s * D:(s + 1) * D], op=ALU.add), deps=[tr2a])
            tln = emit_ln(h, h0tm[:, s * D:(s + 1) * D], h0tm[:, s * D:(s + 1) * D], trash, nfeat=D,
                          eps=1e-5, stats=stats, dep=tr2)

            def st2(sync, s=s, t0=t0):
                sync.dma_start(out=io["s2"][t0 + s * 128:t0 + (s + 1) * 128, :],
                               in_=h0tm[:, s * D:(s + 1) * D]).then_inc(h.st.h, 16)
                h.st.bump(16)
                sync.wait_ge(h.st.h, h.st.n)
                return sync.nop()
            st_toks.append(ST(h, "sync", st2, deps=[tln]))

    ST(h, "sync", lambda sync: sync.wait_ge(h.st.h, h.st.n) and None)



def emit_p3(h, io):
    ident = h.sb("v_ident", [128, 128])
    ones1 = h.sb("v_ones1", [1, 128])
    wtile = h.sb("v_wtile", [128, D])
    wr8 = h.sb("v_wr8", [128, 8])
    wr8t = h.sb("v_wr8t", [8, 128])
    rsrow = h.sb("v_rsrow", [1, D])
    rsbc = h.sb("v_rsbc", [128, D])
    x2tm = h.sb("v_x2tm", [128, 16 * D])
    x2fm = h.sb("v_x2fm", [128, 8 * T])
    scb2_ = [h.sb("v_scbA", [128, T]), h.sb("v_scbB", [128, T])]
    pmk2_ = [h.sb("v_pmkA", [128, T]), h.sb("v_pmkB", [128, T])]
    ptb2_ = [h.sb("v_ptbA", [128, T]), h.sb("v_ptbB", [128, T])]
    att2_ = [h.sb("v_attA", [128, D]), h.sb("v_attB", [128, D])]
    xres2_ = [h.sb("v_xresA", [128, D]), h.sb("v_xresB", [128, D])]
    st3 = h.sb("v_st3", [128, 24])

    pBig = h.ps("v_pBig", [128, 2048])
    pV = h.ps("v_pV", [128, 1024])
    pT2 = h.ps("v_pT2", [128, 512])

    def loads(sync):
        sync.dma_start(out=ident[:], in_=io["ident"][:]).then_inc(h.ld.h, 16)
        h.ld.bump(16)
        return None
    ST(h, "sync", loads)
    ST(h, "vector", lambda v: v.memset(ones1[:], 1.0))

    for t_ in range(8):
        def ldw(sync, t_=t_):
            sync.dma_start(out=wtile[:], in_=io["W1"][t_ * 128:(t_ + 1) * 128, :]).then_inc(h.ld.h, 16)
            h.ld.bump(16)
            return None
        ST(h, "sync", ldw)
        ST(h, "vector", lambda v, t_=t_: v.tensor_reduce(
            out=wr8[:, t_:t_ + 1], in_=wtile[:], axis=AX.X, op=ALU.add))
    ST(h, "tensor", lambda t: t.transpose(pT2[0:8, 0:128], wr8[:], ident[:]))
    ST(h, "vector", lambda v: v.tensor_copy(out=wr8t[:], in_=pT2[0:8, 0:128]))

    def wout(sync):
        sync.dma_start(out=io["rs_scr"][:], in_=wr8t[:]).then_inc(h.st.h, 16)
        h.st.bump(16)
        sync.wait_ge(h.st.h, h.st.n)
        return sync.nop()
    ST(h, "sync", wout)

    def win(sync):
        sync.dma_start(out=rsrow[:], in_=io["rs_scr"].rearrange("a b -> (a b)").unsqueeze(0)).then_inc(h.ld.h, 16)
        h.ld.bump(16)
        return None
    ST(h, "sync", win)

    def rbc(t):
        t.matmul(pT2[:, 0:512], ones1[:], rsrow[:, 0:512], start=True, stop=True)
        return t.matmul(pV[:, 0:512], ones1[:], rsrow[:, 512:1024], start=True, stop=True)
    ST(h, "tensor", rbc)

    def rbc2(v):
        v.tensor_copy(out=rsbc[:, 0:512], in_=pT2[:, 0:512])
        return v.tensor_copy(out=rsbc[:, 512:1024], in_=pV[:, 0:512])
    ST(h, "vector", rbc2)

    for seq in range(2):
        base_t = seq * T
        for ti in range(16):
            def ldx(sync, ti=ti, base_t=base_t):
                sync.dma_start(out=x2tm[:, ti * D:(ti + 1) * D],
                               in_=io["s2"][base_t + ti * 128:base_t + (ti + 1) * 128, :]).then_inc(h.ld.h, 16)
                h.ld.bump(16)
                return None
            ST(h, "sync", ldx)
            tm_to_fm(h, [x2tm[:, ti * D:(ti + 1) * D]], x2fm,
                     pBig[:, 0:512], pBig[:, 512:1024], ident, fm_stride=T, sub0=ti)

        tok0 = h.tok()
        t_pmk, t_ssum, t_zop, t_trlast, t_pelast, t_av, t_attev, t_st3 = {}, {}, {}, {}, {}, {}, {}, {}

        def front(qi, base_t=base_t):
            p = qi % 2
            b = p * 12
            pair = pBig[:, p * 1024:(p + 1) * 1024]
            t_prev = None
            for r in range(2):
                def smm(t, qi=qi, r=r, pair=pair):
                    last = None
                    for kb in range(2):
                        for c in range(8):
                            last = t.matmul(pair[:, kb * 512:(kb + 1) * 512],
                                            x2fm[:, c * T + qi * 128:c * T + qi * 128 + 128],
                                            x2fm[:, c * T + (r * 2 + kb) * 512:c * T + (r * 2 + kb + 1) * 512],
                                            start=(c == 0), stop=(c == 7))
                    return last
                deps = [t_prev] if t_prev is not None else (
                    [t_pelast[qi - 2]] if qi >= 2 else [tok0])
                tsc = ST(h, "tensor", smm, deps=deps)

                def sev(v, r=r, p=p, pair=pair):
                    v.tensor_scalar(out=scb2_[p][:, r * 1024:r * 1024 + 512], in0=pair[:, 0:512],
                                    scalar1=0.03125, scalar2=None, op0=ALU.mult)
                    return v.tensor_scalar(out=scb2_[p][:, r * 1024 + 512:(r + 1) * 1024],
                                           in0=pair[:, 512:1024],
                                           scalar1=0.03125, scalar2=None, op0=ALU.mult)
                t_prev = ST(h, "vector", sev, deps=[tsc])

            tmax = ST(h, "vector", lambda v, p=p, b=b: v.tensor_reduce(
                out=st3[:, b + 0:b + 1], in_=scb2_[p][:], axis=AX.X, op=ALU.max, negate=True),
                deps=[t_prev])
            texp = ST(h, "scalar", lambda sc, p=p, b=b: sc.activation(
                out=scb2_[p][:], in_=scb2_[p][:], func=AF.Exp,
                bias=st3[:, b + 0:b + 1], accum_out=st3[:, b + 1:b + 2]), deps=[tmax])

            def zop(v, b=b):
                v.reciprocal(out=st3[:, b + 2:b + 3], in_=st3[:, b + 1:b + 2])
                return v.tensor_scalar(out=st3[:, b + 3:b + 4], in0=st3[:, b + 1:b + 2],
                                       scalar1=0.01, scalar2=None, op0=ALU.mult)
            t_zop[qi] = ST(h, "vector", zop, deps=[texp])

            depsm = [t_zop[qi]] + ([t_trlast[qi - 2]] if qi >= 2 else [])
            t_pmk[qi] = ST(h, "vector", lambda v, p=p, b=b: v.scalar_tensor_tensor(
                out=pmk2_[p][:], in0=scb2_[p][:], scalar=st3[:, b + 3:b + 4], in1=scb2_[p][:],
                op0=ALU.is_ge, op1=ALU.mult), deps=depsm)
            t_ssum[qi] = ST(h, "vector", lambda v, p=p, b=b: v.tensor_reduce(
                out=st3[:, b + 4:b + 5], in_=pmk2_[p][:], axis=AX.X, op=ALU.add), deps=[t_pmk[qi]])

        def back(qi, base_t=base_t):
            p = qi % 2
            b = p * 12
            pair = pBig[:, p * 1024:(p + 1) * 1024]
            t_prev = t_pmk[qi]
            for r in range(2):
                def ptr(t, r=r, p=p, pair=pair):
                    last = None
                    for b_ in range(8):
                        ks = r * 8 + b_
                        last = t.transpose(pair[:, b_ * 128:(b_ + 1) * 128],
                                           pmk2_[p][:, ks * 128:(ks + 1) * 128], ident[:])
                    return last
                ttr = ST(h, "tensor", ptr, deps=[t_prev])

                def pev(sc, r=r, p=p, pair=pair):
                    sc.activation(out=ptb2_[p][:, r * 1024:r * 1024 + 512],
                                  in_=pair[:, 0:512], func=AF.Copy)
                    return sc.activation(out=ptb2_[p][:, r * 1024 + 512:(r + 1) * 1024],
                                         in_=pair[:, 512:1024], func=AF.Copy)
                depsv = [ttr] + ([t_av[qi - 2]] if qi >= 2 and r == 0 else [])
                t_prev = ST(h, "scalar", pev, deps=depsv)
            t_trlast[qi] = t_prev
            t_pelast[qi] = t_prev

            def avmm(t, p=p):
                last = None
                for half in range(2):
                    for ks in range(16):
                        last = t.matmul(pV[:, half * 512:(half + 1) * 512],
                                        ptb2_[p][:, ks * 128:(ks + 1) * 128],
                                        x2tm[:, ks * D + half * 512:ks * D + (half + 1) * 512],
                                        start=(ks == 0), stop=(ks == 15))
                return last
            depsav = [t_prev] + ([t_attev[qi - 1]] if qi >= 1 else [])
            t_av[qi] = ST(h, "tensor", avmm, deps=depsav)

            def sfr(v, b=b):
                return v.tensor_tensor(out=st3[:, b + 5:b + 6], in0=st3[:, b + 4:b + 5],
                                       in1=st3[:, b + 2:b + 3], op=ALU.mult)
            tsf = ST(h, "vector", sfr, deps=[t_ssum[qi]])

            def den(v, b=b):
                v.tensor_scalar(out=st3[:, b + 6:b + 7], in0=st3[:, b + 5:b + 6],
                                scalar1=1e-9, scalar2=None, op0=ALU.add)
                return v.tensor_scalar(out=st3[:, b + 7:b + 8], in0=st3[:, b + 5:b + 6],
                                       scalar1=-0.01, scalar2=0.01, op0=ALU.mult, op1=ALU.add)
            tdn = ST(h, "vector", den, deps=[tsf])
            tdi = ST(h, "vector", lambda v, b=b: v.reciprocal(
                out=st3[:, b + 6:b + 7], in_=st3[:, b + 6:b + 7]), deps=[tdn])
            tsc2 = ST(h, "vector", lambda v, b=b: v.tensor_tensor(
                out=st3[:, b + 6:b + 7], in0=st3[:, b + 6:b + 7], in1=st3[:, b + 2:b + 3],
                op=ALU.mult), deps=[tdi])

            def attev(v, p=p, b=b):
                v.tensor_scalar(out=att2_[p][:, 0:512], in0=pV[:, 0:512],
                                scalar1=st3[:, b + 6:b + 7], scalar2=None, op0=ALU.mult)
                return v.tensor_scalar(out=att2_[p][:, 512:1024], in0=pV[:, 512:1024],
                                       scalar1=st3[:, b + 6:b + 7], scalar2=None, op0=ALU.mult)
            t_attev[qi] = ST(h, "vector", attev, deps=[t_av[qi], tsc2])

            tcas = ST(h, "vector", lambda v, p=p, b=b: v.scalar_tensor_tensor(
                out=att2_[p][:], in0=rsbc[:], scalar=st3[:, b + 7:b + 8], in1=att2_[p][:],
                op0=ALU.mult, op1=ALU.add), deps=[t_attev[qi]])

            depsr = [tcas] + ([t_st3[qi - 1]] if qi >= 1 else [])
            tres = ST(h, "vector", lambda v, qi=qi, p=p: v.tensor_tensor(
                out=xres2_[p][:], in0=att2_[p][:], in1=x2tm[:, qi * D:(qi + 1) * D],
                op=ALU.add), deps=depsr)

            tln = emit_ln(h, xres2_[p][:], xres2_[p][:], xres2_[1 - p][:], nfeat=D, eps=1e-5,
                          stats=st3[:, 16:24], dep=tres)

            def st3f(sync, qi=qi, p=p, base_t=base_t):
                sync.dma_start(out=io["s3"][base_t + qi * 128:base_t + (qi + 1) * 128, :],
                               in_=xres2_[p][:]).then_inc(h.st.h, 16)
                h.st.bump(16)
                sync.wait_ge(h.st.h, h.st.n)
                return sync.nop()
            t_st3[qi] = ST(h, "sync", st3f, deps=[tln])

        for step in range(17):
            if step < 16:
                front(step)
            if step >= 1:
                back(step - 1)

    ST(h, "sync", lambda sync: sync.wait_ge(h.st.h, h.st.n) and None)



def emit_p4(h, io):
    ident = h.sb("t_ident", [128, 128])
    ones1 = h.sb("t_ones1", [1, 128])
    brow = h.sb("t_brow", [1, D])
    bbc = h.sb("t_bbc", [128, D])
    wrT = h.sb("t_wrT", [128, 4 * 512])
    wiT = h.sb("t_wiT", [128, 4 * 512])
    wiTn = h.sb("t_wiTn", [128, 4 * 512])
    abuf = h.sb("t_abuf", [128, 512])
    bbuf = h.sb("t_bbuf", [128, 512])
    tmpw = h.sb("t_tmpw", [128, 512])
    x3t = h.sb("t_x3t", [128, D])
    x3fm = h.sb("t_x3fm", [128, 8 * 128])
    yb = h.sb("t_yb", [128, D])
    stats = h.sb("t_stats", [128, 8])
    trash = h.sb("t_trash", [128, D])

    pR = h.ps("t_pR", [128, 512])
    pI = h.ps("t_pI", [128, 512])
    pT3 = h.ps("t_pT3", [128, 512])
    pT4 = h.ps("t_pT4", [128, 512])

    def loads(sync):
        sync.dma_start(out=ident[:], in_=io["ident"][:]).then_inc(h.ld.h, 16)
        sync.dma_start(out=brow[:], in_=io["bias_ri_row"][:]).then_inc(h.ld.h, 16)
        h.ld.bump(32)
        return None
    ST(h, "sync", loads)
    ST(h, "vector", lambda v: v.memset(ones1[:], 1.0))

    def bbc_mm(t):
        t.matmul(pR[:], ones1[:], brow[:, 0:512], start=True, stop=True)
        return t.matmul(pI[:], ones1[:], brow[:, 512:1024], start=True, stop=True)
    ST(h, "tensor", bbc_mm)

    def bbc_ev(v):
        v.tensor_copy(out=bbc[:, 0:512], in_=pR[:])
        return v.tensor_copy(out=bbc[:, 512:1024], in_=pI[:])
    ST(h, "vector", bbc_ev)

    for part, dst in [("R", wrT), ("I", wiT)]:
        for ch in range(4):
            for cc in range(4):
                def lab(sync, part=part, ch=ch, cc=cc):
                    sync.dma_start(out=abuf[:], in_=io[f"Abig{part}"][cc, ch * 128:(ch + 1) * 128, :]).then_inc(h.ld.h, 16)
                    sync.dma_start(out=bbuf[:], in_=io[f"Bbig{part}"][cc, ch * 128:(ch + 1) * 128, :]).then_inc(h.ld.h, 16)
                    h.ld.bump(32)
                    return None
                ST(h, "sync", lab)
                if cc == 0:
                    ST(h, "vector", lambda v, dst=dst, ch=ch: v.tensor_tensor(
                        out=dst[:, ch * 512:(ch + 1) * 512], in0=abuf[:], in1=bbuf[:], op=ALU.mult))
                else:
                    ST(h, "vector", lambda v: v.tensor_tensor(
                        out=tmpw[:], in0=abuf[:], in1=bbuf[:], op=ALU.mult))
                    ST(h, "vector", lambda v, dst=dst, ch=ch: v.tensor_tensor(
                        out=dst[:, ch * 512:(ch + 1) * 512], in0=dst[:, ch * 512:(ch + 1) * 512],
                        in1=tmpw[:], op=ALU.add))
    ST(h, "vector", lambda v: v.tensor_scalar(out=wiTn[:], in0=wiT[:], scalar1=-1.0,
                                              scalar2=None, op0=ALU.mult))

    for ti in range(NTOK // 128):
        t0 = ti * 128

        def ldx(sync, t0=t0):
            sync.dma_start(out=x3t[:], in_=io["s3"][t0:t0 + 128, :]).then_inc(h.ld.h, 16)
            h.ld.bump(16)
            return None
        ST(h, "sync", ldx)

        tm_to_fm(h, [x3t], x3fm, pT3, pT4, ident, fm_stride=128, sub0=0)

        def ttmm(t):
            last = None
            for i4 in range(4):
                t.matmul(pR[:], x3fm[:, i4 * 128:(i4 + 1) * 128], wrT[:, i4 * 512:(i4 + 1) * 512],
                         start=(i4 == 0), stop=False)
                t.matmul(pI[:], x3fm[:, i4 * 128:(i4 + 1) * 128], wiT[:, i4 * 512:(i4 + 1) * 512],
                         start=(i4 == 0), stop=False)
            for i4 in range(4):
                t.matmul(pR[:], x3fm[:, (4 + i4) * 128:(5 + i4) * 128], wiTn[:, i4 * 512:(i4 + 1) * 512],
                         start=False, stop=(i4 == 3))
                last = t.matmul(pI[:], x3fm[:, (4 + i4) * 128:(5 + i4) * 128], wrT[:, i4 * 512:(i4 + 1) * 512],
                                start=False, stop=(i4 == 3))
            return last
        ST(h, "tensor", ttmm)

        def yev(v):
            v.tensor_tensor(out=yb[:, 0:512], in0=pR[:], in1=bbc[:, 0:512], op=ALU.add)
            return v.tensor_tensor(out=yb[:, 512:1024], in0=pI[:], in1=bbc[:, 512:1024], op=ALU.add)
        ST(h, "vector", yev)
        ST(h, "vector", lambda v: v.tensor_tensor(out=yb[:], in0=yb[:], in1=x3t[:], op=ALU.add))
        emit_ln(h, yb[:], yb[:], trash[:], nfeat=D, eps=1e-5, stats=stats)

        def st4(sync, t0=t0):
            sync.dma_start(out=io["out"][t0:t0 + 128, :], in_=yb[:]).then_inc(h.st.h, 16)
            h.st.bump(16)
            sync.wait_ge(h.st.h, h.st.n)
            return sync.nop()
        ST(h, "sync", st4)

    ST(h, "sync", lambda sync: sync.wait_ge(h.st.h, h.st.n) and None)


def build_p4():
    nc = bass.Bass("TRN2", target_bir_lowering=False)
    with ExitStack() as ctx:
        io = {}
        io["s3"] = nc.dram_tensor("s3", [NTOK, D], F32, kind="ExternalInput").ap()
        for nm, shape in [("AbigR", [4, 512, 512]), ("BbigR", [4, 512, 512]),
                          ("AbigI", [4, 512, 512]), ("BbigI", [4, 512, 512]),
                          ("bias_ri_row", [1, D]), ("ident", [128, 128])]:
            io[nm] = nc.dram_tensor(nm, shape, F32, kind="ExternalInput").ap()
        io["out"] = nc.dram_tensor("out", [NTOK, D], F32, kind="ExternalOutput").ap()
        block = ctx.enter_context(nc.Block())
        h = H(nc, ctx, block)
        emit_p4(h, io)
    return nc



def build_p0a(variant):
    nc = bass.Bass("TRN2", target_bir_lowering=False)
    with ExitStack() as ctx:
        io = {}
        io["x0"] = nc.dram_tensor("x0", [NTOK, D], F32, kind="ExternalInput").ap()
        for nm, shape in [("wx1T", [D, 160]), ("w2nT", [5, 32, D]), ("mu_x", [D, 1]),
                          ("x_biasT", [D, 5]), ("ident", [128, 128])]:
            io[nm] = nc.dram_tensor(nm, shape, F32, kind="ExternalInput").ap()
        if variant == "A":
            for nm, shape in [("wrT", [D, 512]), ("wkT", [D, 512]), ("ww1T", [D, 64]),
                              ("ww2T", [64, 512]), ("bw2", [512, 1])]:
                io[nm] = nc.dram_tensor(nm, shape, F32, kind="ExternalInput").ap()
            for nm in ["rsc", "ksc", "wsc"]:
                io[nm] = nc.dram_tensor(nm, [512, NTOK], F32, kind="ExternalOutput").ap()
        else:
            for nm, shape in [("wvT", [D, D]), ("wgT", [D, D])]:
                io[nm] = nc.dram_tensor(nm, shape, F32, kind="ExternalInput").ap()
            for nm in ["vsc", "gsc"]:
                io[nm] = nc.dram_tensor(nm, [NTOK, D], F32, kind="ExternalOutput").ap()
        block = ctx.enter_context(nc.Block())
        h = H(nc, ctx, block)
        emit_p0a(h, io, variant)
    return nc


def build_p0b(debug=False):
    nc = bass.Bass("TRN2", target_bir_lowering=False)
    with ExitStack() as ctx:
        io = {}
        io["x0"] = nc.dram_tensor("x0", [NTOK, D], F32, kind="ExternalInput").ap()
        for nm, shape in [("woT", [D, D]), ("bonusT", [128, 4]), ("ident", [128, 128]),
                          ("mask64", [64, 64]), ("ln_w", [1, 256]), ("ln_b", [1, 256])]:
            io[nm] = nc.dram_tensor(nm, shape, F32, kind="ExternalInput").ap()
        for nm in ["rsc", "ksc", "wsc"]:
            io[nm] = nc.dram_tensor(nm, [512, NTOK], F32, kind="ExternalInput").ap()
        for nm in ["vsc", "gsc"]:
            io[nm] = nc.dram_tensor(nm, [NTOK, D], F32, kind="ExternalInput").ap()
        io["s1"] = nc.dram_tensor("s1", [NTOK, D], F32, kind="ExternalOutput").ap()
        if debug:
            io["d_buf"] = nc.dram_tensor("d_buf", [128, 1024], F32, kind="ExternalOutput").ap()
            io["d_otm"] = nc.dram_tensor("d_otm", [512, D], F32, kind="ExternalOutput").ap()
        block = ctx.enter_context(nc.Block())
        h = H(nc, ctx, block)
        emit_p0b(h, io, debug=debug)
    return nc


def build_p3():
    nc = bass.Bass("TRN2", target_bir_lowering=False)
    with ExitStack() as ctx:
        io = {}
        io["s2"] = nc.dram_tensor("s2", [NTOK, D], F32, kind="ExternalInput").ap()
        io["W1"] = nc.dram_tensor("W1", [D, D], F32, kind="ExternalInput").ap()
        io["ident"] = nc.dram_tensor("ident", [128, 128], F32, kind="ExternalInput").ap()
        io["s3"] = nc.dram_tensor("s3", [NTOK, D], F32, kind="ExternalOutput").ap()
        io["rs_scr"] = nc.dram_tensor("rs_scr", [8, 128], F32).ap()
        block = ctx.enter_context(nc.Block())
        h = H(nc, ctx, block)
        emit_p3(h, io)
    return nc


def build_p2():
    nc = bass.Bass("TRN2", target_bir_lowering=False)
    with ExitStack() as ctx:
        io = {}
        io["xa"] = nc.dram_tensor("xa", [NTOK, D], F32, kind="ExternalInput").ap()
        io["s1"] = nc.dram_tensor("s1", [NTOK, D], F32, kind="ExternalInput").ap()
        for nm, shape in [("ident", [128, 128]), ("fc_logits_row", [1, 8]), ("pj_logits_row", [1, 8]),
                          ("fc_mixB", [BANK, 32, 128, D]), ("pj_mixTB", [BANK, 32, 128, D]),
                          ("fc_s1f", [BANK * D, BLK]), ("fc_s2f", [BANK * HID, BLK]),
                          ("pj_s1f", [BANK * HID, BLK]), ("pj_s2f", [BANK * D, BLK])]:
            io[nm] = nc.dram_tensor(nm, shape, F32, kind="ExternalInput").ap()
        io["s2"] = nc.dram_tensor("s2", [NTOK, D], F32, kind="ExternalOutput").ap()
        io["fc_sel"] = nc.dram_tensor("fc_sel", [2 * 4096, D], F32).ap()
        io["pj_sel"] = nc.dram_tensor("pj_sel", [2 * 4096, D], F32).ap()
        block = ctx.enter_context(nc.Block())
        h = H(nc, ctx, block)
        emit_p2(h, io)
    return nc


def build_p1():
    nc = bass.Bass("TRN2", target_bir_lowering=False)
    with ExitStack() as ctx:
        io = {}
        io["s1"] = nc.dram_tensor("s1", [NTOK, D], F32, kind="ExternalInput").ap()
        for nm, shape in [("wqT2", [D, D]), ("wkT2", [D, 256]), ("wvT2", [D, 256]),
                          ("woT2", [D, D]), ("rope_cos", [T, 32]), ("rope_sin", [T, 32]),
                          ("q_gain_row", [1, H0]), ("k_gain_row", [1, HKV]),
                          ("ident", [128, 128]), ("tri_neg", [128, 128])]:
            io[nm] = nc.dram_tensor(nm, shape, F32, kind="ExternalInput").ap()
        io["xa"] = nc.dram_tensor("xa", [NTOK, D], F32, kind="ExternalOutput").ap()
        block = ctx.enter_context(nc.Block())
        h = H(nc, ctx, block)
        emit_p1(h, io)
    return nc



def build_fused():
    nc = bass.Bass("TRN2", target_bir_lowering=False)
    with ExitStack() as ctx:
        io = {}
        io["x0"] = nc.dram_tensor("x0", [NTOK, D], F32, kind="ExternalInput").ap()
        ext = [("wx1T", [D, 160]), ("w2nT", [5, 32, D]), ("mu_x", [D, 1]),
               ("x_biasT", [D, 5]), ("ident", [128, 128]),
               ("wrT", [D, 512]), ("wkT", [D, 512]), ("ww1T", [D, 64]),
               ("ww2T", [64, 512]), ("bw2", [512, 1]),
               ("wvT", [D, D]), ("wgT", [D, D]),
               ("woT", [D, D]), ("bonusT", [128, 4]), ("mask64", [64, 64]),
               ("ln_w", [1, 256]), ("ln_b", [1, 256]),
               ("wqT2", [D, D]), ("wkT2", [D, 256]), ("wvT2", [D, 256]),
               ("woT2", [D, D]), ("rope_cos", [T, 32]), ("rope_sin", [T, 32]),
               ("q_gain_row", [1, H0]), ("k_gain_row", [1, HKV]), ("tri_neg", [128, 128]),
               ("fc_logits_row", [1, 8]), ("pj_logits_row", [1, 8]),
               ("fc_mixB", [BANK, 32, 128, D]), ("pj_mixTB", [BANK, 32, 128, D]),
               ("fc_s1f", [BANK * D, BLK]), ("fc_s2f", [BANK * HID, BLK]),
               ("pj_s1f", [BANK * HID, BLK]), ("pj_s2f", [BANK * D, BLK]),
               ("W1", [D, D]),
               ("AbigR", [4, 512, 512]), ("BbigR", [4, 512, 512]),
               ("AbigI", [4, 512, 512]), ("BbigI", [4, 512, 512]),
               ("bias_ri_row", [1, D])]
        for nm, shape in ext:
            io[nm] = nc.dram_tensor(nm, shape, F32, kind="ExternalInput").ap()
        for nm, shape in [("rsc", [512, NTOK]), ("ksc", [512, NTOK]), ("wsc", [512, NTOK]),
                          ("vsc", [NTOK, D]), ("gsc", [NTOK, D]),
                          ("s1", [NTOK, D]), ("xa", [NTOK, D]), ("s2", [NTOK, D]),
                          ("s3", [NTOK, D]), ("fc_sel", [2 * 4096, D]),
                          ("pj_sel", [2 * 4096, D]), ("rs_scr", [8, 128])]:
            io[nm] = nc.dram_tensor(nm, shape, F32).ap()
        io["out"] = nc.dram_tensor("out", [NTOK, D], F32, kind="ExternalOutput").ap()

        block = ctx.enter_context(nc.Block())
        h = H(nc, ctx, block)
        phases = [("A_", lambda: emit_p0a(h, io, "A")),
                  ("B_", lambda: emit_p0a(h, io, "B")),
                  ("C_", lambda: emit_p0b(h, io)),
                  ("D_", lambda: emit_p1(h, io)),
                  ("E_", lambda: emit_p2(h, io)),
                  ("F_", lambda: emit_p3(h, io)),
                  ("G_", lambda: emit_p4(h, io))]
        outer_ctx = h.ctx
        for pfx, fn in phases:
            with ExitStack() as pctx:
                h.ctx = pctx
                h.pfx = pfx
                fn()
            h.ctx = outer_ctx
        h.pfx = ""
    return nc


_BUILD_CACHE = {}


def _get(name, builder):
    if name not in _BUILD_CACHE:
        _BUILD_CACHE[name] = builder()
    return _BUILD_CACHE[name]


def _kernel_unfused(**inputs):
    g = _prep(inputs)
    x0 = np.asarray(inputs["L0_attn_x"], dtype=np.float32)
    xs = [np.ascontiguousarray(x0[2 * c:2 * c + 2].reshape(NTOK, D)) for c in range(NCORES)]
    cores = list(range(NCORES))

    def run(nc, per_core):
        res = run_bass_kernel_spmd(nc, per_core, core_ids=cores)
        return res.results

    base = {"wx1T": g["wx1T"], "w2nT": g["w2nT"], "mu_x": g["mu_x"],
            "x_biasT": g["x_biasT"], "ident": g["ident"]}
    imA = [dict(base, x0=xs[c], wrT=g["wrT"], wkT=g["wkT"], ww1T=g["ww1T"],
                ww2T=g["ww2T"], bw2=g["bw2"]) for c in cores]
    rA = run(_get("p0aA", lambda: build_p0a("A")), imA)
    imB = [dict(base, x0=xs[c], wvT=g["wvT"], wgT=g["wgT"]) for c in cores]
    rB = run(_get("p0aB", lambda: build_p0a("B")), imB)
    imC = [{"x0": xs[c], "woT": g["woT"], "bonusT": g["bonusT"], "ident": g["ident"],
            "mask64": g["mask64"], "ln_w": g["ln_w"], "ln_b": g["ln_b"],
            "rsc": rA[c]["rsc"], "ksc": rA[c]["ksc"], "wsc": rA[c]["wsc"],
            "vsc": rB[c]["vsc"], "gsc": rB[c]["gsc"]} for c in cores]
    rC = run(_get("p0b", lambda: build_p0b(False)), imC)
    imD = [{"s1": rC[c]["s1"], "wqT2": g["wqT2"], "wkT2": g["wkT2"], "wvT2": g["wvT2"],
            "woT2": g["woT2"], "rope_cos": g["rope_cos"], "rope_sin": g["rope_sin"],
            "q_gain_row": g["q_gain_row"], "k_gain_row": g["k_gain_row"],
            "ident": g["ident"], "tri_neg": g["tri_neg"]} for c in cores]
    rD = run(_get("p1", build_p1), imD)
    imE = [{"xa": rD[c]["xa"], "s1": rC[c]["s1"], "ident": g["ident"], "fc_logits_row": g["fc_logits_row"],
            "pj_logits_row": g["pj_logits_row"], "fc_mixB": g["fc_mixB"],
            "pj_mixTB": g["pj_mixTB"], "fc_s1f": g["fc_s1f"], "fc_s2f": g["fc_s2f"],
            "pj_s1f": g["pj_s1f"], "pj_s2f": g["pj_s2f"]} for c in cores]
    rE = run(_get("p2", build_p2), imE)
    W1 = np.ascontiguousarray(np.asarray(inputs["L1_attn_W"], dtype=np.float32))
    imF = [{"s2": rE[c]["s2"], "W1": W1, "ident": g["ident"]} for c in cores]
    rF = run(_get("p3", build_p3), imF)
    imG = [{"s3": rF[c]["s3"], "AbigR": g["AbigR"], "BbigR": g["BbigR"],
            "AbigI": g["AbigI"], "BbigI": g["BbigI"],
            "bias_ri_row": g["bias_ri_row"], "ident": g["ident"]} for c in cores]
    rG = run(_get("p4", build_p4), imG)

    out = np.stack([np.asarray(rG[c]["out"]) for c in cores])
    return out.reshape(16, T, D).astype(np.float32)


def kernel(**inputs):
    g = _prep(inputs)
    x0 = np.asarray(inputs["L0_attn_x"], dtype=np.float32)
    xs = [np.ascontiguousarray(x0[2 * c:2 * c + 2].reshape(NTOK, D)) for c in range(NCORES)]
    W1 = np.ascontiguousarray(np.asarray(inputs["L1_attn_W"], dtype=np.float32))
    nc = _get("fused", build_fused)
    names = ["wx1T", "w2nT", "mu_x", "x_biasT", "ident", "wrT", "wkT", "ww1T",
             "ww2T", "bw2", "wvT", "wgT", "woT", "bonusT", "mask64", "ln_w", "ln_b",
             "wqT2", "wkT2", "wvT2", "woT2", "rope_cos", "rope_sin",
             "q_gain_row", "k_gain_row", "tri_neg", "fc_logits_row", "pj_logits_row",
             "fc_mixB", "pj_mixTB", "fc_s1f", "fc_s2f", "pj_s1f", "pj_s2f",
             "AbigR", "BbigR", "AbigI", "BbigI", "bias_ri_row"]
    base = {nm: g[nm] for nm in names}
    base["W1"] = W1
    in_maps = [dict(base, x0=xs[c]) for c in range(NCORES)]
    res = run_bass_kernel_spmd(nc, in_maps, core_ids=list(range(NCORES)))
    out = np.stack([np.asarray(res.results[c]["out"]) for c in range(NCORES)])
    return out.reshape(16, T, D).astype(np.float32)


# version 10
# speedup vs baseline: 1.1042x; 1.1042x over previous
import numpy as np
from contextlib import ExitStack

import concourse.bass as bass
import concourse.mybir as mybir
from concourse.bass_utils import run_bass_kernel_spmd

F32 = mybir.dt.float32
BF16 = mybir.dt.bfloat16
I32 = mybir.dt.int32
U32 = mybir.dt.uint32
AF = mybir.ActivationFunctionType
ALU = mybir.AluOpType
AX = mybir.AxisListType

NCORES = 8
T = 2048
NTOK = 2 * T
D = 1024
TT = 512
NTILE = NTOK // TT
NH, DK, DV, CH = 4, 128, 256, 64
H0, HKV, HD = 16, 4, 64
HID, BLK, BANK = 4096, 64, 8



def _prep(inputs):
    g = {}
    c = np.ascontiguousarray
    g["wx1T"] = c(inputs["L0_attn_Wx1"].T)
    g["w2nT"] = c(np.transpose(np.reshape(np.asarray(inputs["L0_attn_Wx2"]), (D, 5, 32)), (1, 2, 0)))
    g["mu_x"] = c(np.asarray(inputs["L0_attn_mu_x"]).reshape(D, 1))
    g["x_biasT"] = c(np.asarray(inputs["L0_attn_x_bias"]).T)
    g["wrT"] = c(inputs["L0_attn_Wr"].T)
    g["wkT"] = c(inputs["L0_attn_Wk"].T)
    g["wvT"] = c(inputs["L0_attn_Wv"].T)
    g["wgT"] = c(inputs["L0_attn_Wg"].T)
    g["ww1T"] = c(inputs["L0_attn_Ww1"].T)
    g["ww2T"] = c(inputs["L0_attn_Ww2"].T)
    g["bw2"] = c(np.asarray(inputs["L0_attn_bw2"]).reshape(512, 1))
    g["bonusT"] = c(np.asarray(inputs["L0_attn_bonus"]).T)
    g["ln_w"] = c(np.asarray(inputs["L0_attn_ln_w"]).reshape(1, 256))
    g["ln_b"] = c(np.asarray(inputs["L0_attn_ln_b"]).reshape(1, 256))
    g["woT"] = c(inputs["L0_attn_Wo"].T)
    P2H = [0, 4, 1, 5, 2, 6, 3, 7, 8, 12, 9, 13, 10, 14, 11, 15]
    wq = np.asarray(inputs["L0_ffn_wq"])
    wqT = wq.T.reshape(D, H0, HD)
    g["wqT2"] = c(wqT[:, P2H, :].reshape(D, D))
    g["wkT2"] = c(inputs["L0_ffn_wk"].T)
    g["wvT2"] = c(inputs["L0_ffn_wv"].T)
    woT = np.asarray(inputs["L0_ffn_wo"]).T.reshape(H0, HD, D)
    g["woT2"] = c(woT[P2H].reshape(D, D))
    g["q_gain_row"] = c(np.asarray(inputs["L0_ffn_q_gain"])[P2H].reshape(1, H0))
    g["k_gain_row"] = c(np.asarray(inputs["L0_ffn_k_gain"]).reshape(1, HKV))
    tpos = np.arange(T, dtype=np.float64)
    inv = 1.0 / (10000.0 ** (np.arange(0, HD, 2, dtype=np.float64) / HD))
    fr = np.outer(tpos, inv)
    g["rope_cos"] = c(np.cos(fr).astype(np.float32))
    g["rope_sin"] = c(np.sin(fr).astype(np.float32))
    g["tri_neg"] = c(np.triu(np.full((128, 128), -1e30, np.float32), 1))
    def mix_perm(n_in):
        idx = np.empty(n_in * BLK, np.int64)
        out = []
        for e in range(BANK):
            for gg in range(n_in):
                for cc in range(BLK):
                    idx[gg * BLK + cc] = cc * n_in + (gg + e) % n_in
            out.append(idx.copy())
        return out
    fcp = mix_perm(16)
    pjp = mix_perm(64)
    fc_mix = np.asarray(inputs["L0_ffn_fc_mix"])
    pj_mix = np.asarray(inputs["L0_ffn_pj_mix"])
    g["fc_mixB"] = c(np.stack([fc_mix[e][:, fcp[e]].reshape(32, 128, D) for e in range(BANK)]))
    g["pj_mixTB"] = c(np.stack([np.ascontiguousarray(pj_mix[e][:, pjp[e]].T).reshape(32, 128, D)
                                for e in range(BANK)]))
    g["fc_s1f"] = c(np.asarray(inputs["L0_ffn_fc_s1"]).reshape(BANK * D, BLK))
    g["fc_s2f"] = c(np.asarray(inputs["L0_ffn_fc_s2"]).reshape(BANK * HID, BLK))
    g["pj_s1f"] = c(np.asarray(inputs["L0_ffn_pj_s1"]).reshape(BANK * HID, BLK))
    g["pj_s2f"] = c(np.asarray(inputs["L0_ffn_pj_s2"]).reshape(BANK * D, BLK))
    g["fc_logits_row"] = c(np.asarray(inputs["L0_ffn_fc_logits"]).reshape(1, BANK))
    g["pj_logits_row"] = c(np.asarray(inputs["L0_ffn_pj_logits"]).reshape(1, BANK))
    for part in ("R", "I"):
        G1 = np.asarray(inputs[f"L1_ffn_G1_{part}"])
        G2 = np.asarray(inputs[f"L1_ffn_G2_{part}"])
        Ab = np.stack([np.broadcast_to(G1[0, :, cc, :].T[:, None, :, None],
                                       (32, 16, 32, 16)).reshape(512, 512) for cc in range(4)])
        Bb = np.stack([np.broadcast_to(G2[cc, :, 0, :].T[None, :, None, :],
                                       (32, 16, 32, 16)).reshape(512, 512) for cc in range(4)])
        g[f"Abig{part}"] = c(Ab)
        g[f"Bbig{part}"] = c(Bb)
    g["bias_ri_row"] = c(np.concatenate([np.asarray(inputs["L1_ffn_bias_real"]),
                                         np.asarray(inputs["L1_ffn_bias_imag"])]).reshape(1, D))
    g["ident"] = np.eye(128, dtype=np.float32)
    g["mask64"] = c(np.triu(np.ones((CH, CH), np.float32), 1))
    for k in g:
        g[k] = np.asarray(g[k], dtype=np.float32)
    return g



class Sem:
    def __init__(self, ctx, nc, name):
        self.h = ctx.enter_context(nc.semaphore(name))
        self.n = 0

    def bump(self, k=1):
        self.n += k
        return self.n


ENGS = ("sync", "tensor", "vector", "scalar", "gpsimd")


class H:
    def __init__(self, nc, ctx, block):
        self.nc = nc
        self.ctx = ctx
        self.block = block
        self.pfx = ""
        self.ld = Sem(ctx, nc, "ld")
        self.st = Sem(ctx, nc, "st")
        self.gd = Sem(ctx, nc, "gd")
        self.esem = {e: Sem(ctx, nc, "es_" + e) for e in ENGS}
        self.waited = {e: {} for e in ENGS}

    def tok(self):
        t = {e: self.esem[e].n for e in ENGS}
        t["ld"] = self.ld.n
        t["st"] = self.st.n
        t["gd"] = self.gd.n
        return t

    def sb(self, name, shape, dt=F32):
        return self.ctx.enter_context(self.nc.sbuf_tensor("s_" + self.pfx + name, shape, dt))

    def ps(self, name, shape, dt=F32):
        return self.ctx.enter_context(self.nc.psum_tensor("p_" + self.pfx + name, shape, dt))


def ST(h, eng, body, deps=None):
    if deps is None:
        tgt = h.tok()
        if eng != "sync":
            tgt.pop("st", None)
    else:
        tgt = {}
        for d in deps:
            for k, v in d.items():
                if v > tgt.get(k, 0):
                    tgt[k] = v
    w = h.waited[eng]
    sems = dict(h.esem)
    sems["ld"] = h.ld
    sems["st"] = h.st
    sems["gd"] = h.gd
    holder = {}

    def _f(e):
        for k, v in tgt.items():
            if v <= 0:
                continue
            if v > w.get(k, 0):
                e.wait_ge(sems[k].h, v)
                w[k] = v
        holder["last"] = body(e)

    getattr(h.block, eng)(_f)
    last = holder.get("last")
    if last is not None:
        last.then_inc(h.esem[eng].h, 1)
        h.esem[eng].bump()
    return h.tok()


def tm_to_fm(h, tm_tiles, fm, pA, pB, ident, nchunk=8, fm_stride=TT, sub0=0):
    for s in range(len(tm_tiles)):
        def tbody(t, s=s):
            last = None
            for c in range(nchunk):
                bank = pA if c < 4 else pB
                last = t.transpose(bank[:, (c % 4) * 128:(c % 4) * 128 + 128],
                                   tm_tiles[s][:, c * 128:(c + 1) * 128], ident[:])
            return last
        ST(h, "tensor", tbody)

        def vbody(v, s=s):
            if fm_stride == 128 and nchunk == 8:
                col0 = (sub0 + s) * 128
                v.tensor_copy(out=fm[:, col0:col0 + 512], in_=pA[:, 0:512])
                return v.tensor_copy(out=fm[:, col0 + 512:col0 + 1024], in_=pB[:, 0:512])
            last = None
            for c in range(nchunk):
                bank = pA if c < 4 else pB
                col = c * fm_stride + (sub0 + s) * 128
                last = v.tensor_copy(out=fm[:, col:col + 128],
                                     in_=bank[:, (c % 4) * 128:(c % 4) * 128 + 128])
            return last
        ST(h, "vector", vbody)


def emit_ln(h, xres, out_ap, trash, nfeat=1024, eps=1e-5, stats=None, dep=None):
    s_ = stats
    dd = None
    if dep is not None:
        dd = [dep] + ([h._ln_last] if getattr(h, "_ln_last", None) else [])
    t1 = ST(h, "vector", lambda v: v.tensor_scalar(out=trash, in0=xres, scalar1=-1.0 / nfeat,
                                                   scalar2=None, op0=ALU.mult, op1=ALU.add,
                                                   accum_out=s_[:, 1:2]), deps=dd)
    t2 = ST(h, "scalar", lambda sc: sc.activation(out=trash, in_=xres, func=AF.Square,
                                                  bias=s_[:, 1:2], scale=1.0, accum_out=s_[:, 2:3]),
            deps=[t1] if dep is not None else None)
    t3 = ST(h, "vector", lambda v: v.tensor_scalar(out=s_[:, 3:4], in0=s_[:, 2:3],
                                                   scalar1=1.0 / nfeat, scalar2=float(eps),
                                                   op0=ALU.mult, op1=ALU.add),
            deps=[t2] if dep is not None else None)
    t4 = ST(h, "scalar", lambda sc: sc.activation(out=s_[:, 4:5], in_=s_[:, 3:4], func=AF.Sqrt),
            deps=[t3] if dep is not None else None)
    t5 = ST(h, "vector", lambda v: v.reciprocal(out=s_[:, 5:6], in_=s_[:, 4:5]),
            deps=[t4] if dep is not None else None)
    t6 = ST(h, "vector", lambda v: v.tensor_scalar(out=out_ap, in0=xres, scalar1=s_[:, 1:2],
                                                   scalar2=s_[:, 5:6], op0=ALU.add, op1=ALU.mult),
            deps=[t5] if dep is not None else None)
    if dep is not None:
        h._ln_last = t6
    return t6



def emit_p0b(h, io, debug=False):
    woT = h.sb("b_woT", [128, 8 * 1024])
    bonusT = h.sb("b_bonusT", [128, 4])
    ident = h.sb("b_ident", [128, 128])
    mask64 = h.sb("b_mask64", [64, 64])
    ones1 = h.sb("b_ones1", [1, 128])
    ones128 = h.sb("b_ones128", [128, 1])
    lnrow = h.sb("b_lnrow", [1, 512])
    lnw_bc = h.sb("b_lnwbc", [128, 256])
    lnb_bc = h.sb("b_lnbbc", [128, 256])

    rfm = h.sb("b_rfm", [128, 4 * TT])
    kfm = h.sb("b_kfm", [128, 4 * TT])
    wfm = h.sb("b_wfm", [128, 4 * TT])
    x0tm = [h.sb(f"b_x0tm{s}", [128, D]) for s in range(4)]
    vch = h.sb("b_vch", [64, D])
    gch = h.sb("b_gch", [64, D])
    Sst = [h.sb(f"b_S{hh}", [128, DV]) for hh in range(NH)]
    otm = h.sb("b_otm", [128, 4 * D])
    ofm = h.sb("b_ofm", [128, 8 * TT])
    ewb4_ = [h.sb("b_ewb4a", [128, 4 * CH]), h.sb("b_ewb4b", [128, 4 * CH])]
    cumb4_ = [h.sb("b_cumb4a", [128, 4 * CH]), h.sb("b_cumb4b", [128, 4 * CH])]
    aexp4_ = [h.sb("b_aexp4a", [128, 4 * CH]), h.sb("b_aexp4b", [128, 4 * CH])]
    bexp4_ = [h.sb("b_bexp4a", [128, 4 * CH]), h.sb("b_bexp4b", [128, 4 * CH])]
    pc4_ = [h.sb("b_pc4a", [128, 4]), h.sb("b_pc4b", [128, 4])]
    rt4_ = [h.sb("b_rt4a", [128, 4 * CH]), h.sb("b_rt4b", [128, 4 * CH])]
    kt4_ = [h.sb("b_kt4a", [128, 4 * CH]), h.sb("b_kt4b", [128, 4 * CH])]
    rbk4_ = [h.sb("b_rbk4a", [128, 4 * CH]), h.sb("b_rbk4b", [128, 4 * CH])]
    vch2 = h.sb("b_vch2", [64, D])
    gch2 = h.sb("b_gch2", [64, D])
    msb4 = h.sb("b_msb4", [64, 4 * CH])
    ktT4 = h.sb("b_ktT4", [64, 4 * 128])
    odb4 = h.sb("b_odb4", [64, 4 * DV])
    osb4 = h.sb("b_osb4", [64, 4 * DV])
    onb4 = h.sb("b_onb4", [64, 4 * DV])
    sig4 = h.sb("b_sig4", [64, 4 * DV])
    tr644 = h.sb("b_tr644", [64, 4 * DV])
    tr128 = h.sb("b_tr128", [128, D])
    xres = h.sb("b_xres", [128, D])
    stats = h.sb("b_stats", [128, 8])
    dbgb = h.sb("b_dbgb", [128, 1024]) if debug else None
    st644 = h.sb("b_st644", [64, 32])
    stmp4 = h.sb("b_stmp4", [128, 4 * DV])

    pPool = h.ps("b_pPool", [128, 4096])
    pT = pPool[:, 0:512]
    pP = pPool[:, 512:1024]
    pW0 = pPool[:, 1024:1536]
    pW1 = pPool[:, 1536:2048]

    def loads(sync):
        for c in range(8):
            sync.dma_start(out=woT[:, c * 1024:(c + 1) * 1024], in_=io["woT"][c * 128:(c + 1) * 128, :]).then_inc(h.ld.h, 16)
            h.ld.bump(16)
        for nm, dst in [("bonusT", bonusT), ("ident", ident), ("mask64", mask64)]:
            sync.dma_start(out=dst[:], in_=io[nm][:]).then_inc(h.ld.h, 16)
            h.ld.bump(16)
        sync.dma_start(out=lnrow[:, 0:256], in_=io["ln_w"][:]).then_inc(h.ld.h, 16)
        sync.dma_start(out=lnrow[:, 256:512], in_=io["ln_b"][:]).then_inc(h.ld.h, 16)
        h.ld.bump(32)
        return None
    ST(h, "sync", loads)

    def init(v):
        v.memset(ones1[:], 1.0)
        v.memset(ones128[:], 1.0)
        return v.engine_nop()
    ST(h, "vector", init)

    def bc(t):
        t.matmul(pT[:, 0:256], ones1[:], lnrow[:, 0:256], start=True, stop=True)
        return t.matmul(pT[:, 256:512], ones1[:], lnrow[:, 256:512], start=True, stop=True)
    ST(h, "tensor", bc)

    def bc2(v):
        v.tensor_copy(out=lnw_bc[:], in_=pT[:, 0:256])
        return v.tensor_copy(out=lnb_bc[:], in_=pT[:, 256:512])
    ST(h, "vector", bc2)

    for seq in range(2):
        def zs(v):
            last = None
            for hh in range(NH):
                last = v.memset(Sst[hh][:], 0.0)
            return last
        ST(h, "vector", zs)

        for it in range(T // TT):
            t0 = seq * T + it * TT

            def ldt(sync, t0=t0):
                for hh in range(4):
                    sync.dma_start(out=rfm[:, hh * TT:(hh + 1) * TT], in_=io["rsc"][hh * 128:(hh + 1) * 128, t0:t0 + TT]).then_inc(h.ld.h, 16)
                    sync.dma_start(out=kfm[:, hh * TT:(hh + 1) * TT], in_=io["ksc"][hh * 128:(hh + 1) * 128, t0:t0 + TT]).then_inc(h.ld.h, 16)
                    sync.dma_start(out=wfm[:, hh * TT:(hh + 1) * TT], in_=io["wsc"][hh * 128:(hh + 1) * 128, t0:t0 + TT]).then_inc(h.ld.h, 16)
                    h.ld.bump(48)
                for s in range(4):
                    sync.dma_start(out=x0tm[s][:], in_=io["x0"][t0 + s * 128:t0 + (s + 1) * 128, :]).then_inc(h.ld.h, 16)
                    h.ld.bump(16)
                return None
            ST(h, "sync", ldt)

            ctok0 = h.tok()
            t_fr, t_bk, t_ldc = {}, {}, {}

            def chunk_front(ch, t0=t0):
                p = ch % 2
                tl = ch * CH
                ewb4, cumb4, aexp4, bexp4 = ewb4_[p], cumb4_[p], aexp4_[p], bexp4_[p]
                pc4, rt4, kt4, rbk4 = pc4_[p], rt4_[p], kt4_[p], rbk4_[p]
                vchp = vch if p == 0 else vch2
                gchp = gch if p == 0 else gch2
                rsl = [rfm[:, hh * TT + tl:hh * TT + tl + CH] for hh in range(NH)]
                ksl = [kfm[:, hh * TT + tl:hh * TT + tl + CH] for hh in range(NH)]
                wsl = [wfm[:, hh * TT + tl:hh * TT + tl + CH] for hh in range(NH)]
                guard = t_bk.get(ch - 2, ctok0)

                def ldc(sync, t0=t0, tl=tl, vchp=vchp, gchp=gchp):
                    sync.dma_start(out=vchp[:], in_=io["vsc"][t0 + tl:t0 + tl + CH, :]).then_inc(h.ld.h, 16)
                    sync.dma_start(out=gchp[:], in_=io["gsc"][t0 + tl:t0 + tl + CH, :]).then_inc(h.ld.h, 16)
                    h.ld.bump(32)
                    return None
                ST(h, "sync", ldc, deps=[guard] + ([t_ldc[ch - 1]] if ch >= 1 else []))
                tld = h.tok()
                t_ldc[ch] = tld

                def s1(sc):
                    last = None
                    for hh in range(NH):
                        last = sc.activation(out=ewb4[:, hh * CH:(hh + 1) * CH], in_=wsl[hh], func=AF.Exp)
                    return last
                ta = ST(h, "scalar", s1, deps=[guard, tld])

                def s2(v):
                    last = None
                    for hh in range(NH):
                        last = v.tensor_tensor_scan(out=cumb4[:, hh * CH:(hh + 1) * CH],
                                                    data0=ewb4[:, hh * CH:(hh + 1) * CH],
                                                    data1=ewb4[:, hh * CH:(hh + 1) * CH],
                                                    initial=0.0, op0=ALU.add, op1=ALU.bypass)
                    return last
                tb = ST(h, "vector", s2, deps=[ta])
                tc = ST(h, "vector", lambda v: v.tensor_tensor(out=aexp4[:], in0=ewb4[:], in1=cumb4[:],
                                                               op=ALU.subtract), deps=[tb])

                def s4(sc):
                    sc.activation(out=aexp4[:], in_=aexp4[:], func=AF.Exp)
                    sc.activation(out=bexp4[:], in_=cumb4[:], func=AF.Exp)
                    last = None
                    for hh in range(NH):
                        last = sc.activation(out=pc4[:, hh:hh + 1],
                                             in_=cumb4[:, hh * CH + CH - 1:hh * CH + CH], func=AF.Exp,
                                             scale=-1.0)
                    return last
                td = ST(h, "scalar", s4, deps=[tc])

                def s5(v):
                    last = None
                    for hh in range(NH):
                        v.tensor_tensor(out=rt4[:, hh * CH:(hh + 1) * CH], in0=rsl[hh],
                                        in1=aexp4[:, hh * CH:(hh + 1) * CH], op=ALU.mult)
                        v.tensor_tensor(out=kt4[:, hh * CH:(hh + 1) * CH], in0=ksl[hh],
                                        in1=bexp4[:, hh * CH:(hh + 1) * CH], op=ALU.mult)
                        last = v.scalar_tensor_tensor(out=rbk4[:, hh * CH:(hh + 1) * CH], in0=rsl[hh],
                                                      scalar=bonusT[:, hh:hh + 1], in1=ksl[hh],
                                                      op0=ALU.mult, op1=ALU.mult)
                    return last
                t_fr[ch] = ST(h, "vector", s5, deps=[td])

            def chunk_back(ch):
                p = ch % 2
                base = (ch % 2) * 64
                sub = ch // 2
                pc4, rt4, kt4, rbk4 = pc4_[p], rt4_[p], kt4_[p], rbk4_[p]
                vchp = vch if p == 0 else vch2
                gchp = gch if p == 0 else gch2
                vsl = [vchp[:, hh * DV:(hh + 1) * DV] for hh in range(NH)]
                pSm_ = [pPool[0:64, hh * 512:(hh + 1) * 512] for hh in range(NH)]
                pO_ = [pPool[(hh % 2) * 64:(hh % 2) * 64 + 64,
                             2048 + (hh // 2) * 256:2048 + (hh // 2) * 256 + 256] for hh in range(NH)]
                pKV_ = [pPool[:, 2560 + hh * 256:2560 + (hh + 1) * 256] for hh in range(NH)]
                prev = t_bk.get(ch - 1, ctok0)

                def s6(t):
                    last = None
                    for hh in range(NH):
                        t.matmul(pSm_[hh][:, 0:64], kt4[:, hh * CH:(hh + 1) * CH],
                                 rt4[:, hh * CH:(hh + 1) * CH], start=True, stop=True)
                        t.matmul(pSm_[hh][:, 64:65], rbk4[:, hh * CH:(hh + 1) * CH],
                                 ones128[:], start=True, stop=True)
                        last = t.transpose(pSm_[hh][:, 128:256], kt4[:, hh * CH:(hh + 1) * CH], ident[:])
                    return last
                t6 = ST(h, "tensor", s6, deps=[t_fr[ch], prev])

                def s7(v):
                    last = None
                    for hh in range(NH):
                        v.tensor_tensor(out=msb4[:, hh * CH:(hh + 1) * CH], in0=pSm_[hh][0:64, 0:64],
                                        in1=mask64[:], op=ALU.mult)
                        v.tensor_copy(out=ktT4[:, hh * 128:(hh + 1) * 128], in_=pSm_[hh][0:64, 128:256])
                        last = v.tensor_scalar(out=odb4[:, hh * DV:(hh + 1) * DV], in0=vsl[hh],
                                               scalar1=pSm_[hh][0:64, 64:65], scalar2=None, op0=ALU.mult)
                    return last
                t7 = ST(h, "vector", s7, deps=[t6])

                def s8(t):
                    last = None
                    for hh in range(NH):
                        t.matmul(pO_[hh], rt4[:, hh * CH:(hh + 1) * CH], Sst[hh][:], start=True, stop=False)
                        t.matmul(pO_[hh], msb4[:, hh * CH:(hh + 1) * CH], vsl[hh], start=False, stop=True)
                        last = t.matmul(pKV_[hh], ktT4[:, hh * 128:(hh + 1) * 128], vsl[hh],
                                        start=True, stop=True)
                    return last
                t8 = ST(h, "tensor", s8, deps=[t7])

                def s9(v):
                    last = None
                    for hh in range(NH):
                        last = v.tensor_tensor(out=osb4[:, hh * DV:(hh + 1) * DV], in0=pO_[hh],
                                               in1=odb4[:, hh * DV:(hh + 1) * DV], op=ALU.add)
                    return last
                t9 = ST(h, "vector", s9, deps=[t8])

                def s10(v):
                    last = None
                    for hh in range(NH):
                        last = v.tensor_scalar(out=tr644[:, hh * DV:(hh + 1) * DV],
                                               in0=osb4[:, hh * DV:(hh + 1) * DV],
                                               scalar1=-1.0 / DV, scalar2=None, op0=ALU.mult, op1=ALU.add,
                                               accum_out=st644[:, hh * 8 + 1:hh * 8 + 2])
                    return last
                t10 = ST(h, "vector", s10, deps=[t9])

                def s11(sc):
                    last = None
                    for hh in range(NH):
                        last = sc.activation(out=tr644[:, hh * DV:(hh + 1) * DV],
                                             in_=osb4[:, hh * DV:(hh + 1) * DV], func=AF.Square,
                                             bias=st644[:, hh * 8 + 1:hh * 8 + 2], scale=1.0,
                                             accum_out=st644[:, hh * 8 + 2:hh * 8 + 3])
                    return last
                t11 = ST(h, "scalar", s11, deps=[t10])

                def s12(v):
                    last = None
                    for hh in range(NH):
                        last = v.tensor_scalar(out=st644[:, hh * 8 + 3:hh * 8 + 4],
                                               in0=st644[:, hh * 8 + 2:hh * 8 + 3],
                                               scalar1=1.0 / DV, scalar2=1e-5, op0=ALU.mult, op1=ALU.add)
                    return last
                t12 = ST(h, "vector", s12, deps=[t11])

                def s13(sc, gchp=gchp):
                    sc.activation(out=sig4[:], in_=gchp[0:64, :], func=AF.Exp, scale=-1.0)
                    last = None
                    for hh in range(NH):
                        last = sc.activation(out=st644[:, hh * 8 + 4:hh * 8 + 5],
                                             in_=st644[:, hh * 8 + 3:hh * 8 + 4], func=AF.Sqrt)
                    return last
                t13 = ST(h, "scalar", s13, deps=[t12])

                def s14(v):
                    v.tensor_scalar(out=sig4[:], in0=sig4[:], scalar1=1.0, scalar2=None, op0=ALU.add)
                    last = None
                    for hh in range(NH):
                        last = v.reciprocal(out=st644[:, hh * 8 + 5:hh * 8 + 6],
                                            in_=st644[:, hh * 8 + 4:hh * 8 + 5])
                    return last
                t14 = ST(h, "vector", s14, deps=[t13])

                def s15(v):
                    v.reciprocal(out=sig4[:], in_=sig4[:])
                    last = None
                    for hh in range(NH):
                        last = v.tensor_scalar(out=onb4[:, hh * DV:(hh + 1) * DV],
                                               in0=osb4[:, hh * DV:(hh + 1) * DV],
                                               scalar1=st644[:, hh * 8 + 1:hh * 8 + 2],
                                               scalar2=st644[:, hh * 8 + 5:hh * 8 + 6],
                                               op0=ALU.add, op1=ALU.mult)
                    return last
                t15 = ST(h, "vector", s15, deps=[t14])

                def s16(v, gchp=gchp):
                    lnw_b4 = lnw_bc[0:64, :].unsqueeze(1).broadcast_to([64, 4, 256])
                    onv = onb4[:].rearrange("p (h d) -> p h d", d=256)
                    v.tensor_tensor(out=onv, in0=onv, in1=lnw_b4, op=ALU.mult)
                    return v.tensor_tensor(out=sig4[:], in0=sig4[:], in1=gchp[0:64, :], op=ALU.mult)
                t16 = ST(h, "vector", s16, deps=[t15])

                t17 = ST(h, "vector", lambda v: v.tensor_tensor(
                    out=onb4[:].rearrange("p (h d) -> p h d", d=256),
                    in0=onb4[:].rearrange("p (h d) -> p h d", d=256),
                    in1=lnb_bc[0:64, :].unsqueeze(1).broadcast_to([64, 4, 256]), op=ALU.add), deps=[t16])

                t18 = ST(h, "vector", lambda v, base=base, sub=sub: v.tensor_tensor(
                    out=otm[base:base + 64, sub * D:(sub + 1) * D],
                    in0=onb4[:], in1=sig4[:], op=ALU.mult), deps=[t17])

                def s19(v):
                    last = None
                    for hh in range(NH):
                        last = v.tensor_tensor(out=stmp4[:, hh * DV:(hh + 1) * DV], in0=Sst[hh][:],
                                               in1=pKV_[hh], op=ALU.add)
                    return last
                t19 = ST(h, "vector", s19, deps=[t8])

                def s20(v, pc4=pc4):
                    last = None
                    for hh in range(NH):
                        last = v.tensor_scalar(out=Sst[hh][:], in0=stmp4[:, hh * DV:(hh + 1) * DV],
                                               scalar1=pc4[:, hh:hh + 1], scalar2=None, op0=ALU.mult)
                    return last
                t20 = ST(h, "vector", s20, deps=[t19])
                t_bk[ch] = t20

            for step in range(TT // CH + 1):
                if step < TT // CH:
                    chunk_front(step)
                if step >= 1:
                    chunk_back(step - 1)

            if debug and seq == 0 and it == 0:
                def dbg2(sync):
                    for s in range(4):
                        sync.dma_start(out=io["d_otm"][s * 128:(s + 1) * 128, :],
                                       in_=otm[:, s * D:(s + 1) * D]).then_inc(h.st.h, 16)
                        h.st.bump(16)
                    return None
                ST(h, "sync", dbg2)
            tm_to_fm(h, [otm[:, s * D:(s + 1) * D] for s in range(4)], ofm, pT, pP, ident, fm_stride=TT)
            for s in range(4):
                def wmm(t, s=s):
                    last = None
                    for half, bank in [(0, pW0), (1, pW1)]:
                        for c in range(8):
                            last = t.matmul(bank[:], ofm[:, c * TT + s * 128:c * TT + (s + 1) * 128],
                                            woT[:, c * 1024 + half * 512:c * 1024 + half * 512 + 512],
                                            start=(c == 0), stop=(c == 7))
                    return last
                ST(h, "tensor", wmm)

                def resid(v, s=s):
                    v.tensor_tensor(out=xres[:, 0:512], in0=pW0[:], in1=x0tm[s][:, 0:512], op=ALU.add)
                    return v.tensor_tensor(out=xres[:, 512:1024], in0=pW1[:], in1=x0tm[s][:, 512:1024], op=ALU.add)
                ST(h, "vector", resid)

                emit_ln(h, xres[:], xres[:], tr128[:], nfeat=D, eps=1e-5, stats=stats)

                def sto(sync, t0=t0, s=s):
                    sync.dma_start(out=io["s1"][t0 + s * 128:t0 + (s + 1) * 128, :], in_=xres[:]).then_inc(h.st.h, 16)
                    h.st.bump(16)
                    sync.wait_ge(h.st.h, h.st.n)
                    return sync.nop()
                ST(h, "sync", sto)

    if debug:
        def dbgout(sync):
            sync.dma_start(out=io["d_buf"][:], in_=dbgb[:]).then_inc(h.st.h, 16)
            h.st.bump(16)
            return None
        ST(h, "sync", dbgout)
    ST(h, "sync", lambda sync: sync.wait_ge(h.st.h, h.st.n) and None)



def emit_p0a(h, io, variant):
    ns = [0, 1, 2] if variant == "A" else [3, 4]

    wx1T = h.sb("wx1T", [128, 8 * 160])
    w2nT = h.sb("w2nT", [32, 5 * 1024])
    mu_x = h.sb("mu_x", [128, 8])
    x_bias = h.sb("x_bias", [128, 8 * 5])
    ident = h.sb("identP0", [128, 128])
    if variant == "A":
        wrT = h.sb("wrT", [128, 8 * 512])
        wkT = h.sb("wkT", [128, 8 * 512])
        ww1T = h.sb("ww1T", [128, 8 * 64])
        ww2T = h.sb("ww2T", [64, 512])
        bw2 = h.sb("bw2", [128, 4])
    else:
        wvT = h.sb("wvT", [128, 8 * 1024])
        wgT = h.sb("wgT", [128, 8 * 1024])

    xtm = [h.sb(f"xtm{s}", [128, D]) for s in range(4)]
    xfm = h.sb("xfm", [128, 8 * TT])
    delta = h.sb("delta", [128, 8 * TT])
    stash = h.sb("stash", [128, 8])
    xmod = h.sb("xmod", [128, 8 * TT])
    musb = h.sb("musb", [128, TT])
    musbB = h.sb("musbB", [128, TT])
    xp = h.sb("xp", [32, 5 * TT])
    ubuf = h.sb("ubuf", [64, TT])
    ofm = h.sb("ofm", [128, 4 * 1024])

    pA = h.ps("pA", [128, 512])
    pB = h.ps("pB", [128, 512])
    pX = h.ps("pX", [32, TT])
    pM = h.ps("pM", [128, TT])
    pM2 = h.ps("pM2", [128, TT])
    pU = h.ps("pU", [64, TT])

    def loads(sync):
        last = None
        for c in range(8):
            last = sync.dma_start(out=wx1T[:, c * 160:(c + 1) * 160], in_=io["wx1T"][c * 128:(c + 1) * 128, :])
            last.then_inc(h.ld.h, 16); h.ld.bump(16)
        for n_ in range(5):
            last = sync.dma_start(out=w2nT[:, n_ * 1024:(n_ + 1) * 1024], in_=io["w2nT"][n_, :, :])
            last.then_inc(h.ld.h, 16); h.ld.bump(16)
        for c in range(8):
            last = sync.dma_start(out=mu_x[:, c:c + 1], in_=io["mu_x"][c * 128:(c + 1) * 128, :])
            last.then_inc(h.ld.h, 16); h.ld.bump(16)
            last = sync.dma_start(out=x_bias[:, c * 5:(c + 1) * 5], in_=io["x_biasT"][c * 128:(c + 1) * 128, :])
            last.then_inc(h.ld.h, 16); h.ld.bump(16)
        last = sync.dma_start(out=ident[:], in_=io["ident"][:])
        last.then_inc(h.ld.h, 16); h.ld.bump(16)
        if variant == "A":
            for c in range(8):
                for dst, src, wdt in [(wrT, io["wrT"], 512), (wkT, io["wkT"], 512), (ww1T, io["ww1T"], 64)]:
                    last = sync.dma_start(out=dst[:, c * wdt:(c + 1) * wdt], in_=src[c * 128:(c + 1) * 128, :])
                    last.then_inc(h.ld.h, 16); h.ld.bump(16)
            last = sync.dma_start(out=ww2T[:], in_=io["ww2T"][:])
            last.then_inc(h.ld.h, 16); h.ld.bump(16)
            for o in range(4):
                last = sync.dma_start(out=bw2[:, o:o + 1], in_=io["bw2"][o * 128:(o + 1) * 128, :])
                last.then_inc(h.ld.h, 16); h.ld.bump(16)
        else:
            for c in range(8):
                for dst, src in [(wvT, io["wvT"]), (wgT, io["wgT"])]:
                    last = sync.dma_start(out=dst[:, c * 1024:(c + 1) * 1024], in_=src[c * 128:(c + 1) * 128, :])
                    last.then_inc(h.ld.h, 16); h.ld.bump(16)
        return None
    ST(h, "sync", loads)
    ST(h, "vector", lambda v: v.memset(stash[:], 0.0))

    for i in range(NTILE):
        t0 = i * TT

        def ldx(sync, t0=t0):
            for s in range(4):
                sync.dma_start(out=xtm[s][:], in_=io["x0"][t0 + s * 128:t0 + (s + 1) * 128, :]).then_inc(h.ld.h, 16)
                h.ld.bump(16)
            return None
        ST(h, "sync", ldx)

        tm_to_fm(h, xtm, xfm, pA, pB, ident)

        def dstage1(v, i=i):
            last = None
            for c in range(8):
                xc = xfm[:, c * TT:(c + 1) * TT]
                dc = delta[:, c * TT:(c + 1) * TT]
                v.tensor_tensor(out=dc[:, 1:TT], in0=xc[:, 0:TT - 1], in1=xc[:, 1:TT], op=ALU.subtract)
                if i % (T // TT) == 0:
                    last = v.tensor_scalar(out=dc[:, 0:1], in0=xc[:, 0:1], scalar1=-1.0, scalar2=None, op0=ALU.mult)
                else:
                    last = v.tensor_tensor(out=dc[:, 0:1], in0=stash[:, c:c + 1], in1=xc[:, 0:1], op=ALU.subtract)
            return last
        ST(h, "vector", dstage1)

        def dstage2(v):
            last = None
            for c in range(8):
                last = v.tensor_copy(out=stash[:, c:c + 1], in_=xfm[:, c * TT + TT - 1:c * TT + TT])
            return last
        ST(h, "vector", dstage2)

        def dstage3(v):
            last = None
            for c in range(8):
                last = v.scalar_tensor_tensor(out=xmod[:, c * TT:(c + 1) * TT], in0=delta[:, c * TT:(c + 1) * TT],
                                              scalar=mu_x[:, c:c + 1], in1=xfm[:, c * TT:(c + 1) * TT],
                                              op0=ALU.mult, op1=ALU.add)
            return last
        ST(h, "vector", dstage3)

        base0 = h.tok()
        t_xp, t_tanh = {}, {}
        prev_n = None
        ofm_guard = base0
        for idx_, n_ in enumerate(ns):
            def xpmm(t, n_=n_):
                last = None
                for c in range(8):
                    last = t.matmul(pX[:], wx1T[:, c * 160 + n_ * 32:c * 160 + (n_ + 1) * 32],
                                    xmod[:, c * TT:(c + 1) * TT], start=(c == 0), stop=(c == 7))
                return last
            dd = [base0] + ([t_tanh[ns[idx_ - 1]]] if idx_ > 0 else [])
            t_xp[n_] = ST(h, "tensor", xpmm, deps=dd)
            t_tanh[n_] = ST(h, "scalar", lambda s, n_=n_: s.activation(
                out=xp[:, n_ * TT:(n_ + 1) * TT], in_=pX[:], func=AF.Tanh), deps=[t_xp[n_]])
        xp_done = t_xp[ns[-1]]

        for n_ in ns:
            mb = [pM, pM2]
            mub = [musb, musbB]
            tmm, tst, txm = {}, {}, {}
            xmod_guard = prev_n if prev_n is not None else xp_done

            def em_musmm(c, n_=n_):
                deps = [t_tanh[n_]] + ([tst[c - 2]] if c >= 2 else [xmod_guard])
                tmm[c] = ST(h, "tensor", lambda t, n_=n_, c=c: t.matmul(
                    mb[c % 2][:], w2nT[:, n_ * 1024 + c * 128:n_ * 1024 + (c + 1) * 128],
                    xp[:, n_ * TT:(n_ + 1) * TT], start=True, stop=True), deps=deps)

            def em_stt(c, n_=n_):
                deps = [tmm[c]] + ([txm[c - 2]] if c >= 2 else [])
                tst[c] = ST(h, "vector", lambda v, n_=n_, c=c: v.scalar_tensor_tensor(
                    out=mub[c % 2][:], in0=mb[c % 2][:], scalar=x_bias[:, c * 5 + n_:c * 5 + n_ + 1],
                    in1=delta[:, c * TT:(c + 1) * TT], op0=ALU.add, op1=ALU.mult), deps=deps)
                deps2 = [tst[c], xmod_guard]
                txm[c] = ST(h, "vector", lambda v, c=c: v.tensor_tensor(
                    out=xmod[:, c * TT:(c + 1) * TT], in0=mub[c % 2][:],
                    in1=xfm[:, c * TT:(c + 1) * TT], op=ALU.add), deps=deps2)

            for c in range(9):
                if c < 8:
                    em_musmm(c)
                if c >= 1:
                    em_stt(c - 1)
            xm_all = txm[7]

            if variant == "A" and n_ in (0, 2):
                wT = wrT if n_ == 0 else wkT
                dst = io["rsc"] if n_ == 0 else io["ksc"]
                pb = [pA, pB]
                tpm, tpe = {}, {}

                def em_pmm(o, wT=wT):
                    def pmm(t, o=o, wT=wT):
                        last = None
                        for c in range(8):
                            last = t.matmul(pb[o % 2][:], wT[:, c * 512 + o * 128:c * 512 + (o + 1) * 128],
                                            xmod[:, c * TT:(c + 1) * TT], start=(c == 0), stop=(c == 7))
                        return last
                    deps = [xm_all] + ([tpe[o - 2]] if o >= 2 else [])
                    tpm[o] = ST(h, "tensor", pmm, deps=deps)

                def em_pev(o):
                    tpe[o] = ST(h, "vector", lambda v, o=o: v.tensor_copy(
                        out=ofm[:, o * 1024:o * 1024 + TT], in_=pb[o % 2][:]),
                        deps=[tpm[o], ofm_guard])

                for o in range(5):
                    if o < 4:
                        em_pmm(o)
                    if o >= 1:
                        em_pev(o - 1)
                prev_n = tpe[3]

                def stores(sync, dst=dst, t0=t0):
                    for o in range(4):
                        sync.dma_start(out=dst[o * 128:(o + 1) * 128, t0:t0 + TT],
                                       in_=ofm[:, o * 1024:o * 1024 + TT]).then_inc(h.st.h, 16)
                        h.st.bump(16)
                    sync.wait_ge(h.st.h, h.st.n)
                    return sync.nop()
                ofm_guard = ST(h, "sync", stores, deps=[tpe[3]])
            elif variant == "A":
                def wmm(t):
                    last = None
                    for c in range(8):
                        last = t.matmul(pU[:], ww1T[:, c * 64:(c + 1) * 64],
                                        xmod[:, c * TT:(c + 1) * TT], start=(c == 0), stop=(c == 7))
                    return last
                twm = ST(h, "tensor", wmm, deps=[xm_all])
                ttn = ST(h, "scalar", lambda s: s.activation(out=ubuf[:], in_=pU[:], func=AF.Tanh),
                         deps=[twm])
                tpe3 = None
                bank_guard = prev_n if prev_n is not None else base0
                for o in range(4):
                    tw2 = ST(h, "tensor", lambda t, o=o: t.matmul(
                        pA[:] if o % 2 == 0 else pB[:], ww2T[:, o * 128:(o + 1) * 128],
                        ubuf[:], start=True, stop=True),
                        deps=[ttn, bank_guard] + ([tpe3] if tpe3 is not None and o >= 2 else []))
                    tpe3 = ST(h, "vector", lambda v, o=o: v.tensor_scalar(
                        out=ofm[:, o * 1024:o * 1024 + TT], in0=pA[:] if o % 2 == 0 else pB[:],
                        scalar1=bw2[:, o:o + 1], scalar2=None, op0=ALU.add), deps=[tw2, ofm_guard])
                prev_n = tpe3

                def storew(sync, t0=t0):
                    for o in range(4):
                        sync.dma_start(out=io["wsc"][o * 128:(o + 1) * 128, t0:t0 + TT],
                                       in_=ofm[:, o * 1024:o * 1024 + TT]).then_inc(h.st.h, 16)
                        h.st.bump(16)
                    sync.wait_ge(h.st.h, h.st.n)
                    return sync.nop()
                ofm_guard = ST(h, "sync", storew, deps=[tpe3])
            else:
                wT = wvT if n_ == 3 else wgT
                dst = io["vsc"] if n_ == 3 else io["gsc"]
                pairs = [(pA, pB), (pM, pM2)]
                tvm, tve = {}, {}

                def em_vmm(s, wT=wT):
                    def vmm(t, s=s, wT=wT):
                        last = None
                        for half in range(2):
                            bank = pairs[s % 2][half]
                            for c in range(8):
                                last = t.matmul(bank[:],
                                                xmod[:, c * TT + s * 128:c * TT + (s + 1) * 128],
                                                wT[:, c * 1024 + half * 512:c * 1024 + half * 512 + 512],
                                                start=(c == 0), stop=(c == 7))
                        return last
                    deps = [xm_all] + ([tve[s - 2]] if s >= 2 else [])
                    if prev_n is not None and s < 2:
                        deps.append(prev_n)
                    tvm[s] = ST(h, "tensor", vmm, deps=deps)

                def em_vev(s):
                    def vevac(v, s=s):
                        v.tensor_copy(out=ofm[:, s * 1024:s * 1024 + 512], in_=pairs[s % 2][0][:])
                        return v.tensor_copy(out=ofm[:, s * 1024 + 512:(s + 1) * 1024], in_=pairs[s % 2][1][:])
                    tve[s] = ST(h, "vector", vevac, deps=[tvm[s], ofm_guard])

                for s in range(5):
                    if s < 4:
                        em_vmm(s)
                    if s >= 1:
                        em_vev(s - 1)
                prev_n = tve[3]

                def storev(sync, dst=dst, t0=t0):
                    for s in range(4):
                        sync.dma_start(out=dst[t0 + s * 128:t0 + (s + 1) * 128, :],
                                       in_=ofm[:, s * 1024:(s + 1) * 1024]).then_inc(h.st.h, 16)
                        h.st.bump(16)
                    sync.wait_ge(h.st.h, h.st.n)
                    return sync.nop()
                ofm_guard = ST(h, "sync", storev, deps=[tve[3]])

    ST(h, "sync", lambda sync: sync.wait_ge(h.st.h, h.st.n) and None)



def emit_rms(h, xin, out_ap, trash, stats, nfeat=1024, eps=1e-6):
    s_ = stats
    ST(h, "scalar", lambda sc: sc.activation(out=trash, in_=xin, func=AF.Square, accum_out=s_[:, 2:3]))
    ST(h, "vector", lambda v: v.tensor_scalar(out=s_[:, 3:4], in0=s_[:, 2:3],
                                              scalar1=1.0 / nfeat, scalar2=float(eps), op0=ALU.mult, op1=ALU.add))
    ST(h, "scalar", lambda sc: sc.activation(out=s_[:, 4:5], in_=s_[:, 3:4], func=AF.Sqrt))
    ST(h, "vector", lambda v: v.reciprocal(out=s_[:, 5:6], in_=s_[:, 4:5]))
    if out_ap is not None:
        ST(h, "vector", lambda v: v.tensor_scalar(out=out_ap, in0=xin, scalar1=s_[:, 5:6],
                                                  scalar2=None, op0=ALU.mult))


def emit_p1(h, io):
    wqT = h.sb("q_wqT", [128, 8 * D])
    wkT = h.sb("q_wkT", [128, 8 * 256])
    wvT = h.sb("q_wvT", [128, 8 * 256])
    woT = h.sb("q_woT", [128, 8 * D])
    ident = h.sb("q_ident", [128, 128])
    trineg = h.sb("q_trineg", [128, 128])
    cosf = h.sb("q_cos", [128, 16 * 32])
    sinf = h.sb("q_sin", [128, 16 * 32])
    grow = h.sb("q_grow", [1, 32])
    gq = h.sb("q_gq", [128, H0])
    gk = h.sb("q_gk", [128, HKV])
    ones1 = h.sb("q_ones1", [1, 128])

    x1t = h.sb("q_x1t", [128, D])
    xn = h.sb("q_xn", [128, D])
    xfm = h.sb("q_xfm", [128, 8 * 128])
    qtm = h.sb("q_qtm", [128, D])
    qrp = h.sb("q_qrp", [128, D])
    Ab = h.sb("q_Ab", [128, D])
    Bb = h.sb("q_Bb", [128, D])
    qfm = h.sb("q_qfm", [128, 8 * 128])
    ktm = h.sb("q_ktm", [128, 256])
    krp = h.sb("q_krp", [128, 256])
    sqb = h.sb("q_sqb", [128, D])
    red = h.sb("q_red", [128, H0])
    rstdh = h.sb("q_rstdh", [128, H0])
    kfm = h.sb("q_kfm", [128, 2 * T])
    vseq = h.sb("q_vseq", [128, 16 * 256])
    scb2_ = [h.sb("q_scbA", [128, T]), h.sb("q_scbB", [128, T])]
    ptb2_ = [h.sb("q_ptbA", [128, 16 * 128]), h.sb("q_ptbB", [128, 16 * 128])]
    otm = h.sb("q_otm", [128, D])
    ofm = h.sb("q_ofm", [128, 8 * 128])
    stats = h.sb("q_stats", [128, 8])
    trash = h.sb("q_trash", [128, D])

    pBig = h.ps("q_pBig", [128, 2048])
    pAV = h.ps("q_pAV", [128, 128])
    pW0 = h.ps("q_pW0", [128, 512])
    pW1 = h.ps("q_pW1", [128, 512])

    def loads(sync):
        for c in range(8):
            sync.dma_start(out=wqT[:, c * D:(c + 1) * D], in_=io["wqT2"][c * 128:(c + 1) * 128, :]).then_inc(h.ld.h, 16)
            sync.dma_start(out=wkT[:, c * 256:(c + 1) * 256], in_=io["wkT2"][c * 128:(c + 1) * 128, :]).then_inc(h.ld.h, 16)
            sync.dma_start(out=wvT[:, c * 256:(c + 1) * 256], in_=io["wvT2"][c * 128:(c + 1) * 128, :]).then_inc(h.ld.h, 16)
            sync.dma_start(out=woT[:, c * D:(c + 1) * D], in_=io["woT2"][c * 128:(c + 1) * 128, :]).then_inc(h.ld.h, 16)
            h.ld.bump(64)
        for i in range(16):
            sync.dma_start(out=cosf[:, i * 32:(i + 1) * 32], in_=io["rope_cos"][i * 128:(i + 1) * 128, :]).then_inc(h.ld.h, 16)
            sync.dma_start(out=sinf[:, i * 32:(i + 1) * 32], in_=io["rope_sin"][i * 128:(i + 1) * 128, :]).then_inc(h.ld.h, 16)
            h.ld.bump(32)
        for nm, sl in [("ident", ident[:]), ("tri_neg", trineg[:])]:
            sync.dma_start(out=sl, in_=io[nm][:]).then_inc(h.ld.h, 16)
            h.ld.bump(16)
        sync.dma_start(out=grow[:, 0:16], in_=io["q_gain_row"][:]).then_inc(h.ld.h, 16)
        sync.dma_start(out=grow[:, 16:20], in_=io["k_gain_row"][:]).then_inc(h.ld.h, 16)
        h.ld.bump(32)
        return None
    ST(h, "sync", loads)
    ST(h, "vector", lambda v: v.memset(ones1[:], 1.0))

    def gbc(t):
        t.matmul(pW0[:, 0:16], ones1[:], grow[:, 0:16], start=True, stop=True)
        return t.matmul(pW0[:, 16:20], ones1[:], grow[:, 16:20], start=True, stop=True)
    ST(h, "tensor", gbc)

    def gbc2(v):
        v.tensor_scalar(out=gq[:], in0=pW0[:, 0:16], scalar1=0.125, scalar2=None, op0=ALU.mult)
        return v.tensor_copy(out=gk[:], in_=pW0[:, 16:20])
    ST(h, "vector", gbc2)

    def head_rms_rope(src, dst, nh, gain, ti):
        ST(h, "scalar", lambda sc: sc.activation(out=sqb[:, 0:nh * 64], in_=src, func=AF.Square))
        ST(h, "vector", lambda v: v.tensor_reduce(
            out=red[:, 0:nh], in_=sqb[:, 0:nh * 64].rearrange("p (h d) -> p h d", d=64),
            axis=AX.X, op=ALU.add))
        ST(h, "vector", lambda v: v.tensor_scalar(out=red[:, 0:nh], in0=red[:, 0:nh],
                                                  scalar1=1.0 / 64, scalar2=1e-6, op0=ALU.mult, op1=ALU.add))
        ST(h, "scalar", lambda sc: sc.activation(out=red[:, 0:nh], in_=red[:, 0:nh], func=AF.Sqrt))
        ST(h, "vector", lambda v: v.reciprocal(out=rstdh[:, 0:nh], in_=red[:, 0:nh]))

        def norm(v):
            last = None
            for hh in range(nh):
                last = v.tensor_scalar(out=src[:, hh * 64:(hh + 1) * 64], in0=src[:, hh * 64:(hh + 1) * 64],
                                       scalar1=rstdh[:, hh:hh + 1], scalar2=gain[:, hh:hh + 1],
                                       op0=ALU.mult, op1=ALU.mult)
            return last
        ST(h, "vector", norm)

        cb = cosf[:, ti * 32:(ti + 1) * 32].unsqueeze(1).broadcast_to([128, nh, 32])
        sb_ = sinf[:, ti * 32:(ti + 1) * 32].unsqueeze(1).broadcast_to([128, nh, 32])
        sv = src.rearrange("p (h d) -> p h d", d=64)
        Av = Ab[:, 0:nh * 64].rearrange("p (h d) -> p h d", d=64)
        Bv = Bb[:, 0:nh * 64].rearrange("p (h d) -> p h d", d=64)
        dv = dst.rearrange("p (h d) -> p h d", d=64)

        def ropeAB(v):
            v.tensor_tensor(out=Av[:, :, 0:32], in0=sv[:, :, 0:32], in1=cb, op=ALU.mult)
            v.tensor_tensor(out=Av[:, :, 32:64], in0=sv[:, :, 32:64], in1=cb, op=ALU.mult)
            v.tensor_tensor(out=Bv[:, :, 0:32], in0=sv[:, :, 0:32], in1=sb_, op=ALU.mult)
            return v.tensor_tensor(out=Bv[:, :, 32:64], in0=sv[:, :, 32:64], in1=sb_, op=ALU.mult)
        ST(h, "vector", ropeAB)

        def ropeC(v):
            v.tensor_tensor(out=dv[:, :, 0:32], in0=Av[:, :, 0:32], in1=Bv[:, :, 32:64], op=ALU.add)
            return v.tensor_tensor(out=dv[:, :, 32:64], in0=Av[:, :, 32:64], in1=Bv[:, :, 0:32], op=ALU.subtract)
        ST(h, "vector", ropeC)

    for seq in range(2):
        base_t = seq * T
        for ti in range(16):
            t0 = base_t + ti * 128

            def ldx(sync, t0=t0):
                sync.dma_start(out=x1t[:], in_=io["s1"][t0:t0 + 128, :]).then_inc(h.ld.h, 16)
                h.ld.bump(16)
                return None
            ST(h, "sync", ldx)

            emit_rms(h, x1t[:], xn[:], trash[:], stats)
            tm_to_fm(h, [xn], xfm, pBig[:, 0:512], pBig[:, 512:1024], ident, fm_stride=128, sub0=0)

            def kvmm(t):
                last = None
                for c in range(8):
                    t.matmul(pBig[:, 1024:1280], xfm[:, c * 128:(c + 1) * 128],
                             wkT[:, c * 256:(c + 1) * 256], start=(c == 0), stop=(c == 7))
                for c in range(8):
                    last = t.matmul(pBig[:, 1280:1536], xfm[:, c * 128:(c + 1) * 128],
                                    wvT[:, c * 256:(c + 1) * 256], start=(c == 0), stop=(c == 7))
                return last
            ST(h, "tensor", kvmm)

            def kvevac(v, ti=ti):
                v.tensor_copy(out=ktm[:], in_=pBig[:, 1024:1280])
                return v.tensor_copy(out=vseq[:, ti * 256:(ti + 1) * 256], in_=pBig[:, 1280:1536])
            ST(h, "vector", kvevac)

            head_rms_rope(ktm[:], krp[:], HKV, gk, ti)

            def ktr(t, ti=ti):
                t.transpose(pBig[:, 1024:1152], krp[:, 0:128], ident[:])
                return t.transpose(pBig[:, 1152:1280], krp[:, 128:256], ident[:])
            ST(h, "tensor", ktr)

            def kev(v, ti=ti):
                v.tensor_copy(out=kfm[:, ti * 128:(ti + 1) * 128], in_=pBig[:, 1024:1152])
                return v.tensor_copy(out=kfm[:, T + ti * 128:T + (ti + 1) * 128], in_=pBig[:, 1152:1280])
            ST(h, "vector", kev)

        for qi in range(16):
            t0 = base_t + qi * 128
            klen = (qi + 1) * 128

            def ldq(sync, t0=t0):
                sync.dma_start(out=x1t[:], in_=io["s1"][t0:t0 + 128, :]).then_inc(h.ld.h, 16)
                h.ld.bump(16)
                return None
            ST(h, "sync", ldq)

            emit_rms(h, x1t[:], xn[:], trash[:], stats)
            tm_to_fm(h, [xn], xfm, pBig[:, 0:512], pBig[:, 512:1024], ident, fm_stride=128, sub0=0)

            def qmm(t):
                last = None
                for half, lo in [(0, 1024), (1, 1536)]:
                    for c in range(8):
                        last = t.matmul(pBig[:, lo:lo + 512], xfm[:, c * 128:(c + 1) * 128],
                                        wqT[:, c * D + half * 512:c * D + half * 512 + 512],
                                        start=(c == 0), stop=(c == 7))
                return last
            ST(h, "tensor", qmm)

            def qevac(v):
                v.tensor_copy(out=qtm[:, 0:512], in_=pBig[:, 1024:1536])
                return v.tensor_copy(out=qtm[:, 512:1024], in_=pBig[:, 1536:2048])
            ST(h, "vector", qevac)

            head_rms_rope(qtm[:], qrp[:], H0, gq, qi)

            def qtr(t):
                last = None
                for c in range(8):
                    last = t.transpose(pBig[:, c * 128:c * 128 + 128],
                                       qrp[:, c * 128:(c + 1) * 128], ident[:])
                return last
            ST(h, "tensor", qtr)

            def qev(v):
                v.tensor_copy(out=qfm[:, 0:512], in_=pBig[:, 0:512])
                return v.tensor_copy(out=qfm[:, 512:1024], in_=pBig[:, 512:1024])
            ST(h, "vector", qev)

            P2H = [0, 4, 1, 5, 2, 6, 3, 7, 8, 12, 9, 13, 10, 14, 11, 15]
            tok0 = h.tok()
            nrnd = (klen + 1023) // 1024
            t_se_last, t_exp, t_rcp, t_pe_last, t_av, t_oev = {}, {}, {}, {}, {}, {}

            def front(hh, qi=qi, klen=klen):
                p = hh % 2
                kvh = P2H[hh] // 4
                qbase = (hh % 2) * 64
                kchunk = kvh // 2
                kbase = (kvh % 2) * 64
                q_sl = qfm[qbase:qbase + 64, (hh // 2) * 128:(hh // 2) * 128 + 128]
                pair = pBig[:, p * 1024:(p + 1) * 1024]
                t_prev = None
                for r in range(nrnd):
                    lo = r * 1024
                    w_ = min(1024, klen - lo)

                    def smm(t, q_sl=q_sl, kchunk=kchunk, kbase=kbase, lo=lo, w_=w_, pair=pair):
                        last = None
                        for kb in range(0, w_, 512):
                            ww = min(512, w_ - kb)
                            last = t.matmul(pair[:, kb:kb + ww], q_sl,
                                            kfm[kbase:kbase + 64,
                                                kchunk * T + lo + kb:kchunk * T + lo + kb + ww],
                                            start=True, stop=True)
                        return last
                    deps = [t_prev] if t_prev is not None else (
                        [t_pe_last[hh - 2]] if hh >= 2 else [tok0])
                    tsc = ST(h, "tensor", smm, deps=deps)

                    def sev(v, lo=lo, w_=w_, qi=qi, p=p, pair=pair):
                        last = None
                        dlo = qi * 128
                        for kb in range(0, w_, 512):
                            ww = min(512, w_ - kb)
                            a0 = lo + kb
                            a1 = a0 + ww
                            if dlo >= a0 and dlo < a1:
                                if dlo > a0:
                                    v.tensor_copy(out=scb2_[p][:, a0:dlo], in_=pair[:, kb:kb + dlo - a0])
                                last = v.tensor_tensor(out=scb2_[p][:, dlo:dlo + 128],
                                                       in0=pair[:, dlo - lo:dlo - lo + 128],
                                                       in1=trineg[:], op=ALU.add)
                            else:
                                last = v.tensor_copy(out=scb2_[p][:, a0:a1], in_=pair[:, kb:kb + ww])
                        return last
                    t_prev = ST(h, "vector", sev, deps=[tsc])
                t_se_last[hh] = t_prev
                tmax = ST(h, "vector", lambda v, p=p, klen=klen: v.tensor_reduce(
                    out=stats[:, 0 + p:1 + p], in_=scb2_[p][:, 0:klen], axis=AX.X,
                    op=ALU.max, negate=True), deps=[t_prev])
                t_exp[hh] = ST(h, "scalar", lambda sc, p=p, klen=klen: sc.activation(
                    out=scb2_[p][:, 0:klen], in_=scb2_[p][:, 0:klen], func=AF.Exp,
                    bias=stats[:, 0 + p:1 + p], accum_out=stats[:, 2 + p:3 + p]), deps=[tmax])
                t_rcp[hh] = ST(h, "vector", lambda v, p=p: v.reciprocal(
                    out=stats[:, 4 + p:5 + p], in_=stats[:, 2 + p:3 + p]), deps=[t_exp[hh]])

            def back(hh, qi=qi):
                p = hh % 2
                kvh = P2H[hh] // 4
                pair = pBig[:, p * 1024:(p + 1) * 1024]
                nks = qi + 1
                t_prev = t_exp[hh]
                for r in range(0, nks, 8):
                    nblk = min(8, nks - r)

                    def ptr(t, r=r, nblk=nblk, p=p, pair=pair):
                        last = None
                        for b_ in range(nblk):
                            ks = r + b_
                            last = t.transpose(pair[:, b_ * 128:(b_ + 1) * 128],
                                               scb2_[p][:, ks * 128:(ks + 1) * 128], ident[:])
                        return last
                    ttr = ST(h, "tensor", ptr, deps=[t_prev])

                    def pev(sc, r=r, nblk=nblk, p=p, pair=pair):
                        last = None
                        for b0 in range(0, nblk, 4):
                            w = min(4, nblk - b0) * 128
                            last = sc.activation(out=ptb2_[p][:, (r + b0) * 128:(r + b0) * 128 + w],
                                                 in_=pair[:, b0 * 128:b0 * 128 + w], func=AF.Copy)
                        return last
                    depsv = [ttr] + ([t_av[hh - 2]] if hh >= 2 and r == 0 else [])
                    t_prev = ST(h, "scalar", pev, deps=depsv)
                t_pe_last[hh] = t_prev

                def avmm(t, qi=qi, kvh=kvh, p=p):
                    last = None
                    for ks in range(qi + 1):
                        last = t.matmul(pAV[:, p * 64:(p + 1) * 64],
                                        ptb2_[p][:, ks * 128:(ks + 1) * 128],
                                        vseq[:, ks * 256 + kvh * 64:ks * 256 + (kvh + 1) * 64],
                                        start=(ks == 0), stop=(ks == qi))
                    return last
                depsav = [t_prev, t_rcp[hh]] + ([t_oev[hh - 2]] if hh >= 2 else [])
                t_av[hh] = ST(h, "tensor", avmm, deps=depsav)
                t_oev[hh] = ST(h, "vector", lambda v, hh=hh, p=p: v.tensor_scalar(
                    out=otm[:, hh * 64:(hh + 1) * 64], in0=pAV[:, p * 64:(p + 1) * 64],
                    scalar1=stats[:, 4 + p:5 + p], scalar2=None, op0=ALU.mult), deps=[t_av[hh]])

            for step in range(H0 + 1):
                if step < H0:
                    front(step)
                if step >= 1:
                    back(step - 1)

            tm_to_fm(h, [otm], ofm, pBig[:, 0:512], pBig[:, 512:1024], ident, fm_stride=128, sub0=0)

            def omm(t):
                last = None
                for half, bank in [(0, pW0), (1, pW1)]:
                    for c in range(8):
                        last = t.matmul(bank[:], ofm[:, c * 128:(c + 1) * 128],
                                        woT[:, c * D + half * 512:c * D + half * 512 + 512],
                                        start=(c == 0), stop=(c == 7))
                return last
            ST(h, "tensor", omm)

            def resid(v):
                v.tensor_tensor(out=xn[:, 0:512], in0=pW0[:], in1=x1t[:, 0:512], op=ALU.add)
                return v.tensor_tensor(out=xn[:, 512:1024], in0=pW1[:], in1=x1t[:, 512:1024], op=ALU.add)
            ST(h, "vector", resid)

            def sxa(sync, t0=t0):
                sync.dma_start(out=io["xa"][t0:t0 + 128, :], in_=xn[:]).then_inc(h.st.h, 16)
                h.st.bump(16)
                sync.wait_ge(h.st.h, h.st.n)
                return sync.nop()
            ST(h, "sync", sxa)

    ST(h, "sync", lambda sync: sync.wait_ge(h.st.h, h.st.n) and None)



def emit_p2(h, io):
    ident = h.sb("m_ident", [128, 128])
    ones1 = h.sb("m_ones1", [1, 128])
    iota8 = h.sb("m_iota8", [1, 8], I32)
    iotaf = h.sb("m_iotaf", [1, 8])
    lrow = h.sb("m_lrow", [1, 16])
    tks = h.sb("m_tks", [1, 16])
    coefb = h.sb("m_coefb", [128, 4])
    ebase_i = h.sb("m_ebasei", [1, 4], I32)

    fcs1 = h.sb("m_fcs1", [128, 8 * 64])
    fcs1b = h.sb("m_fcs1b", [128, 8 * 64])
    fcs2 = h.sb("m_fcs2", [128, 32 * 64])
    fcs2b = h.sb("m_fcs2b", [128, 32 * 64])
    pjs1 = h.sb("m_pjs1", [128, 32 * 64])
    pjs1b = h.sb("m_pjs1b", [128, 32 * 64])
    pjs2 = h.sb("m_pjs2", [128, 8 * 64])
    pjs2b = h.sb("m_pjs2b", [128, 8 * 64])

    xat = [h.sb(f"m_xat{s}", [128, D]) for s in range(4)]
    h0tm = h.sb("m_h0tm", [128, 4 * D])
    h0fm = h.sb("m_h0fm", [128, 8 * TT])
    h1 = h.sb("m_h1", [128, 8 * TT], BF16)
    h2 = h.sb("m_h2", [128, 16 * TT])
    t1 = h.sb("m_t1", [128, 16 * TT], BF16)
    pmix = h.sb("m_pmix", [128, 8 * TT])
    gbuf2_ = [h.sb("m_gbufA", [128, 1024]), h.sb("m_gbufB", [128, 1024])]
    gbufb2_ = [h.sb("m_gbufbA", [128, 1024], BF16), h.sb("m_gbufbB", [128, 1024], BF16)]
    lbuf2_ = [h.sb("m_lbufA", [128, 8 * 128], BF16), h.sb("m_lbufB", [128, 8 * 128], BF16)]
    strip2_ = [h.sb("m_stripA", [128, TT]), h.sb("m_stripB", [128, TT])]
    gbuf = gbuf2_[0]
    lbuf = lbuf2_[0]
    strip = strip2_[0]
    stats = h.sb("m_stats", [128, 8])
    trash = gbuf2_[0][:]

    pF = h.ps("m_pF", [128, 2048])
    pMix = h.ps("m_pMix", [128, 512])
    pTr = h.ps("m_pTr", [128, 512])
    pB2 = h.ps("m_pB2", [128, 512])
    p8 = h.ps("m_p8", [128, 512])
    pjBanks = [pF[:, 0:512], pF[:, 512:1024], pF[:, 1024:1536], pF[:, 1536:2048],
               pMix[:], pTr[:], pB2[:], p8[:]]

    rgF0 = h.ctx.enter_context(h.nc.gpsimd.register("rgF0"))
    rgF1 = h.ctx.enter_context(h.nc.gpsimd.register("rgF1"))
    rgP0 = h.ctx.enter_context(h.nc.gpsimd.register("rgP0"))
    rgP1 = h.ctx.enter_context(h.nc.gpsimd.register("rgP1"))
    rgT = h.ctx.enter_context(h.nc.gpsimd.register("rgT"))

    def loads(sync):
        sync.dma_start(out=ident[:], in_=io["ident"][:]).then_inc(h.ld.h, 16)
        sync.dma_start(out=lrow[:, 0:8], in_=io["fc_logits_row"][:]).then_inc(h.ld.h, 16)
        sync.dma_start(out=lrow[:, 8:16], in_=io["pj_logits_row"][:]).then_inc(h.ld.h, 16)
        h.ld.bump(48)
        return None
    ST(h, "sync", loads)

    def init(v):
        v.memset(ones1[:], 1.0)
        return v.memset(tks[:], 0.0)
    ST(h, "vector", init)
    ST(h, "gpsimd", lambda gp: gp.iota(iota8[:], base=0, pattern=[[1, 8]], channel_multiplier=0))
    ST(h, "vector", lambda v: v.tensor_copy(out=iotaf[:], in_=iota8[:]))

    for gi, lo in [(0, 0), (1, 8)]:
        lsl = lrow[:, lo:lo + 8]
        b = gi * 8
        ST(h, "vector", lambda v, lsl=lsl, b=b: v.tensor_reduce(
            out=tks[:, b + 0:b + 1], in_=lsl, axis=AX.X, op=ALU.max))
        ST(h, "vector", lambda v, lsl=lsl, b=b: v.tensor_scalar(
            out=trash[0:1, 0:8], in0=lsl, scalar1=tks[:, b + 0:b + 1], scalar2=None, op0=ALU.is_ge))
        ST(h, "vector", lambda v, b=b: v.tensor_tensor(
            out=trash[0:1, 8:16], in0=trash[0:1, 0:8], in1=iotaf[:], op=ALU.mult))
        ST(h, "vector", lambda v, b=b: v.tensor_reduce(
            out=tks[:, b + 2:b + 3], in_=trash[0:1, 8:16], axis=AX.X, op=ALU.add))
        ST(h, "vector", lambda v, lsl=lsl, b=b: v.scalar_tensor_tensor(
            out=trash[0:1, 0:8], in0=trash[0:1, 0:8], scalar=-1e30, in1=lsl,
            op0=ALU.mult, op1=ALU.add))
        ST(h, "vector", lambda v, b=b: v.tensor_reduce(
            out=tks[:, b + 1:b + 2], in_=trash[0:1, 0:8], axis=AX.X, op=ALU.max))
        ST(h, "vector", lambda v, b=b: v.tensor_scalar(
            out=trash[0:1, 16:24], in0=trash[0:1, 0:8], scalar1=tks[:, b + 1:b + 2],
            scalar2=None, op0=ALU.is_ge))
        ST(h, "vector", lambda v, b=b: v.tensor_tensor(
            out=trash[0:1, 24:32], in0=trash[0:1, 16:24], in1=iotaf[:], op=ALU.mult))
        ST(h, "vector", lambda v, b=b: v.tensor_reduce(
            out=tks[:, b + 3:b + 4], in_=trash[0:1, 24:32], axis=AX.X, op=ALU.add))
        ST(h, "vector", lambda v, b=b: v.tensor_tensor(
            out=tks[:, b + 7:b + 8], in0=tks[:, b + 1:b + 2], in1=tks[:, b + 0:b + 1], op=ALU.subtract))
        ST(h, "scalar", lambda sc, b=b: sc.activation(
            out=tks[:, b + 4:b + 5], in_=tks[:, b + 7:b + 8], func=AF.Exp))
        ST(h, "vector", lambda v, b=b: v.tensor_scalar(
            out=tks[:, b + 5:b + 6], in0=tks[:, b + 4:b + 5], scalar1=1.0, scalar2=None, op0=ALU.add))
        ST(h, "vector", lambda v, b=b: v.reciprocal(out=tks[:, b + 5:b + 6], in_=tks[:, b + 5:b + 6]))
        ST(h, "vector", lambda v, b=b: v.tensor_tensor(
            out=tks[:, b + 6:b + 7], in0=tks[:, b + 4:b + 5], in1=tks[:, b + 5:b + 6], op=ALU.mult))

    def cbc(t):
        t.matmul(pMix[:, 0:1], ones1[:], tks[:, 5:6], start=True, stop=True)
        t.matmul(pMix[:, 1:2], ones1[:], tks[:, 6:7], start=True, stop=True)
        t.matmul(pMix[:, 2:3], ones1[:], tks[:, 13:14], start=True, stop=True)
        return t.matmul(pMix[:, 3:4], ones1[:], tks[:, 14:15], start=True, stop=True)
    ST(h, "tensor", cbc)
    ST(h, "vector", lambda v: v.tensor_copy(out=coefb[:], in_=pMix[:, 0:4]))

    def packi(v):
        v.tensor_copy(out=trash[0:1, 0:1], in_=tks[:, 2:3])
        v.tensor_copy(out=trash[0:1, 1:2], in_=tks[:, 3:4])
        v.tensor_copy(out=trash[0:1, 2:3], in_=tks[:, 10:11])
        return v.tensor_copy(out=trash[0:1, 3:4], in_=tks[:, 11:12])
    ST(h, "vector", packi)
    ST(h, "vector", lambda v: v.tensor_copy(out=ebase_i[:], in_=trash[0:1, 0:4]))

    def gsetup(gp):
        gp.reg_load(rgF0, ebase_i[:1, 0:1])
        gp.reg_load(rgF1, ebase_i[:1, 1:2])
        gp.reg_load(rgP0, ebase_i[:1, 2:3])
        gp.reg_load(rgP1, ebase_i[:1, 3:4])
        for slot, rg, src_, dst in [(0, rgF0, io["fc_mixB"], io["fc_sel"]),
                                    (1, rgF1, io["fc_mixB"], io["fc_sel"]),
                                    (0, rgP0, io["pj_mixTB"], io["pj_sel"]),
                                    (1, rgP1, io["pj_mixTB"], io["pj_sel"])]:
            gp.reg_mov(rgT, 0)
            gp.reg_add(rgT, rgT, rg)
            gp.reg_mul(rgT, rgT, 32 * 128 * D)
            gp.dma_start(out=dst[slot * 4096:(slot + 1) * 4096, :],
                         in_=bass.AP(src_.tensor, rgT, [[32768, 128], [1, 1], [1, 32768]])
                         ).then_inc(h.gd.h, 16)
            h.gd.bump(16)
        for rg, srcs in [(rgF0, [(io["fc_s1f"], fcs1, 8, D), (io["fc_s2f"], fcs2, 32, HID)]),
                         (rgF1, [(io["fc_s1f"], fcs1b, 8, D), (io["fc_s2f"], fcs2b, 32, HID)]),
                         (rgP0, [(io["pj_s1f"], pjs1, 32, HID), (io["pj_s2f"], pjs2, 8, D)]),
                         (rgP1, [(io["pj_s1f"], pjs1b, 32, HID), (io["pj_s2f"], pjs2b, 8, D)])]:
            for src_t, dst_sb, ntile, rows in srcs:
                gp.reg_mov(rgT, 0)
                gp.reg_add(rgT, rgT, rg)
                gp.reg_mul(rgT, rgT, rows * 64)
                gp.dma_start(out=dst_sb[:, 0:ntile * 64],
                             in_=bass.AP(src_t.tensor, rgT,
                                         [[64, 128], [8192, ntile], [1, 64]])
                             ).then_inc(h.gd.h, 16)
                h.gd.bump(16)
        return None
    ST(h, "gpsimd", gsetup)

    for mt in range(NTILE):
        t0 = mt * TT

        def ldxa(sync, t0=t0):
            for s in range(4):
                sync.dma_start(out=xat[s][:], in_=io["xa"][t0 + s * 128:t0 + (s + 1) * 128, :]).then_inc(h.ld.h, 16)
                h.ld.bump(16)
            return None
        ST(h, "sync", ldxa)

        for s in range(4):
            emit_rms(h, xat[s][:], h0tm[:, s * D:(s + 1) * D], trash, stats)
        tm_to_fm(h, [h0tm[:, s * D:(s + 1) * D] for s in range(4)],
                 h0fm, pF[:, 0:512], pF[:, 512:1024], ident, fm_stride=TT)

        def yz(v):
            last = None
            for s in range(4):
                last = v.memset(h0tm[:, s * D:(s + 1) * D], 0.0)
            return last
        ST(h, "vector", yz)

        for half in range(2):
            jlo = half * 16
            for eslot in range(2):
                s1b = fcs1 if eslot == 0 else fcs1b
                s2b = fcs2 if eslot == 0 else fcs2b
                rg = rgF0 if eslot == 0 else rgF1

                for jg in range(2):
                    def bd1mm(t, jg=jg, s1b=s1b):
                        last = None
                        for jj in range(4):
                            ch = jg * 4 + jj
                            for gpar in range(2):
                                g = ch * 2 + gpar
                                base = gpar * 64
                                last = t.matmul(pF[base:base + 64, jj * 512:(jj + 1) * 512],
                                                s1b[base:base + 64, (g // 2) * 64:(g // 2) * 64 + 64],
                                                h0fm[base:base + 64, ch * TT:(ch + 1) * TT],
                                                start=True, stop=True)
                        return last
                    ST(h, "tensor", bd1mm)

                    def bd1ev(v, jg=jg):
                        last = None
                        for jj in range(4):
                            ch = jg * 4 + jj
                            last = v.tensor_copy(out=h1[:, ch * TT:(ch + 1) * TT],
                                                 in_=pF[:, jj * 512:(jj + 1) * 512])
                        return last
                    ST(h, "vector", bd1ev)

                tok0 = h.tok()
                tg, ttr, tev, tmm, tse, tb2, tcm = {}, {}, {}, {}, {}, {}, {}
                trpair = [pF[:, 0:1024], pF[:, 1024:2048]]
                mixb = [pMix, p8]
                b2b = [pTr, pB2]

                def em_gth(jj):
                    p = jj % 2
                    j = jlo + jj

                    def gth(sync, j=j, p=p, eslot=eslot):
                        r0 = eslot * 4096 + j * 128
                        sync.dma_start(out=gbuf2_[p][:], in_=io["fc_sel"][r0:r0 + 128, :]).then_inc(h.ld.h, 16)
                        h.ld.bump(16)
                        return None
                    deps = ([ttr[jj - 2]] if jj >= 2 else [tok0]) + ([tg[jj - 1]] if jj >= 1 else [])
                    ST(h, "sync", gth, deps=deps)
                    tg[jj] = h.tok()

                def em_gtr(jj):
                    p = jj % 2

                    def gtr(t, p=p):
                        last = None
                        for c in range(8):
                            last = t.transpose(trpair[p][:, c * 128:(c + 1) * 128],
                                               gbuf2_[p][:, c * 128:(c + 1) * 128], ident[:])
                        return last
                    deps = [tg[jj]] + ([tev[jj - 2]] if jj >= 2 else [tok0])
                    ttr[jj] = ST(h, "tensor", gtr, deps=deps)

                def em_gev(jj):
                    p = jj % 2

                    def gev(sc, p=p):
                        sc.activation(out=lbuf2_[p][:, 0:512], in_=trpair[p][:, 0:512],
                                      func=AF.Copy)
                        return sc.activation(out=lbuf2_[p][:, 512:1024], in_=trpair[p][:, 512:1024],
                                             func=AF.Copy)
                    deps = [ttr[jj]] + ([tmm[jj - 2]] if jj >= 2 else [tok0])
                    tev[jj] = ST(h, "scalar", gev, deps=deps)

                def em_mix(jj):
                    p = jj % 2

                    def mixmm(t, p=p):
                        last = None
                        for c in range(8):
                            last = t.matmul(mixb[p][:], lbuf2_[p][:, c * 128:(c + 1) * 128],
                                            h1[:, c * TT:(c + 1) * TT], start=(c == 0), stop=(c == 7))
                        return last
                    deps = [tev[jj]] + ([tse[jj - 2]] if jj >= 2 else [tok0])
                    tmm[jj] = ST(h, "tensor", mixmm, deps=deps)

                def em_sev(jj):
                    p = jj % 2
                    deps = [tmm[jj]] + ([tb2[jj - 2]] if jj >= 2 else [tok0])
                    tse[jj] = ST(h, "vector",
                                 lambda v, p=p: v.tensor_copy(out=strip2_[p][:], in_=mixb[p][:]),
                                 deps=deps)

                def em_b2(jj):
                    p = jj % 2
                    j = jlo + jj

                    def b2mm(t, j=j, p=p, s2b=s2b):
                        last = None
                        for gpar in range(2):
                            g = j * 2 + gpar
                            base = gpar * 64
                            last = t.matmul(b2b[p][base:base + 64, 0:TT],
                                            s2b[base:base + 64, (g // 2) * 64:(g // 2) * 64 + 64],
                                            strip2_[p][base:base + 64, :], start=True, stop=True)
                        return last
                    deps = [tse[jj]] + ([tcm[jj - 2]] if jj >= 2 else [tok0])
                    tb2[jj] = ST(h, "tensor", b2mm, deps=deps)

                def em_comb(jj):
                    p = jj % 2
                    j = jlo + jj
                    if eslot == 0:
                        tcm[jj] = ST(h, "vector", lambda v, j=j, p=p, half=half: v.tensor_scalar(
                            out=h2[:, (j - half * 16) * TT:(j - half * 16 + 1) * TT], in0=b2b[p][:, 0:TT],
                            scalar1=coefb[:, 0:1], scalar2=None, op0=ALU.mult), deps=[tb2[jj]])
                    else:
                        tcm[jj] = ST(h, "vector", lambda v, j=j, p=p, half=half: v.scalar_tensor_tensor(
                            out=h2[:, (j - half * 16) * TT:(j - half * 16 + 1) * TT], in0=b2b[p][:, 0:TT],
                            scalar=coefb[:, 1:2],
                            in1=h2[:, (j - half * 16) * TT:(j - half * 16 + 1) * TT],
                            op0=ALU.mult, op1=ALU.add), deps=[tb2[jj]])

                for jj in range(18):
                    if jj < 16:
                        em_gth(jj)
                        em_gtr(jj)
                    if 1 <= jj <= 16:
                        em_gev(jj - 1)
                        em_mix(jj - 1)
                    if jj >= 2:
                        em_sev(jj - 2)
                        em_b2(jj - 2)
                        em_comb(jj - 2)

            def relu_(v):
                last = None
                for c in range(16):
                    last = v.tensor_scalar(out=h2[:, c * TT:(c + 1) * TT], in0=h2[:, c * TT:(c + 1) * TT],
                                           scalar1=0.0, scalar2=None, op0=ALU.max)
                return last
            ST(h, "vector", relu_)

            def sq_(sc):
                last = None
                for c in range(16):
                    last = sc.activation(out=h2[:, c * TT:(c + 1) * TT], in_=h2[:, c * TT:(c + 1) * TT],
                                         func=AF.Square)
                return last
            ST(h, "scalar", sq_)

            for eslot in range(2):
                s1b = pjs1 if eslot == 0 else pjs1b
                s2b = pjs2 if eslot == 0 else pjs2b
                rg = rgP0 if eslot == 0 else rgP1
                dcol = 2 + eslot

                for jg in range(4):
                    def pbd1(t, jg=jg, s1b=s1b, half=half):
                        last = None
                        for jj in range(4):
                            ch = jg * 4 + jj
                            gch = half * 16 + ch
                            for gpar in range(2):
                                g = gch * 2 + gpar
                                base = gpar * 64
                                last = t.matmul(pF[base:base + 64, jj * 512:(jj + 1) * 512],
                                                s1b[base:base + 64, (g // 2) * 64:(g // 2) * 64 + 64],
                                                h2[base:base + 64, ch * TT:(ch + 1) * TT],
                                                start=True, stop=True)
                        return last
                    ST(h, "tensor", pbd1)

                    def pbd1e(v, jg=jg):
                        last = None
                        for jj in range(4):
                            ch = jg * 4 + jj
                            last = v.tensor_copy(out=t1[:, ch * TT:(ch + 1) * TT],
                                                 in_=pF[:, jj * 512:(jj + 1) * 512])
                        return last
                    ST(h, "vector", pbd1e)

                ptok0 = h.tok()
                tpg, tpm, tpc = {}, {}, {}

                def em_pgth(ii):
                    p = ii % 2
                    gi_ = half * 16 + ii

                    def pgth(sync, gi_=gi_, p=p, eslot=eslot):
                        r0 = eslot * 4096 + gi_ * 128
                        sync.dma_start(out=gbuf2_[p][:], in_=io["pj_sel"][r0:r0 + 128, :]).then_inc(h.ld.h, 16)
                        h.ld.bump(16)
                        return None
                    deps = ([tpc[ii - 2]] if ii >= 2 else [ptok0]) + ([tpg[ii - 1]] if ii >= 1 else [])
                    ST(h, "sync", pgth, deps=deps)
                    tpg[ii] = h.tok()

                def em_pcast(ii):
                    p = ii % 2
                    deps = [tpg[ii]] + ([tpm[ii - 2]] if ii >= 2 else [ptok0])
                    tpc[ii] = ST(h, "vector", lambda v, p=p: v.tensor_copy(
                        out=gbufb2_[p][:], in_=gbuf2_[p][:]), deps=deps)

                def em_pmm(ii):
                    p = ii % 2

                    def pmm(t, ii=ii, p=p):
                        last = None
                        for o in range(8):
                            last = t.matmul(pjBanks[o], gbufb2_[p][:, o * 128:(o + 1) * 128],
                                            t1[:, ii * TT:(ii + 1) * TT],
                                            start=(ii == 0), stop=(ii == 15))
                        return last
                    deps = [tpc[ii]] + ([ptok0] if ii < 2 else [])
                    tpm[ii] = ST(h, "tensor", pmm, deps=deps)

                for ii in range(17):
                    if ii < 16:
                        em_pgth(ii)
                        em_pcast(ii)
                    if ii >= 1:
                        em_pmm(ii - 1)

                def pev(v):
                    last = None
                    for o in range(8):
                        last = v.tensor_copy(out=pmix[:, o * TT:(o + 1) * TT], in_=pjBanks[o])
                    return last
                ST(h, "vector", pev)

                for sub in range(4):
                    def pb2(t, sub=sub, s2b=s2b):
                        last = None
                        for g in range(16):
                            base = (g % 2) * 64
                            bank = pB2 if g % 2 == 0 else pTr
                            last = t.matmul(bank[:, (g // 2) * 64:(g // 2) * 64 + 64],
                                            pmix[base:base + 64, (g // 2) * TT + sub * 128:(g // 2) * TT + sub * 128 + 128],
                                            s2b[base:base + 64, (g // 2) * 64:(g // 2) * 64 + 64],
                                            start=True, stop=True)
                        return last
                    ST(h, "tensor", pb2)

                    def pacc(v, sub=sub, dcol=dcol):
                        yv = h0tm[:, sub * D:(sub + 1) * D].rearrange("p (a b k) -> p a b k", b=2, k=64)
                        ev = yv[:, :, 0:1, :]
                        ov = yv[:, :, 1:2, :]
                        pbv = pB2[:].rearrange("p (a k) -> p a k", k=64).unsqueeze(2)
                        ptv = pTr[:].rearrange("p (a k) -> p a k", k=64).unsqueeze(2)
                        v.scalar_tensor_tensor(out=ev, in0=pbv, scalar=coefb[:, dcol:dcol + 1],
                                               in1=ev, op0=ALU.mult, op1=ALU.add)
                        return v.scalar_tensor_tensor(out=ov, in0=ptv, scalar=coefb[:, dcol:dcol + 1],
                                                      in1=ov, op0=ALU.mult, op1=ALU.add)
                    ST(h, "vector", pacc)

        def ldx1(sync, t0=t0):
            for s in range(4):
                sync.dma_start(out=h0fm[:, s * D:(s + 1) * D],
                               in_=io["s1"][t0 + s * 128:t0 + (s + 1) * 128, :]).then_inc(h.ld.h, 16)
                h.ld.bump(16)
            return None
        ST(h, "sync", ldx1)
        ep0 = h.tok()
        st_toks = []
        for s in range(4):
            tr2a = ST(h, "vector", lambda v, s=s: v.tensor_tensor(
                out=h0tm[:, s * D:(s + 1) * D], in0=h0tm[:, s * D:(s + 1) * D],
                in1=xat[s][:], op=ALU.add), deps=[ep0])
            tr2 = ST(h, "vector", lambda v, s=s: v.tensor_tensor(
                out=h0tm[:, s * D:(s + 1) * D], in0=h0tm[:, s * D:(s + 1) * D],
                in1=h0fm[:, s * D:(s + 1) * D], op=ALU.add), deps=[tr2a])
            tln = emit_ln(h, h0tm[:, s * D:(s + 1) * D], h0tm[:, s * D:(s + 1) * D], trash, nfeat=D,
                          eps=1e-5, stats=stats, dep=tr2)

            def st2(sync, s=s, t0=t0):
                sync.dma_start(out=io["s2"][t0 + s * 128:t0 + (s + 1) * 128, :],
                               in_=h0tm[:, s * D:(s + 1) * D]).then_inc(h.st.h, 16)
                h.st.bump(16)
                sync.wait_ge(h.st.h, h.st.n)
                return sync.nop()
            st_toks.append(ST(h, "sync", st2, deps=[tln]))

    ST(h, "sync", lambda sync: sync.wait_ge(h.st.h, h.st.n) and None)



def emit_p3(h, io):
    ident = h.sb("v_ident", [128, 128])
    ones1 = h.sb("v_ones1", [1, 128])
    wtile = h.sb("v_wtile", [128, D])
    wr8 = h.sb("v_wr8", [128, 8])
    wr8t = h.sb("v_wr8t", [8, 128])
    rsrow = h.sb("v_rsrow", [1, D])
    rsbc = h.sb("v_rsbc", [128, D])
    x2tm = h.sb("v_x2tm", [128, 16 * D])
    x2fm = h.sb("v_x2fm", [128, 8 * T])
    scb2_ = [h.sb("v_scbA", [128, T]), h.sb("v_scbB", [128, T])]
    pmk2_ = [h.sb("v_pmkA", [128, T]), h.sb("v_pmkB", [128, T])]
    ptb2_ = [h.sb("v_ptbA", [128, T]), h.sb("v_ptbB", [128, T])]
    att2_ = [h.sb("v_attA", [128, D]), h.sb("v_attB", [128, D])]
    xres2_ = [h.sb("v_xresA", [128, D]), h.sb("v_xresB", [128, D])]
    st3 = h.sb("v_st3", [128, 24])

    pBig = h.ps("v_pBig", [128, 2048])
    pV = h.ps("v_pV", [128, 1024])
    pT2 = h.ps("v_pT2", [128, 512])

    def loads(sync):
        sync.dma_start(out=ident[:], in_=io["ident"][:]).then_inc(h.ld.h, 16)
        h.ld.bump(16)
        return None
    ST(h, "sync", loads)
    ST(h, "vector", lambda v: v.memset(ones1[:], 1.0))

    for t_ in range(8):
        def ldw(sync, t_=t_):
            sync.dma_start(out=wtile[:], in_=io["W1"][t_ * 128:(t_ + 1) * 128, :]).then_inc(h.ld.h, 16)
            h.ld.bump(16)
            return None
        ST(h, "sync", ldw)
        ST(h, "vector", lambda v, t_=t_: v.tensor_reduce(
            out=wr8[:, t_:t_ + 1], in_=wtile[:], axis=AX.X, op=ALU.add))
    ST(h, "tensor", lambda t: t.transpose(pT2[0:8, 0:128], wr8[:], ident[:]))
    ST(h, "vector", lambda v: v.tensor_copy(out=wr8t[:], in_=pT2[0:8, 0:128]))

    def wout(sync):
        sync.dma_start(out=io["rs_scr"][:], in_=wr8t[:]).then_inc(h.st.h, 16)
        h.st.bump(16)
        sync.wait_ge(h.st.h, h.st.n)
        return sync.nop()
    ST(h, "sync", wout)

    def win(sync):
        sync.dma_start(out=rsrow[:], in_=io["rs_scr"].rearrange("a b -> (a b)").unsqueeze(0)).then_inc(h.ld.h, 16)
        h.ld.bump(16)
        return None
    ST(h, "sync", win)

    def rbc(t):
        t.matmul(pT2[:, 0:512], ones1[:], rsrow[:, 0:512], start=True, stop=True)
        return t.matmul(pV[:, 0:512], ones1[:], rsrow[:, 512:1024], start=True, stop=True)
    ST(h, "tensor", rbc)

    def rbc2(v):
        v.tensor_copy(out=rsbc[:, 0:512], in_=pT2[:, 0:512])
        return v.tensor_copy(out=rsbc[:, 512:1024], in_=pV[:, 0:512])
    ST(h, "vector", rbc2)

    for seq in range(2):
        base_t = seq * T
        for ti in range(16):
            def ldx(sync, ti=ti, base_t=base_t):
                sync.dma_start(out=x2tm[:, ti * D:(ti + 1) * D],
                               in_=io["s2"][base_t + ti * 128:base_t + (ti + 1) * 128, :]).then_inc(h.ld.h, 16)
                h.ld.bump(16)
                return None
            ST(h, "sync", ldx)
            tm_to_fm(h, [x2tm[:, ti * D:(ti + 1) * D]], x2fm,
                     pBig[:, 0:512], pBig[:, 512:1024], ident, fm_stride=T, sub0=ti)

        tok0 = h.tok()
        t_pmk, t_ssum, t_zop, t_trlast, t_pelast, t_av, t_attev, t_st3 = {}, {}, {}, {}, {}, {}, {}, {}

        def front(qi, base_t=base_t):
            p = qi % 2
            b = p * 12
            pair = pBig[:, p * 1024:(p + 1) * 1024]
            t_prev = None
            for r in range(2):
                def smm(t, qi=qi, r=r, pair=pair):
                    last = None
                    for kb in range(2):
                        for c in range(8):
                            last = t.matmul(pair[:, kb * 512:(kb + 1) * 512],
                                            x2fm[:, c * T + qi * 128:c * T + qi * 128 + 128],
                                            x2fm[:, c * T + (r * 2 + kb) * 512:c * T + (r * 2 + kb + 1) * 512],
                                            start=(c == 0), stop=(c == 7))
                    return last
                deps = [t_prev] if t_prev is not None else (
                    [t_pelast[qi - 2]] if qi >= 2 else [tok0])
                tsc = ST(h, "tensor", smm, deps=deps)

                def sev(v, r=r, p=p, pair=pair):
                    v.tensor_scalar(out=scb2_[p][:, r * 1024:r * 1024 + 512], in0=pair[:, 0:512],
                                    scalar1=0.03125, scalar2=None, op0=ALU.mult)
                    return v.tensor_scalar(out=scb2_[p][:, r * 1024 + 512:(r + 1) * 1024],
                                           in0=pair[:, 512:1024],
                                           scalar1=0.03125, scalar2=None, op0=ALU.mult)
                t_prev = ST(h, "vector", sev, deps=[tsc])

            tmax = ST(h, "vector", lambda v, p=p, b=b: v.tensor_reduce(
                out=st3[:, b + 0:b + 1], in_=scb2_[p][:], axis=AX.X, op=ALU.max, negate=True),
                deps=[t_prev])
            texp = ST(h, "scalar", lambda sc, p=p, b=b: sc.activation(
                out=scb2_[p][:], in_=scb2_[p][:], func=AF.Exp,
                bias=st3[:, b + 0:b + 1], accum_out=st3[:, b + 1:b + 2]), deps=[tmax])

            def zop(v, b=b):
                v.reciprocal(out=st3[:, b + 2:b + 3], in_=st3[:, b + 1:b + 2])
                return v.tensor_scalar(out=st3[:, b + 3:b + 4], in0=st3[:, b + 1:b + 2],
                                       scalar1=0.01, scalar2=None, op0=ALU.mult)
            t_zop[qi] = ST(h, "vector", zop, deps=[texp])

            depsm = [t_zop[qi]] + ([t_trlast[qi - 2]] if qi >= 2 else [])
            t_pmk[qi] = ST(h, "vector", lambda v, p=p, b=b: v.scalar_tensor_tensor(
                out=pmk2_[p][:], in0=scb2_[p][:], scalar=st3[:, b + 3:b + 4], in1=scb2_[p][:],
                op0=ALU.is_ge, op1=ALU.mult), deps=depsm)
            t_ssum[qi] = ST(h, "vector", lambda v, p=p, b=b: v.tensor_reduce(
                out=st3[:, b + 4:b + 5], in_=pmk2_[p][:], axis=AX.X, op=ALU.add), deps=[t_pmk[qi]])

        def back(qi, base_t=base_t):
            p = qi % 2
            b = p * 12
            pair = pBig[:, p * 1024:(p + 1) * 1024]
            t_prev = t_pmk[qi]
            for r in range(2):
                def ptr(t, r=r, p=p, pair=pair):
                    last = None
                    for b_ in range(8):
                        ks = r * 8 + b_
                        last = t.transpose(pair[:, b_ * 128:(b_ + 1) * 128],
                                           pmk2_[p][:, ks * 128:(ks + 1) * 128], ident[:])
                    return last
                ttr = ST(h, "tensor", ptr, deps=[t_prev])

                def pev(sc, r=r, p=p, pair=pair):
                    sc.activation(out=ptb2_[p][:, r * 1024:r * 1024 + 512],
                                  in_=pair[:, 0:512], func=AF.Copy)
                    return sc.activation(out=ptb2_[p][:, r * 1024 + 512:(r + 1) * 1024],
                                         in_=pair[:, 512:1024], func=AF.Copy)
                depsv = [ttr] + ([t_av[qi - 2]] if qi >= 2 and r == 0 else [])
                t_prev = ST(h, "scalar", pev, deps=depsv)
            t_trlast[qi] = t_prev
            t_pelast[qi] = t_prev

            def avmm(t, p=p):
                last = None
                for half in range(2):
                    for ks in range(16):
                        last = t.matmul(pV[:, half * 512:(half + 1) * 512],
                                        ptb2_[p][:, ks * 128:(ks + 1) * 128],
                                        x2tm[:, ks * D + half * 512:ks * D + (half + 1) * 512],
                                        start=(ks == 0), stop=(ks == 15))
                return last
            depsav = [t_prev] + ([t_attev[qi - 1]] if qi >= 1 else [])
            t_av[qi] = ST(h, "tensor", avmm, deps=depsav)

            def sfr(v, b=b):
                return v.tensor_tensor(out=st3[:, b + 5:b + 6], in0=st3[:, b + 4:b + 5],
                                       in1=st3[:, b + 2:b + 3], op=ALU.mult)
            tsf = ST(h, "vector", sfr, deps=[t_ssum[qi]])

            def den(v, b=b):
                v.tensor_scalar(out=st3[:, b + 6:b + 7], in0=st3[:, b + 5:b + 6],
                                scalar1=1e-9, scalar2=None, op0=ALU.add)
                return v.tensor_scalar(out=st3[:, b + 7:b + 8], in0=st3[:, b + 5:b + 6],
                                       scalar1=-0.01, scalar2=0.01, op0=ALU.mult, op1=ALU.add)
            tdn = ST(h, "vector", den, deps=[tsf])
            tdi = ST(h, "vector", lambda v, b=b: v.reciprocal(
                out=st3[:, b + 6:b + 7], in_=st3[:, b + 6:b + 7]), deps=[tdn])
            tsc2 = ST(h, "vector", lambda v, b=b: v.tensor_tensor(
                out=st3[:, b + 6:b + 7], in0=st3[:, b + 6:b + 7], in1=st3[:, b + 2:b + 3],
                op=ALU.mult), deps=[tdi])

            def attev(v, p=p, b=b):
                v.tensor_scalar(out=att2_[p][:, 0:512], in0=pV[:, 0:512],
                                scalar1=st3[:, b + 6:b + 7], scalar2=None, op0=ALU.mult)
                return v.tensor_scalar(out=att2_[p][:, 512:1024], in0=pV[:, 512:1024],
                                       scalar1=st3[:, b + 6:b + 7], scalar2=None, op0=ALU.mult)
            t_attev[qi] = ST(h, "vector", attev, deps=[t_av[qi], tsc2])

            tcas = ST(h, "vector", lambda v, p=p, b=b: v.scalar_tensor_tensor(
                out=att2_[p][:], in0=rsbc[:], scalar=st3[:, b + 7:b + 8], in1=att2_[p][:],
                op0=ALU.mult, op1=ALU.add), deps=[t_attev[qi]])

            depsr = [tcas] + ([t_st3[qi - 1]] if qi >= 1 else [])
            tres = ST(h, "vector", lambda v, qi=qi, p=p: v.tensor_tensor(
                out=xres2_[p][:], in0=att2_[p][:], in1=x2tm[:, qi * D:(qi + 1) * D],
                op=ALU.add), deps=depsr)

            tln = emit_ln(h, xres2_[p][:], xres2_[p][:], xres2_[1 - p][:], nfeat=D, eps=1e-5,
                          stats=st3[:, 16:24], dep=tres)

            def st3f(sync, qi=qi, p=p, base_t=base_t):
                sync.dma_start(out=io["s3"][base_t + qi * 128:base_t + (qi + 1) * 128, :],
                               in_=xres2_[p][:]).then_inc(h.st.h, 16)
                h.st.bump(16)
                sync.wait_ge(h.st.h, h.st.n)
                return sync.nop()
            t_st3[qi] = ST(h, "sync", st3f, deps=[tln])

        for step in range(17):
            if step < 16:
                front(step)
            if step >= 1:
                back(step - 1)

    ST(h, "sync", lambda sync: sync.wait_ge(h.st.h, h.st.n) and None)



def emit_p4(h, io):
    ident = h.sb("t_ident", [128, 128])
    ones1 = h.sb("t_ones1", [1, 128])
    brow = h.sb("t_brow", [1, D])
    bbc = h.sb("t_bbc", [128, D])
    wrT = h.sb("t_wrT", [128, 4 * 512])
    wiT = h.sb("t_wiT", [128, 4 * 512])
    wiTn = h.sb("t_wiTn", [128, 4 * 512])
    abuf = h.sb("t_abuf", [128, 512])
    bbuf = h.sb("t_bbuf", [128, 512])
    tmpw = h.sb("t_tmpw", [128, 512])
    x3t = h.sb("t_x3t", [128, D])
    x3fm = h.sb("t_x3fm", [128, 8 * 128])
    yb = h.sb("t_yb", [128, D])
    stats = h.sb("t_stats", [128, 8])
    trash = h.sb("t_trash", [128, D])

    pR = h.ps("t_pR", [128, 512])
    pI = h.ps("t_pI", [128, 512])
    pT3 = h.ps("t_pT3", [128, 512])
    pT4 = h.ps("t_pT4", [128, 512])

    def loads(sync):
        sync.dma_start(out=ident[:], in_=io["ident"][:]).then_inc(h.ld.h, 16)
        sync.dma_start(out=brow[:], in_=io["bias_ri_row"][:]).then_inc(h.ld.h, 16)
        h.ld.bump(32)
        return None
    ST(h, "sync", loads)
    ST(h, "vector", lambda v: v.memset(ones1[:], 1.0))

    def bbc_mm(t):
        t.matmul(pR[:], ones1[:], brow[:, 0:512], start=True, stop=True)
        return t.matmul(pI[:], ones1[:], brow[:, 512:1024], start=True, stop=True)
    ST(h, "tensor", bbc_mm)

    def bbc_ev(v):
        v.tensor_copy(out=bbc[:, 0:512], in_=pR[:])
        return v.tensor_copy(out=bbc[:, 512:1024], in_=pI[:])
    ST(h, "vector", bbc_ev)

    for part, dst in [("R", wrT), ("I", wiT)]:
        for ch in range(4):
            for cc in range(4):
                def lab(sync, part=part, ch=ch, cc=cc):
                    sync.dma_start(out=abuf[:], in_=io[f"Abig{part}"][cc, ch * 128:(ch + 1) * 128, :]).then_inc(h.ld.h, 16)
                    sync.dma_start(out=bbuf[:], in_=io[f"Bbig{part}"][cc, ch * 128:(ch + 1) * 128, :]).then_inc(h.ld.h, 16)
                    h.ld.bump(32)
                    return None
                ST(h, "sync", lab)
                if cc == 0:
                    ST(h, "vector", lambda v, dst=dst, ch=ch: v.tensor_tensor(
                        out=dst[:, ch * 512:(ch + 1) * 512], in0=abuf[:], in1=bbuf[:], op=ALU.mult))
                else:
                    ST(h, "vector", lambda v: v.tensor_tensor(
                        out=tmpw[:], in0=abuf[:], in1=bbuf[:], op=ALU.mult))
                    ST(h, "vector", lambda v, dst=dst, ch=ch: v.tensor_tensor(
                        out=dst[:, ch * 512:(ch + 1) * 512], in0=dst[:, ch * 512:(ch + 1) * 512],
                        in1=tmpw[:], op=ALU.add))
    ST(h, "vector", lambda v: v.tensor_scalar(out=wiTn[:], in0=wiT[:], scalar1=-1.0,
                                              scalar2=None, op0=ALU.mult))

    for ti in range(NTOK // 128):
        t0 = ti * 128

        def ldx(sync, t0=t0):
            sync.dma_start(out=x3t[:], in_=io["s3"][t0:t0 + 128, :]).then_inc(h.ld.h, 16)
            h.ld.bump(16)
            return None
        ST(h, "sync", ldx)

        tm_to_fm(h, [x3t], x3fm, pT3, pT4, ident, fm_stride=128, sub0=0)

        def ttmm(t):
            last = None
            for i4 in range(4):
                t.matmul(pR[:], x3fm[:, i4 * 128:(i4 + 1) * 128], wrT[:, i4 * 512:(i4 + 1) * 512],
                         start=(i4 == 0), stop=False)
                t.matmul(pI[:], x3fm[:, i4 * 128:(i4 + 1) * 128], wiT[:, i4 * 512:(i4 + 1) * 512],
                         start=(i4 == 0), stop=False)
            for i4 in range(4):
                t.matmul(pR[:], x3fm[:, (4 + i4) * 128:(5 + i4) * 128], wiTn[:, i4 * 512:(i4 + 1) * 512],
                         start=False, stop=(i4 == 3))
                last = t.matmul(pI[:], x3fm[:, (4 + i4) * 128:(5 + i4) * 128], wrT[:, i4 * 512:(i4 + 1) * 512],
                                start=False, stop=(i4 == 3))
            return last
        ST(h, "tensor", ttmm)

        def yev(v):
            v.tensor_tensor(out=yb[:, 0:512], in0=pR[:], in1=bbc[:, 0:512], op=ALU.add)
            return v.tensor_tensor(out=yb[:, 512:1024], in0=pI[:], in1=bbc[:, 512:1024], op=ALU.add)
        ST(h, "vector", yev)
        ST(h, "vector", lambda v: v.tensor_tensor(out=yb[:], in0=yb[:], in1=x3t[:], op=ALU.add))
        emit_ln(h, yb[:], yb[:], trash[:], nfeat=D, eps=1e-5, stats=stats)

        def st4(sync, t0=t0):
            sync.dma_start(out=io["out"][t0:t0 + 128, :], in_=yb[:]).then_inc(h.st.h, 16)
            h.st.bump(16)
            sync.wait_ge(h.st.h, h.st.n)
            return sync.nop()
        ST(h, "sync", st4)

    ST(h, "sync", lambda sync: sync.wait_ge(h.st.h, h.st.n) and None)


def build_p4():
    nc = bass.Bass("TRN2", target_bir_lowering=False)
    with ExitStack() as ctx:
        io = {}
        io["s3"] = nc.dram_tensor("s3", [NTOK, D], F32, kind="ExternalInput").ap()
        for nm, shape in [("AbigR", [4, 512, 512]), ("BbigR", [4, 512, 512]),
                          ("AbigI", [4, 512, 512]), ("BbigI", [4, 512, 512]),
                          ("bias_ri_row", [1, D]), ("ident", [128, 128])]:
            io[nm] = nc.dram_tensor(nm, shape, F32, kind="ExternalInput").ap()
        io["out"] = nc.dram_tensor("out", [NTOK, D], F32, kind="ExternalOutput").ap()
        block = ctx.enter_context(nc.Block())
        h = H(nc, ctx, block)
        emit_p4(h, io)
    return nc



def build_p0a(variant):
    nc = bass.Bass("TRN2", target_bir_lowering=False)
    with ExitStack() as ctx:
        io = {}
        io["x0"] = nc.dram_tensor("x0", [NTOK, D], F32, kind="ExternalInput").ap()
        for nm, shape in [("wx1T", [D, 160]), ("w2nT", [5, 32, D]), ("mu_x", [D, 1]),
                          ("x_biasT", [D, 5]), ("ident", [128, 128])]:
            io[nm] = nc.dram_tensor(nm, shape, F32, kind="ExternalInput").ap()
        if variant == "A":
            for nm, shape in [("wrT", [D, 512]), ("wkT", [D, 512]), ("ww1T", [D, 64]),
                              ("ww2T", [64, 512]), ("bw2", [512, 1])]:
                io[nm] = nc.dram_tensor(nm, shape, F32, kind="ExternalInput").ap()
            for nm in ["rsc", "ksc", "wsc"]:
                io[nm] = nc.dram_tensor(nm, [512, NTOK], F32, kind="ExternalOutput").ap()
        else:
            for nm, shape in [("wvT", [D, D]), ("wgT", [D, D])]:
                io[nm] = nc.dram_tensor(nm, shape, F32, kind="ExternalInput").ap()
            for nm in ["vsc", "gsc"]:
                io[nm] = nc.dram_tensor(nm, [NTOK, D], F32, kind="ExternalOutput").ap()
        block = ctx.enter_context(nc.Block())
        h = H(nc, ctx, block)
        emit_p0a(h, io, variant)
    return nc


def build_p0b(debug=False):
    nc = bass.Bass("TRN2", target_bir_lowering=False)
    with ExitStack() as ctx:
        io = {}
        io["x0"] = nc.dram_tensor("x0", [NTOK, D], F32, kind="ExternalInput").ap()
        for nm, shape in [("woT", [D, D]), ("bonusT", [128, 4]), ("ident", [128, 128]),
                          ("mask64", [64, 64]), ("ln_w", [1, 256]), ("ln_b", [1, 256])]:
            io[nm] = nc.dram_tensor(nm, shape, F32, kind="ExternalInput").ap()
        for nm in ["rsc", "ksc", "wsc"]:
            io[nm] = nc.dram_tensor(nm, [512, NTOK], F32, kind="ExternalInput").ap()
        for nm in ["vsc", "gsc"]:
            io[nm] = nc.dram_tensor(nm, [NTOK, D], F32, kind="ExternalInput").ap()
        io["s1"] = nc.dram_tensor("s1", [NTOK, D], F32, kind="ExternalOutput").ap()
        if debug:
            io["d_buf"] = nc.dram_tensor("d_buf", [128, 1024], F32, kind="ExternalOutput").ap()
            io["d_otm"] = nc.dram_tensor("d_otm", [512, D], F32, kind="ExternalOutput").ap()
        block = ctx.enter_context(nc.Block())
        h = H(nc, ctx, block)
        emit_p0b(h, io, debug=debug)
    return nc


def build_p3():
    nc = bass.Bass("TRN2", target_bir_lowering=False)
    with ExitStack() as ctx:
        io = {}
        io["s2"] = nc.dram_tensor("s2", [NTOK, D], F32, kind="ExternalInput").ap()
        io["W1"] = nc.dram_tensor("W1", [D, D], F32, kind="ExternalInput").ap()
        io["ident"] = nc.dram_tensor("ident", [128, 128], F32, kind="ExternalInput").ap()
        io["s3"] = nc.dram_tensor("s3", [NTOK, D], F32, kind="ExternalOutput").ap()
        io["rs_scr"] = nc.dram_tensor("rs_scr", [8, 128], F32).ap()
        block = ctx.enter_context(nc.Block())
        h = H(nc, ctx, block)
        emit_p3(h, io)
    return nc


def build_p2():
    nc = bass.Bass("TRN2", target_bir_lowering=False)
    with ExitStack() as ctx:
        io = {}
        io["xa"] = nc.dram_tensor("xa", [NTOK, D], F32, kind="ExternalInput").ap()
        io["s1"] = nc.dram_tensor("s1", [NTOK, D], F32, kind="ExternalInput").ap()
        for nm, shape in [("ident", [128, 128]), ("fc_logits_row", [1, 8]), ("pj_logits_row", [1, 8]),
                          ("fc_mixB", [BANK, 32, 128, D]), ("pj_mixTB", [BANK, 32, 128, D]),
                          ("fc_s1f", [BANK * D, BLK]), ("fc_s2f", [BANK * HID, BLK]),
                          ("pj_s1f", [BANK * HID, BLK]), ("pj_s2f", [BANK * D, BLK])]:
            io[nm] = nc.dram_tensor(nm, shape, F32, kind="ExternalInput").ap()
        io["s2"] = nc.dram_tensor("s2", [NTOK, D], F32, kind="ExternalOutput").ap()
        io["fc_sel"] = nc.dram_tensor("fc_sel", [2 * 4096, D], F32).ap()
        io["pj_sel"] = nc.dram_tensor("pj_sel", [2 * 4096, D], F32).ap()
        block = ctx.enter_context(nc.Block())
        h = H(nc, ctx, block)
        emit_p2(h, io)
    return nc


def build_p1():
    nc = bass.Bass("TRN2", target_bir_lowering=False)
    with ExitStack() as ctx:
        io = {}
        io["s1"] = nc.dram_tensor("s1", [NTOK, D], F32, kind="ExternalInput").ap()
        for nm, shape in [("wqT2", [D, D]), ("wkT2", [D, 256]), ("wvT2", [D, 256]),
                          ("woT2", [D, D]), ("rope_cos", [T, 32]), ("rope_sin", [T, 32]),
                          ("q_gain_row", [1, H0]), ("k_gain_row", [1, HKV]),
                          ("ident", [128, 128]), ("tri_neg", [128, 128])]:
            io[nm] = nc.dram_tensor(nm, shape, F32, kind="ExternalInput").ap()
        io["xa"] = nc.dram_tensor("xa", [NTOK, D], F32, kind="ExternalOutput").ap()
        block = ctx.enter_context(nc.Block())
        h = H(nc, ctx, block)
        emit_p1(h, io)
    return nc



def build_fused():
    nc = bass.Bass("TRN2", target_bir_lowering=False)
    with ExitStack() as ctx:
        io = {}
        io["x0"] = nc.dram_tensor("x0", [NTOK, D], F32, kind="ExternalInput").ap()
        ext = [("wx1T", [D, 160]), ("w2nT", [5, 32, D]), ("mu_x", [D, 1]),
               ("x_biasT", [D, 5]), ("ident", [128, 128]),
               ("wrT", [D, 512]), ("wkT", [D, 512]), ("ww1T", [D, 64]),
               ("ww2T", [64, 512]), ("bw2", [512, 1]),
               ("wvT", [D, D]), ("wgT", [D, D]),
               ("woT", [D, D]), ("bonusT", [128, 4]), ("mask64", [64, 64]),
               ("ln_w", [1, 256]), ("ln_b", [1, 256]),
               ("wqT2", [D, D]), ("wkT2", [D, 256]), ("wvT2", [D, 256]),
               ("woT2", [D, D]), ("rope_cos", [T, 32]), ("rope_sin", [T, 32]),
               ("q_gain_row", [1, H0]), ("k_gain_row", [1, HKV]), ("tri_neg", [128, 128]),
               ("fc_logits_row", [1, 8]), ("pj_logits_row", [1, 8]),
               ("fc_mixB", [BANK, 32, 128, D]), ("pj_mixTB", [BANK, 32, 128, D]),
               ("fc_s1f", [BANK * D, BLK]), ("fc_s2f", [BANK * HID, BLK]),
               ("pj_s1f", [BANK * HID, BLK]), ("pj_s2f", [BANK * D, BLK]),
               ("W1", [D, D]),
               ("AbigR", [4, 512, 512]), ("BbigR", [4, 512, 512]),
               ("AbigI", [4, 512, 512]), ("BbigI", [4, 512, 512]),
               ("bias_ri_row", [1, D])]
        for nm, shape in ext:
            io[nm] = nc.dram_tensor(nm, shape, F32, kind="ExternalInput").ap()
        for nm, shape in [("rsc", [512, NTOK]), ("ksc", [512, NTOK]), ("wsc", [512, NTOK]),
                          ("vsc", [NTOK, D]), ("gsc", [NTOK, D]),
                          ("s1", [NTOK, D]), ("xa", [NTOK, D]), ("s2", [NTOK, D]),
                          ("s3", [NTOK, D]), ("fc_sel", [2 * 4096, D]),
                          ("pj_sel", [2 * 4096, D]), ("rs_scr", [8, 128])]:
            io[nm] = nc.dram_tensor(nm, shape, F32).ap()
        io["out"] = nc.dram_tensor("out", [NTOK, D], F32, kind="ExternalOutput").ap()

        block = ctx.enter_context(nc.Block())
        h = H(nc, ctx, block)
        phases = [("A_", lambda: emit_p0a(h, io, "A")),
                  ("B_", lambda: emit_p0a(h, io, "B")),
                  ("C_", lambda: emit_p0b(h, io)),
                  ("D_", lambda: emit_p1(h, io)),
                  ("E_", lambda: emit_p2(h, io)),
                  ("F_", lambda: emit_p3(h, io)),
                  ("G_", lambda: emit_p4(h, io))]
        outer_ctx = h.ctx
        for pfx, fn in phases:
            with ExitStack() as pctx:
                h.ctx = pctx
                h.pfx = pfx
                fn()
            h.ctx = outer_ctx
        h.pfx = ""
    return nc


_BUILD_CACHE = {}


def _get(name, builder):
    if name not in _BUILD_CACHE:
        _BUILD_CACHE[name] = builder()
    return _BUILD_CACHE[name]


def _kernel_unfused(**inputs):
    g = _prep(inputs)
    x0 = np.asarray(inputs["L0_attn_x"], dtype=np.float32)
    xs = [np.ascontiguousarray(x0[2 * c:2 * c + 2].reshape(NTOK, D)) for c in range(NCORES)]
    cores = list(range(NCORES))

    def run(nc, per_core):
        res = run_bass_kernel_spmd(nc, per_core, core_ids=cores)
        return res.results

    base = {"wx1T": g["wx1T"], "w2nT": g["w2nT"], "mu_x": g["mu_x"],
            "x_biasT": g["x_biasT"], "ident": g["ident"]}
    imA = [dict(base, x0=xs[c], wrT=g["wrT"], wkT=g["wkT"], ww1T=g["ww1T"],
                ww2T=g["ww2T"], bw2=g["bw2"]) for c in cores]
    rA = run(_get("p0aA", lambda: build_p0a("A")), imA)
    imB = [dict(base, x0=xs[c], wvT=g["wvT"], wgT=g["wgT"]) for c in cores]
    rB = run(_get("p0aB", lambda: build_p0a("B")), imB)
    imC = [{"x0": xs[c], "woT": g["woT"], "bonusT": g["bonusT"], "ident": g["ident"],
            "mask64": g["mask64"], "ln_w": g["ln_w"], "ln_b": g["ln_b"],
            "rsc": rA[c]["rsc"], "ksc": rA[c]["ksc"], "wsc": rA[c]["wsc"],
            "vsc": rB[c]["vsc"], "gsc": rB[c]["gsc"]} for c in cores]
    rC = run(_get("p0b", lambda: build_p0b(False)), imC)
    imD = [{"s1": rC[c]["s1"], "wqT2": g["wqT2"], "wkT2": g["wkT2"], "wvT2": g["wvT2"],
            "woT2": g["woT2"], "rope_cos": g["rope_cos"], "rope_sin": g["rope_sin"],
            "q_gain_row": g["q_gain_row"], "k_gain_row": g["k_gain_row"],
            "ident": g["ident"], "tri_neg": g["tri_neg"]} for c in cores]
    rD = run(_get("p1", build_p1), imD)
    imE = [{"xa": rD[c]["xa"], "s1": rC[c]["s1"], "ident": g["ident"], "fc_logits_row": g["fc_logits_row"],
            "pj_logits_row": g["pj_logits_row"], "fc_mixB": g["fc_mixB"],
            "pj_mixTB": g["pj_mixTB"], "fc_s1f": g["fc_s1f"], "fc_s2f": g["fc_s2f"],
            "pj_s1f": g["pj_s1f"], "pj_s2f": g["pj_s2f"]} for c in cores]
    rE = run(_get("p2", build_p2), imE)
    W1 = np.ascontiguousarray(np.asarray(inputs["L1_attn_W"], dtype=np.float32))
    imF = [{"s2": rE[c]["s2"], "W1": W1, "ident": g["ident"]} for c in cores]
    rF = run(_get("p3", build_p3), imF)
    imG = [{"s3": rF[c]["s3"], "AbigR": g["AbigR"], "BbigR": g["BbigR"],
            "AbigI": g["AbigI"], "BbigI": g["BbigI"],
            "bias_ri_row": g["bias_ri_row"], "ident": g["ident"]} for c in cores]
    rG = run(_get("p4", build_p4), imG)

    out = np.stack([np.asarray(rG[c]["out"]) for c in cores])
    return out.reshape(16, T, D).astype(np.float32)


def kernel(**inputs):
    g = _prep(inputs)
    x0 = np.asarray(inputs["L0_attn_x"], dtype=np.float32)
    xs = [np.ascontiguousarray(x0[2 * c:2 * c + 2].reshape(NTOK, D)) for c in range(NCORES)]
    W1 = np.ascontiguousarray(np.asarray(inputs["L1_attn_W"], dtype=np.float32))
    nc = _get("fused", build_fused)
    names = ["wx1T", "w2nT", "mu_x", "x_biasT", "ident", "wrT", "wkT", "ww1T",
             "ww2T", "bw2", "wvT", "wgT", "woT", "bonusT", "mask64", "ln_w", "ln_b",
             "wqT2", "wkT2", "wvT2", "woT2", "rope_cos", "rope_sin",
             "q_gain_row", "k_gain_row", "tri_neg", "fc_logits_row", "pj_logits_row",
             "fc_mixB", "pj_mixTB", "fc_s1f", "fc_s2f", "pj_s1f", "pj_s2f",
             "AbigR", "BbigR", "AbigI", "BbigI", "bias_ri_row"]
    base = {nm: g[nm] for nm in names}
    base["W1"] = W1
    in_maps = [dict(base, x0=xs[c]) for c in range(NCORES)]
    res = run_bass_kernel_spmd(nc, in_maps, core_ids=list(range(NCORES)))
    out = np.stack([np.asarray(res.results[c]["out"]) for c in range(NCORES)])
    return out.reshape(16, T, D).astype(np.float32)


# version 11 (final)
# speedup vs baseline: 1.1507x; 1.0421x over previous
import numpy as np
from contextlib import ExitStack

import concourse.bass as bass
import concourse.mybir as mybir
from concourse.bass_utils import run_bass_kernel_spmd

F32 = mybir.dt.float32
BF16 = mybir.dt.bfloat16
I32 = mybir.dt.int32
U32 = mybir.dt.uint32
AF = mybir.ActivationFunctionType
ALU = mybir.AluOpType
AX = mybir.AxisListType

NCORES = 8
T = 2048
NTOK = 2 * T
D = 1024
TT = 512
NTILE = NTOK // TT
NH, DK, DV, CH = 4, 128, 256, 64
H0, HKV, HD = 16, 4, 64
HID, BLK, BANK = 4096, 64, 8



def _prep(inputs):
    g = {}
    c = np.ascontiguousarray
    g["wx1T"] = c(inputs["L0_attn_Wx1"].T)
    g["w2nT"] = c(np.transpose(np.reshape(np.asarray(inputs["L0_attn_Wx2"]), (D, 5, 32)), (1, 2, 0)))
    g["mu_x"] = c(np.asarray(inputs["L0_attn_mu_x"]).reshape(D, 1))
    g["x_biasT"] = c(np.asarray(inputs["L0_attn_x_bias"]).T)
    g["wrT"] = c(inputs["L0_attn_Wr"].T)
    g["wkT"] = c(inputs["L0_attn_Wk"].T)
    g["wvT"] = c(inputs["L0_attn_Wv"].T)
    g["wgT"] = c(inputs["L0_attn_Wg"].T)
    g["ww1T"] = c(inputs["L0_attn_Ww1"].T)
    g["ww2T"] = c(inputs["L0_attn_Ww2"].T)
    g["bw2"] = c(np.asarray(inputs["L0_attn_bw2"]).reshape(512, 1))
    g["bonusT"] = c(np.asarray(inputs["L0_attn_bonus"]).T)
    g["ln_w"] = c(np.asarray(inputs["L0_attn_ln_w"]).reshape(1, 256))
    g["ln_b"] = c(np.asarray(inputs["L0_attn_ln_b"]).reshape(1, 256))
    g["woT"] = c(inputs["L0_attn_Wo"].T)
    P2H = [0, 4, 1, 5, 2, 6, 3, 7, 8, 12, 9, 13, 10, 14, 11, 15]
    wq = np.asarray(inputs["L0_ffn_wq"])
    wqT = wq.T.reshape(D, H0, HD)
    g["wqT2"] = c(wqT[:, P2H, :].reshape(D, D))
    g["wkT2"] = c(inputs["L0_ffn_wk"].T)
    g["wvT2"] = c(inputs["L0_ffn_wv"].T)
    woT = np.asarray(inputs["L0_ffn_wo"]).T.reshape(H0, HD, D)
    g["woT2"] = c(woT[P2H].reshape(D, D))
    g["q_gain_row"] = c(np.asarray(inputs["L0_ffn_q_gain"])[P2H].reshape(1, H0))
    g["k_gain_row"] = c(np.asarray(inputs["L0_ffn_k_gain"]).reshape(1, HKV))
    tpos = np.arange(T, dtype=np.float64)
    inv = 1.0 / (10000.0 ** (np.arange(0, HD, 2, dtype=np.float64) / HD))
    fr = np.outer(tpos, inv)
    g["rope_cos"] = c(np.cos(fr).astype(np.float32))
    g["rope_sin"] = c(np.sin(fr).astype(np.float32))
    g["tri_neg"] = c(np.triu(np.full((128, 128), -1e30, np.float32), 1))
    def mix_perm(n_in):
        idx = np.empty(n_in * BLK, np.int64)
        out = []
        for e in range(BANK):
            for gg in range(n_in):
                for cc in range(BLK):
                    idx[gg * BLK + cc] = cc * n_in + (gg + e) % n_in
            out.append(idx.copy())
        return out
    fcp = mix_perm(16)
    pjp = mix_perm(64)
    fc_mix = np.asarray(inputs["L0_ffn_fc_mix"])
    pj_mix = np.asarray(inputs["L0_ffn_pj_mix"])
    g["fc_mixB"] = c(np.stack([fc_mix[e][:, fcp[e]].reshape(32, 128, D) for e in range(BANK)]))
    g["pj_mixTB"] = c(np.stack([np.ascontiguousarray(pj_mix[e][:, pjp[e]].T).reshape(32, 128, D)
                                for e in range(BANK)]))
    g["fc_s1f"] = c(np.asarray(inputs["L0_ffn_fc_s1"]).reshape(BANK * D, BLK))
    g["fc_s2f"] = c(np.asarray(inputs["L0_ffn_fc_s2"]).reshape(BANK * HID, BLK))
    g["pj_s1f"] = c(np.asarray(inputs["L0_ffn_pj_s1"]).reshape(BANK * HID, BLK))
    g["pj_s2f"] = c(np.asarray(inputs["L0_ffn_pj_s2"]).reshape(BANK * D, BLK))
    g["fc_logits_row"] = c(np.asarray(inputs["L0_ffn_fc_logits"]).reshape(1, BANK))
    g["pj_logits_row"] = c(np.asarray(inputs["L0_ffn_pj_logits"]).reshape(1, BANK))
    for part in ("R", "I"):
        G1 = np.asarray(inputs[f"L1_ffn_G1_{part}"])
        G2 = np.asarray(inputs[f"L1_ffn_G2_{part}"])
        Ab = np.stack([np.broadcast_to(G1[0, :, cc, :].T[:, None, :, None],
                                       (32, 16, 32, 16)).reshape(512, 512) for cc in range(4)])
        Bb = np.stack([np.broadcast_to(G2[cc, :, 0, :].T[None, :, None, :],
                                       (32, 16, 32, 16)).reshape(512, 512) for cc in range(4)])
        g[f"Abig{part}"] = c(Ab)
        g[f"Bbig{part}"] = c(Bb)
    g["bias_ri_row"] = c(np.concatenate([np.asarray(inputs["L1_ffn_bias_real"]),
                                         np.asarray(inputs["L1_ffn_bias_imag"])]).reshape(1, D))
    g["ident"] = np.eye(128, dtype=np.float32)
    g["mask64"] = c(np.triu(np.ones((CH, CH), np.float32), 1))
    for k in g:
        g[k] = np.asarray(g[k], dtype=np.float32)
    return g



class Sem:
    def __init__(self, ctx, nc, name):
        self.h = ctx.enter_context(nc.semaphore(name))
        self.n = 0

    def bump(self, k=1):
        self.n += k
        return self.n


ENGS = ("sync", "tensor", "vector", "scalar", "gpsimd")


class H:
    def __init__(self, nc, ctx, block):
        self.nc = nc
        self.ctx = ctx
        self.block = block
        self.pfx = ""
        self.ld = Sem(ctx, nc, "ld")
        self.st = Sem(ctx, nc, "st")
        self.gd = Sem(ctx, nc, "gd")
        self.esem = {e: Sem(ctx, nc, "es_" + e) for e in ENGS}
        self.waited = {e: {} for e in ENGS}

    def tok(self):
        t = {e: self.esem[e].n for e in ENGS}
        t["ld"] = self.ld.n
        t["st"] = self.st.n
        t["gd"] = self.gd.n
        return t

    def sb(self, name, shape, dt=F32):
        return self.ctx.enter_context(self.nc.sbuf_tensor("s_" + self.pfx + name, shape, dt))

    def ps(self, name, shape, dt=F32):
        return self.ctx.enter_context(self.nc.psum_tensor("p_" + self.pfx + name, shape, dt))


def ST(h, eng, body, deps=None):
    if deps is None:
        tgt = h.tok()
        if eng != "sync":
            tgt.pop("st", None)
    else:
        tgt = {}
        for d in deps:
            for k, v in d.items():
                if v > tgt.get(k, 0):
                    tgt[k] = v
    w = h.waited[eng]
    sems = dict(h.esem)
    sems["ld"] = h.ld
    sems["st"] = h.st
    sems["gd"] = h.gd
    holder = {}

    def _f(e):
        for k, v in tgt.items():
            if v <= 0:
                continue
            if v > w.get(k, 0):
                e.wait_ge(sems[k].h, v)
                w[k] = v
        holder["last"] = body(e)

    getattr(h.block, eng)(_f)
    last = holder.get("last")
    if last is not None:
        last.then_inc(h.esem[eng].h, 1)
        h.esem[eng].bump()
    return h.tok()


def tm_to_fm(h, tm_tiles, fm, pA, pB, ident, nchunk=8, fm_stride=TT, sub0=0):
    for s in range(len(tm_tiles)):
        def tbody(t, s=s):
            last = None
            for c in range(nchunk):
                bank = pA if c < 4 else pB
                last = t.transpose(bank[:, (c % 4) * 128:(c % 4) * 128 + 128],
                                   tm_tiles[s][:, c * 128:(c + 1) * 128], ident[:])
            return last
        ST(h, "tensor", tbody)

        def vbody(v, s=s):
            if fm_stride == 128 and nchunk == 8:
                col0 = (sub0 + s) * 128
                v.tensor_copy(out=fm[:, col0:col0 + 512], in_=pA[:, 0:512])
                return v.tensor_copy(out=fm[:, col0 + 512:col0 + 1024], in_=pB[:, 0:512])
            last = None
            for c in range(nchunk):
                bank = pA if c < 4 else pB
                col = c * fm_stride + (sub0 + s) * 128
                last = v.tensor_copy(out=fm[:, col:col + 128],
                                     in_=bank[:, (c % 4) * 128:(c % 4) * 128 + 128])
            return last
        ST(h, "vector", vbody)


def emit_ln(h, xres, out_ap, trash, nfeat=1024, eps=1e-5, stats=None, dep=None):
    s_ = stats
    dd = None
    if dep is not None:
        dd = [dep] + ([h._ln_last] if getattr(h, "_ln_last", None) else [])
    t1 = ST(h, "vector", lambda v: v.tensor_scalar(out=trash, in0=xres, scalar1=-1.0 / nfeat,
                                                   scalar2=None, op0=ALU.mult, op1=ALU.add,
                                                   accum_out=s_[:, 1:2]), deps=dd)
    t2 = ST(h, "scalar", lambda sc: sc.activation(out=trash, in_=xres, func=AF.Square,
                                                  bias=s_[:, 1:2], scale=1.0, accum_out=s_[:, 2:3]),
            deps=[t1] if dep is not None else None)
    t3 = ST(h, "vector", lambda v: v.tensor_scalar(out=s_[:, 3:4], in0=s_[:, 2:3],
                                                   scalar1=1.0 / nfeat, scalar2=float(eps),
                                                   op0=ALU.mult, op1=ALU.add),
            deps=[t2] if dep is not None else None)
    t4 = ST(h, "scalar", lambda sc: sc.activation(out=s_[:, 4:5], in_=s_[:, 3:4], func=AF.Sqrt),
            deps=[t3] if dep is not None else None)
    t5 = ST(h, "vector", lambda v: v.reciprocal(out=s_[:, 5:6], in_=s_[:, 4:5]),
            deps=[t4] if dep is not None else None)
    t6 = ST(h, "vector", lambda v: v.tensor_scalar(out=out_ap, in0=xres, scalar1=s_[:, 1:2],
                                                   scalar2=s_[:, 5:6], op0=ALU.add, op1=ALU.mult),
            deps=[t5] if dep is not None else None)
    if dep is not None:
        h._ln_last = t6
    return t6



def emit_p0b(h, io, debug=False):
    woT = h.sb("b_woT", [128, 8 * 1024])
    bonusT = h.sb("b_bonusT", [128, 4])
    ident = h.sb("b_ident", [128, 128])
    mask64 = h.sb("b_mask64", [64, 64])
    ones1 = h.sb("b_ones1", [1, 128])
    ones128 = h.sb("b_ones128", [128, 1])
    lnrow = h.sb("b_lnrow", [1, 512])
    lnw_bc = h.sb("b_lnwbc", [128, 256])
    lnb_bc = h.sb("b_lnbbc", [128, 256])

    rfm = h.sb("b_rfm", [128, 4 * TT])
    kfm = h.sb("b_kfm", [128, 4 * TT])
    wfm = h.sb("b_wfm", [128, 4 * TT])
    x0tm = [h.sb(f"b_x0tm{s}", [128, D]) for s in range(4)]
    vch = h.sb("b_vch", [64, D])
    gch = h.sb("b_gch", [64, D])
    Sst = [h.sb(f"b_S{hh}", [128, DV]) for hh in range(NH)]
    otm = h.sb("b_otm", [128, 4 * D])
    ofm = h.sb("b_ofm", [128, 8 * TT])
    ewb4_ = [h.sb("b_ewb4a", [128, 4 * CH]), h.sb("b_ewb4b", [128, 4 * CH])]
    cumb4_ = [h.sb("b_cumb4a", [128, 4 * CH]), h.sb("b_cumb4b", [128, 4 * CH])]
    aexp4_ = [h.sb("b_aexp4a", [128, 4 * CH]), h.sb("b_aexp4b", [128, 4 * CH])]
    bexp4_ = [h.sb("b_bexp4a", [128, 4 * CH]), h.sb("b_bexp4b", [128, 4 * CH])]
    pc4_ = [h.sb("b_pc4a", [128, 4]), h.sb("b_pc4b", [128, 4])]
    rt4_ = [h.sb("b_rt4a", [128, 4 * CH]), h.sb("b_rt4b", [128, 4 * CH])]
    kt4_ = [h.sb("b_kt4a", [128, 4 * CH]), h.sb("b_kt4b", [128, 4 * CH])]
    rbk4_ = [h.sb("b_rbk4a", [128, 4 * CH]), h.sb("b_rbk4b", [128, 4 * CH])]
    vch2 = h.sb("b_vch2", [64, D])
    gch2 = h.sb("b_gch2", [64, D])
    msb4 = h.sb("b_msb4", [64, 4 * CH])
    ktT4 = h.sb("b_ktT4", [64, 4 * 128])
    odb4 = h.sb("b_odb4", [64, 4 * DV])
    osb4 = h.sb("b_osb4", [64, 4 * DV])
    onb4 = h.sb("b_onb4", [64, 4 * DV])
    sig4 = h.sb("b_sig4", [64, 4 * DV])
    tr644 = h.sb("b_tr644", [64, 4 * DV])
    tr128 = h.sb("b_tr128", [128, D])
    xres = h.sb("b_xres", [128, D])
    stats = h.sb("b_stats", [128, 8])
    dbgb = h.sb("b_dbgb", [128, 1024]) if debug else None
    st644 = h.sb("b_st644", [64, 32])
    stmp4 = h.sb("b_stmp4", [128, 4 * DV])

    pPool = h.ps("b_pPool", [128, 4096])
    pT = pPool[:, 0:512]
    pP = pPool[:, 512:1024]
    pW0 = pPool[:, 1024:1536]
    pW1 = pPool[:, 1536:2048]

    def loads(sync):
        for c in range(8):
            sync.dma_start(out=woT[:, c * 1024:(c + 1) * 1024], in_=io["woT"][c * 128:(c + 1) * 128, :]).then_inc(h.ld.h, 16)
            h.ld.bump(16)
        for nm, dst in [("bonusT", bonusT), ("ident", ident), ("mask64", mask64)]:
            sync.dma_start(out=dst[:], in_=io[nm][:]).then_inc(h.ld.h, 16)
            h.ld.bump(16)
        sync.dma_start(out=lnrow[:, 0:256], in_=io["ln_w"][:]).then_inc(h.ld.h, 16)
        sync.dma_start(out=lnrow[:, 256:512], in_=io["ln_b"][:]).then_inc(h.ld.h, 16)
        h.ld.bump(32)
        return None
    ST(h, "sync", loads)

    def init(v):
        v.memset(ones1[:], 1.0)
        v.memset(ones128[:], 1.0)
        return v.engine_nop()
    ST(h, "vector", init)

    def bc(t):
        t.matmul(pT[:, 0:256], ones1[:], lnrow[:, 0:256], start=True, stop=True)
        return t.matmul(pT[:, 256:512], ones1[:], lnrow[:, 256:512], start=True, stop=True)
    ST(h, "tensor", bc)

    def bc2(v):
        v.tensor_copy(out=lnw_bc[:], in_=pT[:, 0:256])
        return v.tensor_copy(out=lnb_bc[:], in_=pT[:, 256:512])
    ST(h, "vector", bc2)

    for seq in range(2):
        def zs(v):
            last = None
            for hh in range(NH):
                last = v.memset(Sst[hh][:], 0.0)
            return last
        ST(h, "vector", zs)

        for it in range(T // TT):
            t0 = seq * T + it * TT

            def ldt(sync, t0=t0):
                for hh in range(4):
                    sync.dma_start(out=rfm[:, hh * TT:(hh + 1) * TT], in_=io["rsc"][hh * 128:(hh + 1) * 128, t0:t0 + TT]).then_inc(h.ld.h, 16)
                    sync.dma_start(out=kfm[:, hh * TT:(hh + 1) * TT], in_=io["ksc"][hh * 128:(hh + 1) * 128, t0:t0 + TT]).then_inc(h.ld.h, 16)
                    sync.dma_start(out=wfm[:, hh * TT:(hh + 1) * TT], in_=io["wsc"][hh * 128:(hh + 1) * 128, t0:t0 + TT]).then_inc(h.ld.h, 16)
                    h.ld.bump(48)
                for s in range(4):
                    sync.dma_start(out=x0tm[s][:], in_=io["x0"][t0 + s * 128:t0 + (s + 1) * 128, :]).then_inc(h.ld.h, 16)
                    h.ld.bump(16)
                return None
            ST(h, "sync", ldt)

            ctok0 = h.tok()
            t_fr, t_bk, t_ldc = {}, {}, {}

            def chunk_front(ch, t0=t0):
                p = ch % 2
                tl = ch * CH
                ewb4, cumb4, aexp4, bexp4 = ewb4_[p], cumb4_[p], aexp4_[p], bexp4_[p]
                pc4, rt4, kt4, rbk4 = pc4_[p], rt4_[p], kt4_[p], rbk4_[p]
                vchp = vch if p == 0 else vch2
                gchp = gch if p == 0 else gch2
                rsl = [rfm[:, hh * TT + tl:hh * TT + tl + CH] for hh in range(NH)]
                ksl = [kfm[:, hh * TT + tl:hh * TT + tl + CH] for hh in range(NH)]
                wsl = [wfm[:, hh * TT + tl:hh * TT + tl + CH] for hh in range(NH)]
                guard = t_bk.get(ch - 2, ctok0)

                def ldc(sync, t0=t0, tl=tl, vchp=vchp, gchp=gchp):
                    sync.dma_start(out=vchp[:], in_=io["vsc"][t0 + tl:t0 + tl + CH, :]).then_inc(h.ld.h, 16)
                    sync.dma_start(out=gchp[:], in_=io["gsc"][t0 + tl:t0 + tl + CH, :]).then_inc(h.ld.h, 16)
                    h.ld.bump(32)
                    return None
                ST(h, "sync", ldc, deps=[guard] + ([t_ldc[ch - 1]] if ch >= 1 else []))
                tld = h.tok()
                t_ldc[ch] = tld

                def s1(sc):
                    last = None
                    for hh in range(NH):
                        last = sc.activation(out=ewb4[:, hh * CH:(hh + 1) * CH], in_=wsl[hh], func=AF.Exp)
                    return last
                ta = ST(h, "scalar", s1, deps=[guard, tld])

                def s2(v):
                    last = None
                    for hh in range(NH):
                        last = v.tensor_tensor_scan(out=cumb4[:, hh * CH:(hh + 1) * CH],
                                                    data0=ewb4[:, hh * CH:(hh + 1) * CH],
                                                    data1=ewb4[:, hh * CH:(hh + 1) * CH],
                                                    initial=0.0, op0=ALU.add, op1=ALU.bypass)
                    return last
                tb = ST(h, "vector", s2, deps=[ta])
                tc = ST(h, "vector", lambda v: v.tensor_tensor(out=aexp4[:], in0=ewb4[:], in1=cumb4[:],
                                                               op=ALU.subtract), deps=[tb])

                def s4(sc):
                    sc.activation(out=aexp4[:], in_=aexp4[:], func=AF.Exp)
                    sc.activation(out=bexp4[:], in_=cumb4[:], func=AF.Exp)
                    last = None
                    for hh in range(NH):
                        last = sc.activation(out=pc4[:, hh:hh + 1],
                                             in_=cumb4[:, hh * CH + CH - 1:hh * CH + CH], func=AF.Exp,
                                             scale=-1.0)
                    return last
                td = ST(h, "scalar", s4, deps=[tc])

                def s5(v):
                    last = None
                    for hh in range(NH):
                        v.tensor_tensor(out=rt4[:, hh * CH:(hh + 1) * CH], in0=rsl[hh],
                                        in1=aexp4[:, hh * CH:(hh + 1) * CH], op=ALU.mult)
                        v.tensor_tensor(out=kt4[:, hh * CH:(hh + 1) * CH], in0=ksl[hh],
                                        in1=bexp4[:, hh * CH:(hh + 1) * CH], op=ALU.mult)
                        last = v.scalar_tensor_tensor(out=rbk4[:, hh * CH:(hh + 1) * CH], in0=rsl[hh],
                                                      scalar=bonusT[:, hh:hh + 1], in1=ksl[hh],
                                                      op0=ALU.mult, op1=ALU.mult)
                    return last
                t_fr[ch] = ST(h, "vector", s5, deps=[td])

            def chunk_back(ch):
                p = ch % 2
                base = (ch % 2) * 64
                sub = ch // 2
                pc4, rt4, kt4, rbk4 = pc4_[p], rt4_[p], kt4_[p], rbk4_[p]
                vchp = vch if p == 0 else vch2
                gchp = gch if p == 0 else gch2
                vsl = [vchp[:, hh * DV:(hh + 1) * DV] for hh in range(NH)]
                pSm_ = [pPool[0:64, hh * 512:(hh + 1) * 512] for hh in range(NH)]
                pO_ = [pPool[(hh % 2) * 64:(hh % 2) * 64 + 64,
                             2048 + (hh // 2) * 256:2048 + (hh // 2) * 256 + 256] for hh in range(NH)]
                pKV_ = [pPool[:, 2560 + hh * 256:2560 + (hh + 1) * 256] for hh in range(NH)]
                prev = t_bk.get(ch - 1, ctok0)

                def s6(t):
                    last = None
                    for hh in range(NH):
                        t.matmul(pSm_[hh][:, 0:64], kt4[:, hh * CH:(hh + 1) * CH],
                                 rt4[:, hh * CH:(hh + 1) * CH], start=True, stop=True)
                        t.matmul(pSm_[hh][:, 64:65], rbk4[:, hh * CH:(hh + 1) * CH],
                                 ones128[:], start=True, stop=True)
                        last = t.transpose(pSm_[hh][:, 128:256], kt4[:, hh * CH:(hh + 1) * CH], ident[:])
                    return last
                t6 = ST(h, "tensor", s6, deps=[t_fr[ch], prev])

                def s7(v):
                    last = None
                    for hh in range(NH):
                        v.tensor_tensor(out=msb4[:, hh * CH:(hh + 1) * CH], in0=pSm_[hh][0:64, 0:64],
                                        in1=mask64[:], op=ALU.mult)
                        v.tensor_copy(out=ktT4[:, hh * 128:(hh + 1) * 128], in_=pSm_[hh][0:64, 128:256])
                        last = v.tensor_scalar(out=odb4[:, hh * DV:(hh + 1) * DV], in0=vsl[hh],
                                               scalar1=pSm_[hh][0:64, 64:65], scalar2=None, op0=ALU.mult)
                    return last
                t7 = ST(h, "vector", s7, deps=[t6])

                def s8(t):
                    last = None
                    for hh in range(NH):
                        t.matmul(pO_[hh], rt4[:, hh * CH:(hh + 1) * CH], Sst[hh][:], start=True, stop=False)
                        t.matmul(pO_[hh], msb4[:, hh * CH:(hh + 1) * CH], vsl[hh], start=False, stop=True)
                        last = t.matmul(pKV_[hh], ktT4[:, hh * 128:(hh + 1) * 128], vsl[hh],
                                        start=True, stop=True)
                    return last
                t8 = ST(h, "tensor", s8, deps=[t7])

                def s9(v):
                    last = None
                    for hh in range(NH):
                        last = v.tensor_tensor(out=osb4[:, hh * DV:(hh + 1) * DV], in0=pO_[hh],
                                               in1=odb4[:, hh * DV:(hh + 1) * DV], op=ALU.add)
                    return last
                t9 = ST(h, "vector", s9, deps=[t8])

                def s10(v):
                    last = None
                    for hh in range(NH):
                        last = v.tensor_scalar(out=tr644[:, hh * DV:(hh + 1) * DV],
                                               in0=osb4[:, hh * DV:(hh + 1) * DV],
                                               scalar1=-1.0 / DV, scalar2=None, op0=ALU.mult, op1=ALU.add,
                                               accum_out=st644[:, hh * 8 + 1:hh * 8 + 2])
                    return last
                t10 = ST(h, "vector", s10, deps=[t9])

                def s11(sc):
                    last = None
                    for hh in range(NH):
                        last = sc.activation(out=tr644[:, hh * DV:(hh + 1) * DV],
                                             in_=osb4[:, hh * DV:(hh + 1) * DV], func=AF.Square,
                                             bias=st644[:, hh * 8 + 1:hh * 8 + 2], scale=1.0,
                                             accum_out=st644[:, hh * 8 + 2:hh * 8 + 3])
                    return last
                t11 = ST(h, "scalar", s11, deps=[t10])

                def s12(v):
                    last = None
                    for hh in range(NH):
                        last = v.tensor_scalar(out=st644[:, hh * 8 + 3:hh * 8 + 4],
                                               in0=st644[:, hh * 8 + 2:hh * 8 + 3],
                                               scalar1=1.0 / DV, scalar2=1e-5, op0=ALU.mult, op1=ALU.add)
                    return last
                t12 = ST(h, "vector", s12, deps=[t11])

                def s13(sc, gchp=gchp):
                    sc.activation(out=sig4[:], in_=gchp[0:64, :], func=AF.Exp, scale=-1.0)
                    last = None
                    for hh in range(NH):
                        last = sc.activation(out=st644[:, hh * 8 + 4:hh * 8 + 5],
                                             in_=st644[:, hh * 8 + 3:hh * 8 + 4], func=AF.Sqrt)
                    return last
                t13 = ST(h, "scalar", s13, deps=[t12])

                def s14(v):
                    v.tensor_scalar(out=sig4[:], in0=sig4[:], scalar1=1.0, scalar2=None, op0=ALU.add)
                    last = None
                    for hh in range(NH):
                        last = v.reciprocal(out=st644[:, hh * 8 + 5:hh * 8 + 6],
                                            in_=st644[:, hh * 8 + 4:hh * 8 + 5])
                    return last
                t14 = ST(h, "vector", s14, deps=[t13])

                def s15(v):
                    v.reciprocal(out=sig4[:], in_=sig4[:])
                    last = None
                    for hh in range(NH):
                        last = v.tensor_scalar(out=onb4[:, hh * DV:(hh + 1) * DV],
                                               in0=osb4[:, hh * DV:(hh + 1) * DV],
                                               scalar1=st644[:, hh * 8 + 1:hh * 8 + 2],
                                               scalar2=st644[:, hh * 8 + 5:hh * 8 + 6],
                                               op0=ALU.add, op1=ALU.mult)
                    return last
                t15 = ST(h, "vector", s15, deps=[t14])

                def s16(v, gchp=gchp):
                    lnw_b4 = lnw_bc[0:64, :].unsqueeze(1).broadcast_to([64, 4, 256])
                    onv = onb4[:].rearrange("p (h d) -> p h d", d=256)
                    v.tensor_tensor(out=onv, in0=onv, in1=lnw_b4, op=ALU.mult)
                    return v.tensor_tensor(out=sig4[:], in0=sig4[:], in1=gchp[0:64, :], op=ALU.mult)
                t16 = ST(h, "vector", s16, deps=[t15])

                t17 = ST(h, "vector", lambda v: v.tensor_tensor(
                    out=onb4[:].rearrange("p (h d) -> p h d", d=256),
                    in0=onb4[:].rearrange("p (h d) -> p h d", d=256),
                    in1=lnb_bc[0:64, :].unsqueeze(1).broadcast_to([64, 4, 256]), op=ALU.add), deps=[t16])

                t18 = ST(h, "vector", lambda v, base=base, sub=sub: v.tensor_tensor(
                    out=otm[base:base + 64, sub * D:(sub + 1) * D],
                    in0=onb4[:], in1=sig4[:], op=ALU.mult), deps=[t17])

                def s19(v):
                    last = None
                    for hh in range(NH):
                        last = v.tensor_tensor(out=stmp4[:, hh * DV:(hh + 1) * DV], in0=Sst[hh][:],
                                               in1=pKV_[hh], op=ALU.add)
                    return last
                t19 = ST(h, "vector", s19, deps=[t8])

                def s20(v, pc4=pc4):
                    last = None
                    for hh in range(NH):
                        last = v.tensor_scalar(out=Sst[hh][:], in0=stmp4[:, hh * DV:(hh + 1) * DV],
                                               scalar1=pc4[:, hh:hh + 1], scalar2=None, op0=ALU.mult)
                    return last
                t20 = ST(h, "vector", s20, deps=[t19])
                t_bk[ch] = t20

            for step in range(TT // CH + 1):
                if step < TT // CH:
                    chunk_front(step)
                if step >= 1:
                    chunk_back(step - 1)

            if debug and seq == 0 and it == 0:
                def dbg2(sync):
                    for s in range(4):
                        sync.dma_start(out=io["d_otm"][s * 128:(s + 1) * 128, :],
                                       in_=otm[:, s * D:(s + 1) * D]).then_inc(h.st.h, 16)
                        h.st.bump(16)
                    return None
                ST(h, "sync", dbg2)
            tm_to_fm(h, [otm[:, s * D:(s + 1) * D] for s in range(4)], ofm, pT, pP, ident, fm_stride=TT)
            for s in range(4):
                def wmm(t, s=s):
                    last = None
                    for half, bank in [(0, pW0), (1, pW1)]:
                        for c in range(8):
                            last = t.matmul(bank[:], ofm[:, c * TT + s * 128:c * TT + (s + 1) * 128],
                                            woT[:, c * 1024 + half * 512:c * 1024 + half * 512 + 512],
                                            start=(c == 0), stop=(c == 7))
                    return last
                ST(h, "tensor", wmm)

                def resid(v, s=s):
                    v.tensor_tensor(out=xres[:, 0:512], in0=pW0[:], in1=x0tm[s][:, 0:512], op=ALU.add)
                    return v.tensor_tensor(out=xres[:, 512:1024], in0=pW1[:], in1=x0tm[s][:, 512:1024], op=ALU.add)
                ST(h, "vector", resid)

                emit_ln(h, xres[:], xres[:], tr128[:], nfeat=D, eps=1e-5, stats=stats)

                def sto(sync, t0=t0, s=s):
                    sync.dma_start(out=io["s1"][t0 + s * 128:t0 + (s + 1) * 128, :], in_=xres[:]).then_inc(h.st.h, 16)
                    h.st.bump(16)
                    sync.wait_ge(h.st.h, h.st.n)
                    return sync.nop()
                ST(h, "sync", sto)

    if debug:
        def dbgout(sync):
            sync.dma_start(out=io["d_buf"][:], in_=dbgb[:]).then_inc(h.st.h, 16)
            h.st.bump(16)
            return None
        ST(h, "sync", dbgout)
    ST(h, "sync", lambda sync: sync.wait_ge(h.st.h, h.st.n) and None)



def emit_p0a(h, io, variant):
    ns = [0, 1, 2] if variant == "A" else [3, 4]

    wx1T = h.sb("wx1T", [128, 8 * 160])
    w2nT = h.sb("w2nT", [32, 5 * 1024])
    mu_x = h.sb("mu_x", [128, 8])
    x_bias = h.sb("x_bias", [128, 8 * 5])
    ident = h.sb("identP0", [128, 128])
    if variant == "A":
        wrT = h.sb("wrT", [128, 8 * 512])
        wkT = h.sb("wkT", [128, 8 * 512])
        ww1T = h.sb("ww1T", [128, 8 * 64])
        ww2T = h.sb("ww2T", [64, 512])
        bw2 = h.sb("bw2", [128, 4])
    else:
        wvT = h.sb("wvT", [128, 8 * 1024])
        wgT = h.sb("wgT", [128, 8 * 1024])

    xtm = [h.sb(f"xtm{s}", [128, D]) for s in range(4)]
    xfm = h.sb("xfm", [128, 8 * TT])
    delta = h.sb("delta", [128, 8 * TT])
    stash = h.sb("stash", [128, 8])
    xmod = h.sb("xmod", [128, 8 * TT])
    musb = h.sb("musb", [128, TT])
    musbB = h.sb("musbB", [128, TT])
    xp = h.sb("xp", [32, 5 * TT])
    ubuf = h.sb("ubuf", [64, TT])
    ofm = h.sb("ofm", [128, 4 * 1024])

    pA = h.ps("pA", [128, 512])
    pB = h.ps("pB", [128, 512])
    pX = h.ps("pX", [32, TT])
    pM = h.ps("pM", [128, TT])
    pM2 = h.ps("pM2", [128, TT])
    pU = h.ps("pU", [64, TT])

    def loads(sync):
        last = None
        for c in range(8):
            last = sync.dma_start(out=wx1T[:, c * 160:(c + 1) * 160], in_=io["wx1T"][c * 128:(c + 1) * 128, :])
            last.then_inc(h.ld.h, 16); h.ld.bump(16)
        for n_ in range(5):
            last = sync.dma_start(out=w2nT[:, n_ * 1024:(n_ + 1) * 1024], in_=io["w2nT"][n_, :, :])
            last.then_inc(h.ld.h, 16); h.ld.bump(16)
        for c in range(8):
            last = sync.dma_start(out=mu_x[:, c:c + 1], in_=io["mu_x"][c * 128:(c + 1) * 128, :])
            last.then_inc(h.ld.h, 16); h.ld.bump(16)
            last = sync.dma_start(out=x_bias[:, c * 5:(c + 1) * 5], in_=io["x_biasT"][c * 128:(c + 1) * 128, :])
            last.then_inc(h.ld.h, 16); h.ld.bump(16)
        last = sync.dma_start(out=ident[:], in_=io["ident"][:])
        last.then_inc(h.ld.h, 16); h.ld.bump(16)
        if variant == "A":
            for c in range(8):
                for dst, src, wdt in [(wrT, io["wrT"], 512), (wkT, io["wkT"], 512), (ww1T, io["ww1T"], 64)]:
                    last = sync.dma_start(out=dst[:, c * wdt:(c + 1) * wdt], in_=src[c * 128:(c + 1) * 128, :])
                    last.then_inc(h.ld.h, 16); h.ld.bump(16)
            last = sync.dma_start(out=ww2T[:], in_=io["ww2T"][:])
            last.then_inc(h.ld.h, 16); h.ld.bump(16)
            for o in range(4):
                last = sync.dma_start(out=bw2[:, o:o + 1], in_=io["bw2"][o * 128:(o + 1) * 128, :])
                last.then_inc(h.ld.h, 16); h.ld.bump(16)
        else:
            for c in range(8):
                for dst, src in [(wvT, io["wvT"]), (wgT, io["wgT"])]:
                    last = sync.dma_start(out=dst[:, c * 1024:(c + 1) * 1024], in_=src[c * 128:(c + 1) * 128, :])
                    last.then_inc(h.ld.h, 16); h.ld.bump(16)
        return None
    ST(h, "sync", loads)
    ST(h, "vector", lambda v: v.memset(stash[:], 0.0))

    for i in range(NTILE):
        t0 = i * TT

        def ldx(sync, t0=t0):
            for s in range(4):
                sync.dma_start(out=xtm[s][:], in_=io["x0"][t0 + s * 128:t0 + (s + 1) * 128, :]).then_inc(h.ld.h, 16)
                h.ld.bump(16)
            return None
        ST(h, "sync", ldx)

        tm_to_fm(h, xtm, xfm, pA, pB, ident)

        def dstage1(v, i=i):
            last = None
            for c in range(8):
                xc = xfm[:, c * TT:(c + 1) * TT]
                dc = delta[:, c * TT:(c + 1) * TT]
                v.tensor_tensor(out=dc[:, 1:TT], in0=xc[:, 0:TT - 1], in1=xc[:, 1:TT], op=ALU.subtract)
                if i % (T // TT) == 0:
                    last = v.tensor_scalar(out=dc[:, 0:1], in0=xc[:, 0:1], scalar1=-1.0, scalar2=None, op0=ALU.mult)
                else:
                    last = v.tensor_tensor(out=dc[:, 0:1], in0=stash[:, c:c + 1], in1=xc[:, 0:1], op=ALU.subtract)
            return last
        ST(h, "vector", dstage1)

        def dstage2(v):
            last = None
            for c in range(8):
                last = v.tensor_copy(out=stash[:, c:c + 1], in_=xfm[:, c * TT + TT - 1:c * TT + TT])
            return last
        ST(h, "vector", dstage2)

        def dstage3(v):
            last = None
            for c in range(8):
                last = v.scalar_tensor_tensor(out=xmod[:, c * TT:(c + 1) * TT], in0=delta[:, c * TT:(c + 1) * TT],
                                              scalar=mu_x[:, c:c + 1], in1=xfm[:, c * TT:(c + 1) * TT],
                                              op0=ALU.mult, op1=ALU.add)
            return last
        ST(h, "vector", dstage3)

        base0 = h.tok()
        t_xp, t_tanh = {}, {}
        prev_n = None
        ofm_guard = base0
        for idx_, n_ in enumerate(ns):
            def xpmm(t, n_=n_):
                last = None
                for c in range(8):
                    last = t.matmul(pX[:], wx1T[:, c * 160 + n_ * 32:c * 160 + (n_ + 1) * 32],
                                    xmod[:, c * TT:(c + 1) * TT], start=(c == 0), stop=(c == 7))
                return last
            dd = [base0] + ([t_tanh[ns[idx_ - 1]]] if idx_ > 0 else [])
            t_xp[n_] = ST(h, "tensor", xpmm, deps=dd)
            t_tanh[n_] = ST(h, "scalar", lambda s, n_=n_: s.activation(
                out=xp[:, n_ * TT:(n_ + 1) * TT], in_=pX[:], func=AF.Tanh), deps=[t_xp[n_]])
        xp_done = t_xp[ns[-1]]

        for n_ in ns:
            mb = [pM, pM2]
            mub = [musb, musbB]
            tmm, tst, txm = {}, {}, {}
            xmod_guard = prev_n if prev_n is not None else xp_done

            def em_musmm(c, n_=n_):
                deps = [t_tanh[n_]] + ([tst[c - 2]] if c >= 2 else [xmod_guard])
                tmm[c] = ST(h, "tensor", lambda t, n_=n_, c=c: t.matmul(
                    mb[c % 2][:], w2nT[:, n_ * 1024 + c * 128:n_ * 1024 + (c + 1) * 128],
                    xp[:, n_ * TT:(n_ + 1) * TT], start=True, stop=True), deps=deps)

            def em_stt(c, n_=n_):
                deps = [tmm[c]] + ([txm[c - 2]] if c >= 2 else [])
                tst[c] = ST(h, "vector", lambda v, n_=n_, c=c: v.scalar_tensor_tensor(
                    out=mub[c % 2][:], in0=mb[c % 2][:], scalar=x_bias[:, c * 5 + n_:c * 5 + n_ + 1],
                    in1=delta[:, c * TT:(c + 1) * TT], op0=ALU.add, op1=ALU.mult), deps=deps)
                deps2 = [tst[c], xmod_guard]
                txm[c] = ST(h, "vector", lambda v, c=c: v.tensor_tensor(
                    out=xmod[:, c * TT:(c + 1) * TT], in0=mub[c % 2][:],
                    in1=xfm[:, c * TT:(c + 1) * TT], op=ALU.add), deps=deps2)

            for c in range(9):
                if c < 8:
                    em_musmm(c)
                if c >= 1:
                    em_stt(c - 1)
            xm_all = txm[7]

            if variant == "A" and n_ in (0, 2):
                wT = wrT if n_ == 0 else wkT
                dst = io["rsc"] if n_ == 0 else io["ksc"]
                pb = [pA, pB]
                tpm, tpe = {}, {}

                def em_pmm(o, wT=wT):
                    def pmm(t, o=o, wT=wT):
                        last = None
                        for c in range(8):
                            last = t.matmul(pb[o % 2][:], wT[:, c * 512 + o * 128:c * 512 + (o + 1) * 128],
                                            xmod[:, c * TT:(c + 1) * TT], start=(c == 0), stop=(c == 7))
                        return last
                    deps = [xm_all] + ([tpe[o - 2]] if o >= 2 else [])
                    tpm[o] = ST(h, "tensor", pmm, deps=deps)

                def em_pev(o):
                    tpe[o] = ST(h, "vector", lambda v, o=o: v.tensor_copy(
                        out=ofm[:, o * 1024:o * 1024 + TT], in_=pb[o % 2][:]),
                        deps=[tpm[o], ofm_guard])

                for o in range(5):
                    if o < 4:
                        em_pmm(o)
                    if o >= 1:
                        em_pev(o - 1)
                prev_n = tpe[3]

                def stores(sync, dst=dst, t0=t0):
                    for o in range(4):
                        sync.dma_start(out=dst[o * 128:(o + 1) * 128, t0:t0 + TT],
                                       in_=ofm[:, o * 1024:o * 1024 + TT]).then_inc(h.st.h, 16)
                        h.st.bump(16)
                    sync.wait_ge(h.st.h, h.st.n)
                    return sync.nop()
                ofm_guard = ST(h, "sync", stores, deps=[tpe[3]])
            elif variant == "A":
                def wmm(t):
                    last = None
                    for c in range(8):
                        last = t.matmul(pU[:], ww1T[:, c * 64:(c + 1) * 64],
                                        xmod[:, c * TT:(c + 1) * TT], start=(c == 0), stop=(c == 7))
                    return last
                twm = ST(h, "tensor", wmm, deps=[xm_all])
                ttn = ST(h, "scalar", lambda s: s.activation(out=ubuf[:], in_=pU[:], func=AF.Tanh),
                         deps=[twm])
                tpe3 = None
                bank_guard = prev_n if prev_n is not None else base0
                for o in range(4):
                    tw2 = ST(h, "tensor", lambda t, o=o: t.matmul(
                        pA[:] if o % 2 == 0 else pB[:], ww2T[:, o * 128:(o + 1) * 128],
                        ubuf[:], start=True, stop=True),
                        deps=[ttn, bank_guard] + ([tpe3] if tpe3 is not None and o >= 2 else []))
                    tpe3 = ST(h, "vector", lambda v, o=o: v.tensor_scalar(
                        out=ofm[:, o * 1024:o * 1024 + TT], in0=pA[:] if o % 2 == 0 else pB[:],
                        scalar1=bw2[:, o:o + 1], scalar2=None, op0=ALU.add), deps=[tw2, ofm_guard])
                prev_n = tpe3

                def storew(sync, t0=t0):
                    for o in range(4):
                        sync.dma_start(out=io["wsc"][o * 128:(o + 1) * 128, t0:t0 + TT],
                                       in_=ofm[:, o * 1024:o * 1024 + TT]).then_inc(h.st.h, 16)
                        h.st.bump(16)
                    sync.wait_ge(h.st.h, h.st.n)
                    return sync.nop()
                ofm_guard = ST(h, "sync", storew, deps=[tpe3])
            else:
                wT = wvT if n_ == 3 else wgT
                dst = io["vsc"] if n_ == 3 else io["gsc"]
                pairs = [(pA, pB), (pM, pM2)]
                tvm, tve = {}, {}

                def em_vmm(s, wT=wT):
                    def vmm(t, s=s, wT=wT):
                        last = None
                        for half in range(2):
                            bank = pairs[s % 2][half]
                            for c in range(8):
                                last = t.matmul(bank[:],
                                                xmod[:, c * TT + s * 128:c * TT + (s + 1) * 128],
                                                wT[:, c * 1024 + half * 512:c * 1024 + half * 512 + 512],
                                                start=(c == 0), stop=(c == 7))
                        return last
                    deps = [xm_all] + ([tve[s - 2]] if s >= 2 else [])
                    if prev_n is not None and s < 2:
                        deps.append(prev_n)
                    tvm[s] = ST(h, "tensor", vmm, deps=deps)

                def em_vev(s):
                    def vevac(v, s=s):
                        v.tensor_copy(out=ofm[:, s * 1024:s * 1024 + 512], in_=pairs[s % 2][0][:])
                        return v.tensor_copy(out=ofm[:, s * 1024 + 512:(s + 1) * 1024], in_=pairs[s % 2][1][:])
                    tve[s] = ST(h, "vector", vevac, deps=[tvm[s], ofm_guard])

                for s in range(5):
                    if s < 4:
                        em_vmm(s)
                    if s >= 1:
                        em_vev(s - 1)
                prev_n = tve[3]

                def storev(sync, dst=dst, t0=t0):
                    for s in range(4):
                        sync.dma_start(out=dst[t0 + s * 128:t0 + (s + 1) * 128, :],
                                       in_=ofm[:, s * 1024:(s + 1) * 1024]).then_inc(h.st.h, 16)
                        h.st.bump(16)
                    sync.wait_ge(h.st.h, h.st.n)
                    return sync.nop()
                ofm_guard = ST(h, "sync", storev, deps=[tve[3]])

    ST(h, "sync", lambda sync: sync.wait_ge(h.st.h, h.st.n) and None)



def emit_rms(h, xin, out_ap, trash, stats, nfeat=1024, eps=1e-6):
    s_ = stats
    ST(h, "scalar", lambda sc: sc.activation(out=trash, in_=xin, func=AF.Square, accum_out=s_[:, 2:3]))
    ST(h, "vector", lambda v: v.tensor_scalar(out=s_[:, 3:4], in0=s_[:, 2:3],
                                              scalar1=1.0 / nfeat, scalar2=float(eps), op0=ALU.mult, op1=ALU.add))
    ST(h, "scalar", lambda sc: sc.activation(out=s_[:, 4:5], in_=s_[:, 3:4], func=AF.Sqrt))
    ST(h, "vector", lambda v: v.reciprocal(out=s_[:, 5:6], in_=s_[:, 4:5]))
    if out_ap is not None:
        ST(h, "vector", lambda v: v.tensor_scalar(out=out_ap, in0=xin, scalar1=s_[:, 5:6],
                                                  scalar2=None, op0=ALU.mult))


def emit_p1(h, io):
    wqT = h.sb("q_wqT", [128, 8 * D])
    wkT = h.sb("q_wkT", [128, 8 * 256])
    wvT = h.sb("q_wvT", [128, 8 * 256])
    woT = h.sb("q_woT", [128, 8 * D])
    ident = h.sb("q_ident", [128, 128])
    trineg = h.sb("q_trineg", [128, 128])
    cosf = h.sb("q_cos", [128, 16 * 32])
    sinf = h.sb("q_sin", [128, 16 * 32])
    grow = h.sb("q_grow", [1, 32])
    gq = h.sb("q_gq", [128, H0])
    gk = h.sb("q_gk", [128, HKV])
    ones1 = h.sb("q_ones1", [1, 128])

    x1t = h.sb("q_x1t", [128, D])
    xn = h.sb("q_xn", [128, D])
    xfm = h.sb("q_xfm", [128, 8 * 128])
    qtm = h.sb("q_qtm", [128, D])
    qrp = h.sb("q_qrp", [128, D])
    Ab = h.sb("q_Ab", [128, D])
    Bb = h.sb("q_Bb", [128, D])
    qfm = h.sb("q_qfm", [128, 8 * 128])
    ktm = h.sb("q_ktm", [128, 256])
    krp = h.sb("q_krp", [128, 256])
    sqb = h.sb("q_sqb", [128, D])
    red = h.sb("q_red", [128, H0])
    rstdh = h.sb("q_rstdh", [128, H0])
    kfm = h.sb("q_kfm", [128, 2 * T])
    vseq = h.sb("q_vseq", [128, 16 * 256])
    scb2_ = [h.sb("q_scbA", [128, T]), h.sb("q_scbB", [128, T])]
    ptb2_ = [h.sb("q_ptbA", [128, 16 * 128]), h.sb("q_ptbB", [128, 16 * 128])]
    otm = h.sb("q_otm", [128, D])
    ofm = h.sb("q_ofm", [128, 8 * 128])
    stats = h.sb("q_stats", [128, 8])
    trash = h.sb("q_trash", [128, D])

    pBig = h.ps("q_pBig", [128, 2048])
    pAV = h.ps("q_pAV", [128, 128])
    pW0 = h.ps("q_pW0", [128, 512])
    pW1 = h.ps("q_pW1", [128, 512])

    def loads(sync):
        for c in range(8):
            sync.dma_start(out=wqT[:, c * D:(c + 1) * D], in_=io["wqT2"][c * 128:(c + 1) * 128, :]).then_inc(h.ld.h, 16)
            sync.dma_start(out=wkT[:, c * 256:(c + 1) * 256], in_=io["wkT2"][c * 128:(c + 1) * 128, :]).then_inc(h.ld.h, 16)
            sync.dma_start(out=wvT[:, c * 256:(c + 1) * 256], in_=io["wvT2"][c * 128:(c + 1) * 128, :]).then_inc(h.ld.h, 16)
            sync.dma_start(out=woT[:, c * D:(c + 1) * D], in_=io["woT2"][c * 128:(c + 1) * 128, :]).then_inc(h.ld.h, 16)
            h.ld.bump(64)
        for i in range(16):
            sync.dma_start(out=cosf[:, i * 32:(i + 1) * 32], in_=io["rope_cos"][i * 128:(i + 1) * 128, :]).then_inc(h.ld.h, 16)
            sync.dma_start(out=sinf[:, i * 32:(i + 1) * 32], in_=io["rope_sin"][i * 128:(i + 1) * 128, :]).then_inc(h.ld.h, 16)
            h.ld.bump(32)
        for nm, sl in [("ident", ident[:]), ("tri_neg", trineg[:])]:
            sync.dma_start(out=sl, in_=io[nm][:]).then_inc(h.ld.h, 16)
            h.ld.bump(16)
        sync.dma_start(out=grow[:, 0:16], in_=io["q_gain_row"][:]).then_inc(h.ld.h, 16)
        sync.dma_start(out=grow[:, 16:20], in_=io["k_gain_row"][:]).then_inc(h.ld.h, 16)
        h.ld.bump(32)
        return None
    ST(h, "sync", loads)
    ST(h, "vector", lambda v: v.memset(ones1[:], 1.0))

    def gbc(t):
        t.matmul(pW0[:, 0:16], ones1[:], grow[:, 0:16], start=True, stop=True)
        return t.matmul(pW0[:, 16:20], ones1[:], grow[:, 16:20], start=True, stop=True)
    ST(h, "tensor", gbc)

    def gbc2(v):
        v.tensor_scalar(out=gq[:], in0=pW0[:, 0:16], scalar1=0.125, scalar2=None, op0=ALU.mult)
        return v.tensor_copy(out=gk[:], in_=pW0[:, 16:20])
    ST(h, "vector", gbc2)

    def head_rms_rope(src, dst, nh, gain, ti):
        ST(h, "scalar", lambda sc: sc.activation(out=sqb[:, 0:nh * 64], in_=src, func=AF.Square))
        ST(h, "vector", lambda v: v.tensor_reduce(
            out=red[:, 0:nh], in_=sqb[:, 0:nh * 64].rearrange("p (h d) -> p h d", d=64),
            axis=AX.X, op=ALU.add))
        ST(h, "vector", lambda v: v.tensor_scalar(out=red[:, 0:nh], in0=red[:, 0:nh],
                                                  scalar1=1.0 / 64, scalar2=1e-6, op0=ALU.mult, op1=ALU.add))
        ST(h, "scalar", lambda sc: sc.activation(out=red[:, 0:nh], in_=red[:, 0:nh], func=AF.Sqrt))
        ST(h, "vector", lambda v: v.reciprocal(out=rstdh[:, 0:nh], in_=red[:, 0:nh]))

        def norm(v):
            last = None
            for hh in range(nh):
                last = v.tensor_scalar(out=src[:, hh * 64:(hh + 1) * 64], in0=src[:, hh * 64:(hh + 1) * 64],
                                       scalar1=rstdh[:, hh:hh + 1], scalar2=gain[:, hh:hh + 1],
                                       op0=ALU.mult, op1=ALU.mult)
            return last
        ST(h, "vector", norm)

        cb = cosf[:, ti * 32:(ti + 1) * 32].unsqueeze(1).broadcast_to([128, nh, 32])
        sb_ = sinf[:, ti * 32:(ti + 1) * 32].unsqueeze(1).broadcast_to([128, nh, 32])
        sv = src.rearrange("p (h d) -> p h d", d=64)
        Av = Ab[:, 0:nh * 64].rearrange("p (h d) -> p h d", d=64)
        Bv = Bb[:, 0:nh * 64].rearrange("p (h d) -> p h d", d=64)
        dv = dst.rearrange("p (h d) -> p h d", d=64)

        def ropeAB(v):
            v.tensor_tensor(out=Av[:, :, 0:32], in0=sv[:, :, 0:32], in1=cb, op=ALU.mult)
            v.tensor_tensor(out=Av[:, :, 32:64], in0=sv[:, :, 32:64], in1=cb, op=ALU.mult)
            v.tensor_tensor(out=Bv[:, :, 0:32], in0=sv[:, :, 0:32], in1=sb_, op=ALU.mult)
            return v.tensor_tensor(out=Bv[:, :, 32:64], in0=sv[:, :, 32:64], in1=sb_, op=ALU.mult)
        ST(h, "vector", ropeAB)

        def ropeC(v):
            v.tensor_tensor(out=dv[:, :, 0:32], in0=Av[:, :, 0:32], in1=Bv[:, :, 32:64], op=ALU.add)
            return v.tensor_tensor(out=dv[:, :, 32:64], in0=Av[:, :, 32:64], in1=Bv[:, :, 0:32], op=ALU.subtract)
        ST(h, "vector", ropeC)

    for seq in range(2):
        base_t = seq * T
        for ti in range(16):
            t0 = base_t + ti * 128

            def ldx(sync, t0=t0):
                sync.dma_start(out=x1t[:], in_=io["s1"][t0:t0 + 128, :]).then_inc(h.ld.h, 16)
                h.ld.bump(16)
                return None
            ST(h, "sync", ldx)

            emit_rms(h, x1t[:], xn[:], trash[:], stats)
            tm_to_fm(h, [xn], xfm, pBig[:, 0:512], pBig[:, 512:1024], ident, fm_stride=128, sub0=0)

            def kvmm(t):
                last = None
                for c in range(8):
                    t.matmul(pBig[:, 1024:1280], xfm[:, c * 128:(c + 1) * 128],
                             wkT[:, c * 256:(c + 1) * 256], start=(c == 0), stop=(c == 7))
                for c in range(8):
                    last = t.matmul(pBig[:, 1280:1536], xfm[:, c * 128:(c + 1) * 128],
                                    wvT[:, c * 256:(c + 1) * 256], start=(c == 0), stop=(c == 7))
                return last
            ST(h, "tensor", kvmm)

            def kvevac(v, ti=ti):
                v.tensor_copy(out=ktm[:], in_=pBig[:, 1024:1280])
                return v.tensor_copy(out=vseq[:, ti * 256:(ti + 1) * 256], in_=pBig[:, 1280:1536])
            ST(h, "vector", kvevac)

            head_rms_rope(ktm[:], krp[:], HKV, gk, ti)

            def ktr(t, ti=ti):
                t.transpose(pBig[:, 1024:1152], krp[:, 0:128], ident[:])
                return t.transpose(pBig[:, 1152:1280], krp[:, 128:256], ident[:])
            ST(h, "tensor", ktr)

            def kev(v, ti=ti):
                v.tensor_copy(out=kfm[:, ti * 128:(ti + 1) * 128], in_=pBig[:, 1024:1152])
                return v.tensor_copy(out=kfm[:, T + ti * 128:T + (ti + 1) * 128], in_=pBig[:, 1152:1280])
            ST(h, "vector", kev)

        for qi in range(16):
            t0 = base_t + qi * 128
            klen = (qi + 1) * 128

            def ldq(sync, t0=t0):
                sync.dma_start(out=x1t[:], in_=io["s1"][t0:t0 + 128, :]).then_inc(h.ld.h, 16)
                h.ld.bump(16)
                return None
            ST(h, "sync", ldq)

            emit_rms(h, x1t[:], xn[:], trash[:], stats)
            tm_to_fm(h, [xn], xfm, pBig[:, 0:512], pBig[:, 512:1024], ident, fm_stride=128, sub0=0)

            def qmm(t):
                last = None
                for half, lo in [(0, 1024), (1, 1536)]:
                    for c in range(8):
                        last = t.matmul(pBig[:, lo:lo + 512], xfm[:, c * 128:(c + 1) * 128],
                                        wqT[:, c * D + half * 512:c * D + half * 512 + 512],
                                        start=(c == 0), stop=(c == 7))
                return last
            ST(h, "tensor", qmm)

            def qevac(v):
                v.tensor_copy(out=qtm[:, 0:512], in_=pBig[:, 1024:1536])
                return v.tensor_copy(out=qtm[:, 512:1024], in_=pBig[:, 1536:2048])
            ST(h, "vector", qevac)

            head_rms_rope(qtm[:], qrp[:], H0, gq, qi)

            def qtr(t):
                last = None
                for c in range(8):
                    last = t.transpose(pBig[:, c * 128:c * 128 + 128],
                                       qrp[:, c * 128:(c + 1) * 128], ident[:])
                return last
            ST(h, "tensor", qtr)

            def qev(v):
                v.tensor_copy(out=qfm[:, 0:512], in_=pBig[:, 0:512])
                return v.tensor_copy(out=qfm[:, 512:1024], in_=pBig[:, 512:1024])
            ST(h, "vector", qev)

            P2H = [0, 4, 1, 5, 2, 6, 3, 7, 8, 12, 9, 13, 10, 14, 11, 15]
            tok0 = h.tok()
            nrnd = (klen + 1023) // 1024
            t_se_last, t_exp, t_rcp, t_pe_last, t_av, t_oev = {}, {}, {}, {}, {}, {}

            def front(hh, qi=qi, klen=klen):
                p = hh % 2
                kvh = P2H[hh] // 4
                qbase = (hh % 2) * 64
                kchunk = kvh // 2
                kbase = (kvh % 2) * 64
                q_sl = qfm[qbase:qbase + 64, (hh // 2) * 128:(hh // 2) * 128 + 128]
                pair = pBig[:, p * 1024:(p + 1) * 1024]
                t_prev = None
                for r in range(nrnd):
                    lo = r * 1024
                    w_ = min(1024, klen - lo)

                    def smm(t, q_sl=q_sl, kchunk=kchunk, kbase=kbase, lo=lo, w_=w_, pair=pair):
                        last = None
                        for kb in range(0, w_, 512):
                            ww = min(512, w_ - kb)
                            last = t.matmul(pair[:, kb:kb + ww], q_sl,
                                            kfm[kbase:kbase + 64,
                                                kchunk * T + lo + kb:kchunk * T + lo + kb + ww],
                                            start=True, stop=True)
                        return last
                    deps = [t_prev] if t_prev is not None else (
                        [t_pe_last[hh - 2]] if hh >= 2 else [tok0])
                    tsc = ST(h, "tensor", smm, deps=deps)

                    def sev(v, lo=lo, w_=w_, qi=qi, p=p, pair=pair):
                        last = None
                        dlo = qi * 128
                        for kb in range(0, w_, 512):
                            ww = min(512, w_ - kb)
                            a0 = lo + kb
                            a1 = a0 + ww
                            if dlo >= a0 and dlo < a1:
                                if dlo > a0:
                                    v.tensor_copy(out=scb2_[p][:, a0:dlo], in_=pair[:, kb:kb + dlo - a0])
                                last = v.tensor_tensor(out=scb2_[p][:, dlo:dlo + 128],
                                                       in0=pair[:, dlo - lo:dlo - lo + 128],
                                                       in1=trineg[:], op=ALU.add)
                            else:
                                last = v.tensor_copy(out=scb2_[p][:, a0:a1], in_=pair[:, kb:kb + ww])
                        return last
                    t_prev = ST(h, "vector", sev, deps=[tsc])
                t_se_last[hh] = t_prev
                tmax = ST(h, "vector", lambda v, p=p, klen=klen: v.tensor_reduce(
                    out=stats[:, 0 + p:1 + p], in_=scb2_[p][:, 0:klen], axis=AX.X,
                    op=ALU.max, negate=True), deps=[t_prev])
                t_exp[hh] = ST(h, "scalar", lambda sc, p=p, klen=klen: sc.activation(
                    out=scb2_[p][:, 0:klen], in_=scb2_[p][:, 0:klen], func=AF.Exp,
                    bias=stats[:, 0 + p:1 + p], accum_out=stats[:, 2 + p:3 + p]), deps=[tmax])
                t_rcp[hh] = ST(h, "vector", lambda v, p=p: v.reciprocal(
                    out=stats[:, 4 + p:5 + p], in_=stats[:, 2 + p:3 + p]), deps=[t_exp[hh]])

            def back(hh, qi=qi):
                p = hh % 2
                kvh = P2H[hh] // 4
                pair = pBig[:, p * 1024:(p + 1) * 1024]
                nks = qi + 1
                t_prev = t_exp[hh]
                for r in range(0, nks, 8):
                    nblk = min(8, nks - r)

                    def ptr(t, r=r, nblk=nblk, p=p, pair=pair):
                        last = None
                        for b_ in range(nblk):
                            ks = r + b_
                            last = t.transpose(pair[:, b_ * 128:(b_ + 1) * 128],
                                               scb2_[p][:, ks * 128:(ks + 1) * 128], ident[:])
                        return last
                    ttr = ST(h, "tensor", ptr, deps=[t_prev])

                    def pev(sc, r=r, nblk=nblk, p=p, pair=pair):
                        last = None
                        for b0 in range(0, nblk, 4):
                            w = min(4, nblk - b0) * 128
                            last = sc.activation(out=ptb2_[p][:, (r + b0) * 128:(r + b0) * 128 + w],
                                                 in_=pair[:, b0 * 128:b0 * 128 + w], func=AF.Copy)
                        return last
                    depsv = [ttr] + ([t_av[hh - 2]] if hh >= 2 and r == 0 else [])
                    t_prev = ST(h, "scalar", pev, deps=depsv)
                t_pe_last[hh] = t_prev

                def avmm(t, qi=qi, kvh=kvh, p=p):
                    last = None
                    for ks in range(qi + 1):
                        last = t.matmul(pAV[:, p * 64:(p + 1) * 64],
                                        ptb2_[p][:, ks * 128:(ks + 1) * 128],
                                        vseq[:, ks * 256 + kvh * 64:ks * 256 + (kvh + 1) * 64],
                                        start=(ks == 0), stop=(ks == qi))
                    return last
                depsav = [t_prev, t_rcp[hh]] + ([t_oev[hh - 2]] if hh >= 2 else [])
                t_av[hh] = ST(h, "tensor", avmm, deps=depsav)
                t_oev[hh] = ST(h, "vector", lambda v, hh=hh, p=p: v.tensor_scalar(
                    out=otm[:, hh * 64:(hh + 1) * 64], in0=pAV[:, p * 64:(p + 1) * 64],
                    scalar1=stats[:, 4 + p:5 + p], scalar2=None, op0=ALU.mult), deps=[t_av[hh]])

            for step in range(H0 + 1):
                if step < H0:
                    front(step)
                if step >= 1:
                    back(step - 1)

            tm_to_fm(h, [otm], ofm, pBig[:, 0:512], pBig[:, 512:1024], ident, fm_stride=128, sub0=0)

            def omm(t):
                last = None
                for half, bank in [(0, pW0), (1, pW1)]:
                    for c in range(8):
                        last = t.matmul(bank[:], ofm[:, c * 128:(c + 1) * 128],
                                        woT[:, c * D + half * 512:c * D + half * 512 + 512],
                                        start=(c == 0), stop=(c == 7))
                return last
            ST(h, "tensor", omm)

            def resid(v):
                v.tensor_tensor(out=xn[:, 0:512], in0=pW0[:], in1=x1t[:, 0:512], op=ALU.add)
                return v.tensor_tensor(out=xn[:, 512:1024], in0=pW1[:], in1=x1t[:, 512:1024], op=ALU.add)
            ST(h, "vector", resid)

            def sxa(sync, t0=t0):
                sync.dma_start(out=io["xa"][t0:t0 + 128, :], in_=xn[:]).then_inc(h.st.h, 16)
                h.st.bump(16)
                sync.wait_ge(h.st.h, h.st.n)
                return sync.nop()
            ST(h, "sync", sxa)

    ST(h, "sync", lambda sync: sync.wait_ge(h.st.h, h.st.n) and None)



def emit_p2(h, io):
    ident = h.sb("m_ident", [128, 128])
    ones1 = h.sb("m_ones1", [1, 128])
    iota8 = h.sb("m_iota8", [1, 8], I32)
    iotaf = h.sb("m_iotaf", [1, 8])
    lrow = h.sb("m_lrow", [1, 16])
    tks = h.sb("m_tks", [1, 16])
    coefb = h.sb("m_coefb", [128, 4])
    ebase_i = h.sb("m_ebasei", [1, 4], I32)

    fcs1 = h.sb("m_fcs1", [128, 8 * 64])
    fcs1b = h.sb("m_fcs1b", [128, 8 * 64])
    fcs2 = h.sb("m_fcs2", [128, 32 * 64])
    fcs2b = h.sb("m_fcs2b", [128, 32 * 64])
    pjs1 = h.sb("m_pjs1", [128, 32 * 64])
    pjs1b = h.sb("m_pjs1b", [128, 32 * 64])
    pjs2 = h.sb("m_pjs2", [128, 8 * 64])
    pjs2b = h.sb("m_pjs2b", [128, 8 * 64])

    xat = [h.sb(f"m_xat{s}", [128, D]) for s in range(4)]
    h0tm = h.sb("m_h0tm", [128, 4 * D])
    h0fm = h.sb("m_h0fm", [128, 8 * TT])
    h1 = h.sb("m_h1", [128, 8 * TT], BF16)
    h2 = h.sb("m_h2", [128, 16 * TT])
    t1 = h.sb("m_t1", [128, 16 * TT], BF16)
    pmix = h.sb("m_pmix", [128, 8 * TT])
    gbuf2_ = [h.sb("m_gbufA", [128, 1024]), h.sb("m_gbufB", [128, 1024])]
    gbufb2_ = [h.sb("m_gbufbA", [128, 1024], BF16), h.sb("m_gbufbB", [128, 1024], BF16)]
    lbuf2_ = [h.sb("m_lbufA", [128, 8 * 128], BF16), h.sb("m_lbufB", [128, 8 * 128], BF16)]
    strip2_ = [h.sb("m_stripA", [128, TT]), h.sb("m_stripB", [128, TT])]
    gbuf = gbuf2_[0]
    lbuf = lbuf2_[0]
    strip = strip2_[0]
    stats = h.sb("m_stats", [128, 8])
    trash = gbuf2_[0][:]

    pF = h.ps("m_pF", [128, 2048])
    pMix = h.ps("m_pMix", [128, 512])
    pTr = h.ps("m_pTr", [128, 512])
    pB2 = h.ps("m_pB2", [128, 512])
    p8 = h.ps("m_p8", [128, 512])
    pjBanks = [pF[:, 0:512], pF[:, 512:1024], pF[:, 1024:1536], pF[:, 1536:2048],
               pMix[:], pTr[:], pB2[:], p8[:]]

    rgF0 = h.ctx.enter_context(h.nc.gpsimd.register("rgF0"))
    rgF1 = h.ctx.enter_context(h.nc.gpsimd.register("rgF1"))
    rgP0 = h.ctx.enter_context(h.nc.gpsimd.register("rgP0"))
    rgP1 = h.ctx.enter_context(h.nc.gpsimd.register("rgP1"))
    rgT = h.ctx.enter_context(h.nc.gpsimd.register("rgT"))

    def loads(sync):
        sync.dma_start(out=ident[:], in_=io["ident"][:]).then_inc(h.ld.h, 16)
        sync.dma_start(out=lrow[:, 0:8], in_=io["fc_logits_row"][:]).then_inc(h.ld.h, 16)
        sync.dma_start(out=lrow[:, 8:16], in_=io["pj_logits_row"][:]).then_inc(h.ld.h, 16)
        h.ld.bump(48)
        return None
    ST(h, "sync", loads)

    def init(v):
        v.memset(ones1[:], 1.0)
        return v.memset(tks[:], 0.0)
    ST(h, "vector", init)
    ST(h, "gpsimd", lambda gp: gp.iota(iota8[:], base=0, pattern=[[1, 8]], channel_multiplier=0))
    ST(h, "vector", lambda v: v.tensor_copy(out=iotaf[:], in_=iota8[:]))

    for gi, lo in [(0, 0), (1, 8)]:
        lsl = lrow[:, lo:lo + 8]
        b = gi * 8
        ST(h, "vector", lambda v, lsl=lsl, b=b: v.tensor_reduce(
            out=tks[:, b + 0:b + 1], in_=lsl, axis=AX.X, op=ALU.max))
        ST(h, "vector", lambda v, lsl=lsl, b=b: v.tensor_scalar(
            out=trash[0:1, 0:8], in0=lsl, scalar1=tks[:, b + 0:b + 1], scalar2=None, op0=ALU.is_ge))
        ST(h, "vector", lambda v, b=b: v.tensor_tensor(
            out=trash[0:1, 8:16], in0=trash[0:1, 0:8], in1=iotaf[:], op=ALU.mult))
        ST(h, "vector", lambda v, b=b: v.tensor_reduce(
            out=tks[:, b + 2:b + 3], in_=trash[0:1, 8:16], axis=AX.X, op=ALU.add))
        ST(h, "vector", lambda v, lsl=lsl, b=b: v.scalar_tensor_tensor(
            out=trash[0:1, 0:8], in0=trash[0:1, 0:8], scalar=-1e30, in1=lsl,
            op0=ALU.mult, op1=ALU.add))
        ST(h, "vector", lambda v, b=b: v.tensor_reduce(
            out=tks[:, b + 1:b + 2], in_=trash[0:1, 0:8], axis=AX.X, op=ALU.max))
        ST(h, "vector", lambda v, b=b: v.tensor_scalar(
            out=trash[0:1, 16:24], in0=trash[0:1, 0:8], scalar1=tks[:, b + 1:b + 2],
            scalar2=None, op0=ALU.is_ge))
        ST(h, "vector", lambda v, b=b: v.tensor_tensor(
            out=trash[0:1, 24:32], in0=trash[0:1, 16:24], in1=iotaf[:], op=ALU.mult))
        ST(h, "vector", lambda v, b=b: v.tensor_reduce(
            out=tks[:, b + 3:b + 4], in_=trash[0:1, 24:32], axis=AX.X, op=ALU.add))
        ST(h, "vector", lambda v, b=b: v.tensor_tensor(
            out=tks[:, b + 7:b + 8], in0=tks[:, b + 1:b + 2], in1=tks[:, b + 0:b + 1], op=ALU.subtract))
        ST(h, "scalar", lambda sc, b=b: sc.activation(
            out=tks[:, b + 4:b + 5], in_=tks[:, b + 7:b + 8], func=AF.Exp))
        ST(h, "vector", lambda v, b=b: v.tensor_scalar(
            out=tks[:, b + 5:b + 6], in0=tks[:, b + 4:b + 5], scalar1=1.0, scalar2=None, op0=ALU.add))
        ST(h, "vector", lambda v, b=b: v.reciprocal(out=tks[:, b + 5:b + 6], in_=tks[:, b + 5:b + 6]))
        ST(h, "vector", lambda v, b=b: v.tensor_tensor(
            out=tks[:, b + 6:b + 7], in0=tks[:, b + 4:b + 5], in1=tks[:, b + 5:b + 6], op=ALU.mult))

    def cbc(t):
        t.matmul(pMix[:, 0:1], ones1[:], tks[:, 5:6], start=True, stop=True)
        t.matmul(pMix[:, 1:2], ones1[:], tks[:, 6:7], start=True, stop=True)
        t.matmul(pMix[:, 2:3], ones1[:], tks[:, 13:14], start=True, stop=True)
        return t.matmul(pMix[:, 3:4], ones1[:], tks[:, 14:15], start=True, stop=True)
    ST(h, "tensor", cbc)
    ST(h, "vector", lambda v: v.tensor_copy(out=coefb[:], in_=pMix[:, 0:4]))

    def packi(v):
        v.tensor_copy(out=trash[0:1, 0:1], in_=tks[:, 2:3])
        v.tensor_copy(out=trash[0:1, 1:2], in_=tks[:, 3:4])
        v.tensor_copy(out=trash[0:1, 2:3], in_=tks[:, 10:11])
        return v.tensor_copy(out=trash[0:1, 3:4], in_=tks[:, 11:12])
    ST(h, "vector", packi)
    ST(h, "vector", lambda v: v.tensor_copy(out=ebase_i[:], in_=trash[0:1, 0:4]))

    def gsetup(gp):
        gp.reg_load(rgF0, ebase_i[:1, 0:1])
        gp.reg_load(rgF1, ebase_i[:1, 1:2])
        gp.reg_load(rgP0, ebase_i[:1, 2:3])
        gp.reg_load(rgP1, ebase_i[:1, 3:4])
        for slot, rg, src_, dst in [(0, rgF0, io["fc_mixB"], io["fc_sel"]),
                                    (1, rgF1, io["fc_mixB"], io["fc_sel"]),
                                    (0, rgP0, io["pj_mixTB"], io["pj_sel"]),
                                    (1, rgP1, io["pj_mixTB"], io["pj_sel"])]:
            gp.reg_mov(rgT, 0)
            gp.reg_add(rgT, rgT, rg)
            gp.reg_mul(rgT, rgT, 32 * 128 * D)
            gp.dma_start(out=dst[slot * 4096:(slot + 1) * 4096, :],
                         in_=bass.AP(src_.tensor, rgT, [[32768, 128], [1, 1], [1, 32768]])
                         ).then_inc(h.gd.h, 16)
            h.gd.bump(16)
        for rg, srcs in [(rgF0, [(io["fc_s1f"], fcs1, 8, D), (io["fc_s2f"], fcs2, 32, HID)]),
                         (rgF1, [(io["fc_s1f"], fcs1b, 8, D), (io["fc_s2f"], fcs2b, 32, HID)]),
                         (rgP0, [(io["pj_s1f"], pjs1, 32, HID), (io["pj_s2f"], pjs2, 8, D)]),
                         (rgP1, [(io["pj_s1f"], pjs1b, 32, HID), (io["pj_s2f"], pjs2b, 8, D)])]:
            for src_t, dst_sb, ntile, rows in srcs:
                gp.reg_mov(rgT, 0)
                gp.reg_add(rgT, rgT, rg)
                gp.reg_mul(rgT, rgT, rows * 64)
                gp.dma_start(out=dst_sb[:, 0:ntile * 64],
                             in_=bass.AP(src_t.tensor, rgT,
                                         [[64, 128], [8192, ntile], [1, 64]])
                             ).then_inc(h.gd.h, 16)
                h.gd.bump(16)
        return None
    ST(h, "gpsimd", gsetup)

    for mt in range(NTILE):
        t0 = mt * TT

        def ldxa(sync, t0=t0):
            for s in range(4):
                sync.dma_start(out=xat[s][:], in_=io["xa"][t0 + s * 128:t0 + (s + 1) * 128, :]).then_inc(h.ld.h, 16)
                h.ld.bump(16)
            return None
        ST(h, "sync", ldxa)

        for s in range(4):
            emit_rms(h, xat[s][:], h0tm[:, s * D:(s + 1) * D], trash, stats)
        tm_to_fm(h, [h0tm[:, s * D:(s + 1) * D] for s in range(4)],
                 h0fm, pF[:, 0:512], pF[:, 512:1024], ident, fm_stride=TT)

        def yz(v):
            last = None
            for s in range(4):
                last = v.memset(h0tm[:, s * D:(s + 1) * D], 0.0)
            return last
        ST(h, "vector", yz)

        for half in range(2):
            jlo = half * 16
            for eslot in range(2):
                s1b = fcs1 if eslot == 0 else fcs1b
                s2b = fcs2 if eslot == 0 else fcs2b
                rg = rgF0 if eslot == 0 else rgF1

                for jg in range(2):
                    def bd1mm(t, jg=jg, s1b=s1b):
                        last = None
                        for jj in range(4):
                            ch = jg * 4 + jj
                            for gpar in range(2):
                                g = ch * 2 + gpar
                                base = gpar * 64
                                last = t.matmul(pF[base:base + 64, jj * 512:(jj + 1) * 512],
                                                s1b[base:base + 64, (g // 2) * 64:(g // 2) * 64 + 64],
                                                h0fm[base:base + 64, ch * TT:(ch + 1) * TT],
                                                start=True, stop=True)
                        return last
                    ST(h, "tensor", bd1mm)

                    def bd1ev(v, jg=jg):
                        last = None
                        for jj in range(4):
                            ch = jg * 4 + jj
                            last = v.tensor_copy(out=h1[:, ch * TT:(ch + 1) * TT],
                                                 in_=pF[:, jj * 512:(jj + 1) * 512])
                        return last
                    ST(h, "vector", bd1ev)

                tok0 = h.tok()
                tg, ttr, tev, tmm, tse, tb2, tcm = {}, {}, {}, {}, {}, {}, {}
                trpair = [pF[:, 0:1024], pF[:, 1024:2048]]
                mixb = [pMix, p8]
                b2b = [pTr, pB2]

                def em_gth(jj):
                    p = jj % 2
                    j = jlo + jj

                    def gth(sync, j=j, p=p, eslot=eslot):
                        r0 = eslot * 4096 + j * 128
                        sync.dma_start(out=gbuf2_[p][:], in_=io["fc_sel"][r0:r0 + 128, :]).then_inc(h.ld.h, 16)
                        h.ld.bump(16)
                        return None
                    deps = ([ttr[jj - 2]] if jj >= 2 else [tok0]) + ([tg[jj - 1]] if jj >= 1 else [])
                    ST(h, "sync", gth, deps=deps)
                    tg[jj] = h.tok()

                def em_gtr(jj):
                    p = jj % 2

                    def gtr(t, p=p):
                        last = None
                        for c in range(8):
                            last = t.transpose(trpair[p][:, c * 128:(c + 1) * 128],
                                               gbuf2_[p][:, c * 128:(c + 1) * 128], ident[:])
                        return last
                    deps = [tg[jj]] + ([tev[jj - 2]] if jj >= 2 else [tok0])
                    ttr[jj] = ST(h, "tensor", gtr, deps=deps)

                def em_gev(jj):
                    p = jj % 2

                    def gev(sc, p=p):
                        sc.activation(out=lbuf2_[p][:, 0:512], in_=trpair[p][:, 0:512],
                                      func=AF.Copy)
                        return sc.activation(out=lbuf2_[p][:, 512:1024], in_=trpair[p][:, 512:1024],
                                             func=AF.Copy)
                    deps = [ttr[jj]] + ([tmm[jj - 2]] if jj >= 2 else [tok0])
                    tev[jj] = ST(h, "scalar", gev, deps=deps)

                def em_mix(jj):
                    p = jj % 2

                    def mixmm(t, p=p):
                        last = None
                        for c in range(8):
                            last = t.matmul(mixb[p][:], lbuf2_[p][:, c * 128:(c + 1) * 128],
                                            h1[:, c * TT:(c + 1) * TT], start=(c == 0), stop=(c == 7))
                        return last
                    deps = [tev[jj]] + ([tse[jj - 2]] if jj >= 2 else [tok0])
                    tmm[jj] = ST(h, "tensor", mixmm, deps=deps)

                def em_sev(jj):
                    p = jj % 2
                    deps = [tmm[jj]] + ([tb2[jj - 2]] if jj >= 2 else [tok0])
                    tse[jj] = ST(h, "vector",
                                 lambda v, p=p: v.tensor_copy(out=strip2_[p][:], in_=mixb[p][:]),
                                 deps=deps)

                def em_b2(jj):
                    p = jj % 2
                    j = jlo + jj

                    def b2mm(t, j=j, p=p, s2b=s2b):
                        last = None
                        for gpar in range(2):
                            g = j * 2 + gpar
                            base = gpar * 64
                            last = t.matmul(b2b[p][base:base + 64, 0:TT],
                                            s2b[base:base + 64, (g // 2) * 64:(g // 2) * 64 + 64],
                                            strip2_[p][base:base + 64, :], start=True, stop=True)
                        return last
                    deps = [tse[jj]] + ([tcm[jj - 2]] if jj >= 2 else [tok0])
                    tb2[jj] = ST(h, "tensor", b2mm, deps=deps)

                def em_comb(jj):
                    p = jj % 2
                    j = jlo + jj
                    if eslot == 0:
                        tcm[jj] = ST(h, "vector", lambda v, j=j, p=p, half=half: v.tensor_scalar(
                            out=h2[:, (j - half * 16) * TT:(j - half * 16 + 1) * TT], in0=b2b[p][:, 0:TT],
                            scalar1=coefb[:, 0:1], scalar2=None, op0=ALU.mult), deps=[tb2[jj]])
                    else:
                        tcm[jj] = ST(h, "vector", lambda v, j=j, p=p, half=half: v.scalar_tensor_tensor(
                            out=h2[:, (j - half * 16) * TT:(j - half * 16 + 1) * TT], in0=b2b[p][:, 0:TT],
                            scalar=coefb[:, 1:2],
                            in1=h2[:, (j - half * 16) * TT:(j - half * 16 + 1) * TT],
                            op0=ALU.mult, op1=ALU.add), deps=[tb2[jj]])

                for jj in range(18):
                    if jj < 16:
                        em_gth(jj)
                        em_gtr(jj)
                    if 1 <= jj <= 16:
                        em_gev(jj - 1)
                        em_mix(jj - 1)
                    if jj >= 2:
                        em_sev(jj - 2)
                        em_b2(jj - 2)
                        em_comb(jj - 2)

            def relu_(v):
                last = None
                for c in range(16):
                    last = v.tensor_scalar(out=h2[:, c * TT:(c + 1) * TT], in0=h2[:, c * TT:(c + 1) * TT],
                                           scalar1=0.0, scalar2=None, op0=ALU.max)
                return last
            ST(h, "vector", relu_)

            def sq_(sc):
                last = None
                for c in range(16):
                    last = sc.activation(out=h2[:, c * TT:(c + 1) * TT], in_=h2[:, c * TT:(c + 1) * TT],
                                         func=AF.Square)
                return last
            ST(h, "scalar", sq_)

            for eslot in range(2):
                s1b = pjs1 if eslot == 0 else pjs1b
                s2b = pjs2 if eslot == 0 else pjs2b
                rg = rgP0 if eslot == 0 else rgP1
                dcol = 2 + eslot

                for jg in range(4):
                    def pbd1(t, jg=jg, s1b=s1b, half=half):
                        last = None
                        for jj in range(4):
                            ch = jg * 4 + jj
                            gch = half * 16 + ch
                            for gpar in range(2):
                                g = gch * 2 + gpar
                                base = gpar * 64
                                last = t.matmul(pF[base:base + 64, jj * 512:(jj + 1) * 512],
                                                s1b[base:base + 64, (g // 2) * 64:(g // 2) * 64 + 64],
                                                h2[base:base + 64, ch * TT:(ch + 1) * TT],
                                                start=True, stop=True)
                        return last
                    ST(h, "tensor", pbd1)

                    def pbd1e(v, jg=jg):
                        last = None
                        for jj in range(4):
                            ch = jg * 4 + jj
                            last = v.tensor_copy(out=t1[:, ch * TT:(ch + 1) * TT],
                                                 in_=pF[:, jj * 512:(jj + 1) * 512])
                        return last
                    ST(h, "vector", pbd1e)

                ptok0 = h.tok()
                tpg, tpm, tpc = {}, {}, {}

                def em_pgth(ii):
                    p = ii % 2
                    gi_ = half * 16 + ii

                    def pgth(sync, gi_=gi_, p=p, eslot=eslot):
                        r0 = eslot * 4096 + gi_ * 128
                        sync.dma_start(out=gbuf2_[p][:], in_=io["pj_sel"][r0:r0 + 128, :]).then_inc(h.ld.h, 16)
                        h.ld.bump(16)
                        return None
                    deps = ([tpc[ii - 2]] if ii >= 2 else [ptok0]) + ([tpg[ii - 1]] if ii >= 1 else [])
                    ST(h, "sync", pgth, deps=deps)
                    tpg[ii] = h.tok()

                def em_pcast(ii):
                    p = ii % 2
                    deps = [tpg[ii]] + ([tpm[ii - 2]] if ii >= 2 else [ptok0])
                    tpc[ii] = ST(h, "vector", lambda v, p=p: v.tensor_copy(
                        out=gbufb2_[p][:], in_=gbuf2_[p][:]), deps=deps)

                def em_pmm(ii):
                    p = ii % 2

                    def pmm(t, ii=ii, p=p):
                        last = None
                        for o in range(8):
                            last = t.matmul(pjBanks[o], gbufb2_[p][:, o * 128:(o + 1) * 128],
                                            t1[:, ii * TT:(ii + 1) * TT],
                                            start=(ii == 0), stop=(ii == 15))
                        return last
                    deps = [tpc[ii]] + ([ptok0] if ii < 2 else [])
                    tpm[ii] = ST(h, "tensor", pmm, deps=deps)

                for ii in range(17):
                    if ii < 16:
                        em_pgth(ii)
                        em_pcast(ii)
                    if ii >= 1:
                        em_pmm(ii - 1)

                def pev(v):
                    last = None
                    for o in range(8):
                        last = v.tensor_copy(out=pmix[:, o * TT:(o + 1) * TT], in_=pjBanks[o])
                    return last
                ST(h, "vector", pev)

                for sub in range(4):
                    def pb2(t, sub=sub, s2b=s2b):
                        last = None
                        for g in range(16):
                            base = (g % 2) * 64
                            bank = pB2 if g % 2 == 0 else pTr
                            last = t.matmul(bank[:, (g // 2) * 64:(g // 2) * 64 + 64],
                                            pmix[base:base + 64, (g // 2) * TT + sub * 128:(g // 2) * TT + sub * 128 + 128],
                                            s2b[base:base + 64, (g // 2) * 64:(g // 2) * 64 + 64],
                                            start=True, stop=True)
                        return last
                    ST(h, "tensor", pb2)

                    def pacc(v, sub=sub, dcol=dcol):
                        yv = h0tm[:, sub * D:(sub + 1) * D].rearrange("p (a b k) -> p a b k", b=2, k=64)
                        ev = yv[:, :, 0:1, :]
                        ov = yv[:, :, 1:2, :]
                        pbv = pB2[:].rearrange("p (a k) -> p a k", k=64).unsqueeze(2)
                        ptv = pTr[:].rearrange("p (a k) -> p a k", k=64).unsqueeze(2)
                        v.scalar_tensor_tensor(out=ev, in0=pbv, scalar=coefb[:, dcol:dcol + 1],
                                               in1=ev, op0=ALU.mult, op1=ALU.add)
                        return v.scalar_tensor_tensor(out=ov, in0=ptv, scalar=coefb[:, dcol:dcol + 1],
                                                      in1=ov, op0=ALU.mult, op1=ALU.add)
                    ST(h, "vector", pacc)

        def ldx1(sync, t0=t0):
            for s in range(4):
                sync.dma_start(out=h0fm[:, s * D:(s + 1) * D],
                               in_=io["s1"][t0 + s * 128:t0 + (s + 1) * 128, :]).then_inc(h.ld.h, 16)
                h.ld.bump(16)
            return None
        ST(h, "sync", ldx1)
        ep0 = h.tok()
        st_toks = []
        for s in range(4):
            tr2a = ST(h, "vector", lambda v, s=s: v.tensor_tensor(
                out=h0tm[:, s * D:(s + 1) * D], in0=h0tm[:, s * D:(s + 1) * D],
                in1=xat[s][:], op=ALU.add), deps=[ep0])
            tr2 = ST(h, "vector", lambda v, s=s: v.tensor_tensor(
                out=h0tm[:, s * D:(s + 1) * D], in0=h0tm[:, s * D:(s + 1) * D],
                in1=h0fm[:, s * D:(s + 1) * D], op=ALU.add), deps=[tr2a])
            tln = emit_ln(h, h0tm[:, s * D:(s + 1) * D], h0tm[:, s * D:(s + 1) * D], trash, nfeat=D,
                          eps=1e-5, stats=stats, dep=tr2)

            def st2(sync, s=s, t0=t0):
                sync.dma_start(out=io["s2"][t0 + s * 128:t0 + (s + 1) * 128, :],
                               in_=h0tm[:, s * D:(s + 1) * D]).then_inc(h.st.h, 16)
                h.st.bump(16)
                sync.wait_ge(h.st.h, h.st.n)
                return sync.nop()
            st_toks.append(ST(h, "sync", st2, deps=[tln]))

    ST(h, "sync", lambda sync: sync.wait_ge(h.st.h, h.st.n) and None)



def emit_p3(h, io):
    ident = h.sb("v_ident", [128, 128])
    ones1 = h.sb("v_ones1", [1, 128])
    wtile = h.sb("v_wtile", [128, D])
    wr8 = h.sb("v_wr8", [128, 8])
    wr8t = h.sb("v_wr8t", [8, 128])
    rsrow = h.sb("v_rsrow", [1, D])
    rsbc = h.sb("v_rsbc", [128, D])
    x2tm = h.sb("v_x2tm", [128, 16 * D])
    x2bm = h.sb("v_x2bm", [128, 16 * D], BF16)
    x2fm = h.sb("v_x2fm", [128, 8 * T], BF16)
    scb2_ = [h.sb("v_scbA", [128, T]), h.sb("v_scbB", [128, T])]
    pmk2_ = [h.sb("v_pmkA", [128, T]), h.sb("v_pmkB", [128, T])]
    ptb2_ = [h.sb("v_ptbA", [128, T], BF16), h.sb("v_ptbB", [128, T], BF16)]
    att2_ = [h.sb("v_attA", [128, D]), h.sb("v_attB", [128, D])]
    xres2_ = [h.sb("v_xresA", [128, D]), h.sb("v_xresB", [128, D])]
    st3 = h.sb("v_st3", [128, 24])

    pBig = h.ps("v_pBig", [128, 2048])
    pV = h.ps("v_pV", [128, 1024])
    pT2 = h.ps("v_pT2", [128, 512])

    def loads(sync):
        sync.dma_start(out=ident[:], in_=io["ident"][:]).then_inc(h.ld.h, 16)
        h.ld.bump(16)
        return None
    ST(h, "sync", loads)
    ST(h, "vector", lambda v: v.memset(ones1[:], 1.0))

    for t_ in range(8):
        def ldw(sync, t_=t_):
            sync.dma_start(out=wtile[:], in_=io["W1"][t_ * 128:(t_ + 1) * 128, :]).then_inc(h.ld.h, 16)
            h.ld.bump(16)
            return None
        ST(h, "sync", ldw)
        ST(h, "vector", lambda v, t_=t_: v.tensor_reduce(
            out=wr8[:, t_:t_ + 1], in_=wtile[:], axis=AX.X, op=ALU.add))
    ST(h, "tensor", lambda t: t.transpose(pT2[0:8, 0:128], wr8[:], ident[:]))
    ST(h, "vector", lambda v: v.tensor_copy(out=wr8t[:], in_=pT2[0:8, 0:128]))

    def wout(sync):
        sync.dma_start(out=io["rs_scr"][:], in_=wr8t[:]).then_inc(h.st.h, 16)
        h.st.bump(16)
        sync.wait_ge(h.st.h, h.st.n)
        return sync.nop()
    ST(h, "sync", wout)

    def win(sync):
        sync.dma_start(out=rsrow[:], in_=io["rs_scr"].rearrange("a b -> (a b)").unsqueeze(0)).then_inc(h.ld.h, 16)
        h.ld.bump(16)
        return None
    ST(h, "sync", win)

    def rbc(t):
        t.matmul(pT2[:, 0:512], ones1[:], rsrow[:, 0:512], start=True, stop=True)
        return t.matmul(pV[:, 0:512], ones1[:], rsrow[:, 512:1024], start=True, stop=True)
    ST(h, "tensor", rbc)

    def rbc2(v):
        v.tensor_copy(out=rsbc[:, 0:512], in_=pT2[:, 0:512])
        return v.tensor_copy(out=rsbc[:, 512:1024], in_=pV[:, 0:512])
    ST(h, "vector", rbc2)

    for seq in range(2):
        base_t = seq * T
        for ti in range(16):
            def ldx(sync, ti=ti, base_t=base_t):
                sync.dma_start(out=x2tm[:, ti * D:(ti + 1) * D],
                               in_=io["s2"][base_t + ti * 128:base_t + (ti + 1) * 128, :]).then_inc(h.ld.h, 16)
                h.ld.bump(16)
                return None
            ST(h, "sync", ldx)
            tm_to_fm(h, [x2tm[:, ti * D:(ti + 1) * D]], x2fm,
                     pBig[:, 0:512], pBig[:, 512:1024], ident, fm_stride=T, sub0=ti)

        def xcast(v):
            last = None
            for c in range(4):
                last = v.tensor_copy(out=x2bm[:, c * 4096:(c + 1) * 4096],
                                     in_=x2tm[:, c * 4096:(c + 1) * 4096])
            return last
        ST(h, "vector", xcast)

        tok0 = h.tok()
        t_pmk, t_ssum, t_zop, t_trlast, t_pelast, t_av, t_attev, t_st3 = {}, {}, {}, {}, {}, {}, {}, {}

        def front(qi, base_t=base_t):
            p = qi % 2
            b = p * 12
            pair = pBig[:, p * 1024:(p + 1) * 1024]
            t_prev = None
            for r in range(2):
                def smm(t, qi=qi, r=r, pair=pair):
                    last = None
                    for kb in range(2):
                        for c in range(8):
                            last = t.matmul(pair[:, kb * 512:(kb + 1) * 512],
                                            x2fm[:, c * T + qi * 128:c * T + qi * 128 + 128],
                                            x2fm[:, c * T + (r * 2 + kb) * 512:c * T + (r * 2 + kb + 1) * 512],
                                            start=(c == 0), stop=(c == 7))
                    return last
                deps = [t_prev] if t_prev is not None else (
                    [t_pelast[qi - 2]] if qi >= 2 else [tok0])
                tsc = ST(h, "tensor", smm, deps=deps)

                def sev(v, r=r, p=p, pair=pair):
                    v.tensor_scalar(out=scb2_[p][:, r * 1024:r * 1024 + 512], in0=pair[:, 0:512],
                                    scalar1=0.03125, scalar2=None, op0=ALU.mult)
                    return v.tensor_scalar(out=scb2_[p][:, r * 1024 + 512:(r + 1) * 1024],
                                           in0=pair[:, 512:1024],
                                           scalar1=0.03125, scalar2=None, op0=ALU.mult)
                t_prev = ST(h, "vector", sev, deps=[tsc])

            tmax = ST(h, "vector", lambda v, p=p, b=b: v.tensor_reduce(
                out=st3[:, b + 0:b + 1], in_=scb2_[p][:], axis=AX.X, op=ALU.max, negate=True),
                deps=[t_prev])
            texp = ST(h, "scalar", lambda sc, p=p, b=b: sc.activation(
                out=scb2_[p][:], in_=scb2_[p][:], func=AF.Exp,
                bias=st3[:, b + 0:b + 1], accum_out=st3[:, b + 1:b + 2]), deps=[tmax])

            def zop(v, b=b):
                v.reciprocal(out=st3[:, b + 2:b + 3], in_=st3[:, b + 1:b + 2])
                return v.tensor_scalar(out=st3[:, b + 3:b + 4], in0=st3[:, b + 1:b + 2],
                                       scalar1=0.01, scalar2=None, op0=ALU.mult)
            t_zop[qi] = ST(h, "vector", zop, deps=[texp])

            depsm = [t_zop[qi]] + ([t_trlast[qi - 2]] if qi >= 2 else [])
            t_pmk[qi] = ST(h, "vector", lambda v, p=p, b=b: v.scalar_tensor_tensor(
                out=pmk2_[p][:], in0=scb2_[p][:], scalar=st3[:, b + 3:b + 4], in1=scb2_[p][:],
                op0=ALU.is_ge, op1=ALU.mult), deps=depsm)
            t_ssum[qi] = ST(h, "vector", lambda v, p=p, b=b: v.tensor_reduce(
                out=st3[:, b + 4:b + 5], in_=pmk2_[p][:], axis=AX.X, op=ALU.add), deps=[t_pmk[qi]])

        def back(qi, base_t=base_t):
            p = qi % 2
            b = p * 12
            pair = pBig[:, p * 1024:(p + 1) * 1024]
            t_prev = t_pmk[qi]
            for r in range(2):
                def ptr(t, r=r, p=p, pair=pair):
                    last = None
                    for b_ in range(8):
                        ks = r * 8 + b_
                        last = t.transpose(pair[:, b_ * 128:(b_ + 1) * 128],
                                           pmk2_[p][:, ks * 128:(ks + 1) * 128], ident[:])
                    return last
                ttr = ST(h, "tensor", ptr, deps=[t_prev])

                def pev(sc, r=r, p=p, pair=pair):
                    sc.activation(out=ptb2_[p][:, r * 1024:r * 1024 + 512],
                                  in_=pair[:, 0:512], func=AF.Copy)
                    return sc.activation(out=ptb2_[p][:, r * 1024 + 512:(r + 1) * 1024],
                                         in_=pair[:, 512:1024], func=AF.Copy)
                depsv = [ttr] + ([t_av[qi - 2]] if qi >= 2 and r == 0 else [])
                t_prev = ST(h, "scalar", pev, deps=depsv)
            t_trlast[qi] = t_prev
            t_pelast[qi] = t_prev

            def avmm(t, p=p):
                last = None
                for half in range(2):
                    for ks in range(16):
                        last = t.matmul(pV[:, half * 512:(half + 1) * 512],
                                        ptb2_[p][:, ks * 128:(ks + 1) * 128],
                                        x2bm[:, ks * D + half * 512:ks * D + (half + 1) * 512],
                                        start=(ks == 0), stop=(ks == 15))
                return last
            depsav = [t_prev] + ([t_attev[qi - 1]] if qi >= 1 else [])
            t_av[qi] = ST(h, "tensor", avmm, deps=depsav)

            def sfr(v, b=b):
                return v.tensor_tensor(out=st3[:, b + 5:b + 6], in0=st3[:, b + 4:b + 5],
                                       in1=st3[:, b + 2:b + 3], op=ALU.mult)
            tsf = ST(h, "vector", sfr, deps=[t_ssum[qi]])

            def den(v, b=b):
                v.tensor_scalar(out=st3[:, b + 6:b + 7], in0=st3[:, b + 5:b + 6],
                                scalar1=1e-9, scalar2=None, op0=ALU.add)
                return v.tensor_scalar(out=st3[:, b + 7:b + 8], in0=st3[:, b + 5:b + 6],
                                       scalar1=-0.01, scalar2=0.01, op0=ALU.mult, op1=ALU.add)
            tdn = ST(h, "vector", den, deps=[tsf])
            tdi = ST(h, "vector", lambda v, b=b: v.reciprocal(
                out=st3[:, b + 6:b + 7], in_=st3[:, b + 6:b + 7]), deps=[tdn])
            tsc2 = ST(h, "vector", lambda v, b=b: v.tensor_tensor(
                out=st3[:, b + 6:b + 7], in0=st3[:, b + 6:b + 7], in1=st3[:, b + 2:b + 3],
                op=ALU.mult), deps=[tdi])

            def attev(v, p=p, b=b):
                v.tensor_scalar(out=att2_[p][:, 0:512], in0=pV[:, 0:512],
                                scalar1=st3[:, b + 6:b + 7], scalar2=None, op0=ALU.mult)
                return v.tensor_scalar(out=att2_[p][:, 512:1024], in0=pV[:, 512:1024],
                                       scalar1=st3[:, b + 6:b + 7], scalar2=None, op0=ALU.mult)
            t_attev[qi] = ST(h, "vector", attev, deps=[t_av[qi], tsc2])

            tcas = ST(h, "vector", lambda v, p=p, b=b: v.scalar_tensor_tensor(
                out=att2_[p][:], in0=rsbc[:], scalar=st3[:, b + 7:b + 8], in1=att2_[p][:],
                op0=ALU.mult, op1=ALU.add), deps=[t_attev[qi]])

            depsr = [tcas] + ([t_st3[qi - 1]] if qi >= 1 else [])
            tres = ST(h, "vector", lambda v, qi=qi, p=p: v.tensor_tensor(
                out=xres2_[p][:], in0=att2_[p][:], in1=x2tm[:, qi * D:(qi + 1) * D],
                op=ALU.add), deps=depsr)

            tln = emit_ln(h, xres2_[p][:], xres2_[p][:], xres2_[1 - p][:], nfeat=D, eps=1e-5,
                          stats=st3[:, 16:24], dep=tres)

            def st3f(sync, qi=qi, p=p, base_t=base_t):
                sync.dma_start(out=io["s3"][base_t + qi * 128:base_t + (qi + 1) * 128, :],
                               in_=xres2_[p][:]).then_inc(h.st.h, 16)
                h.st.bump(16)
                sync.wait_ge(h.st.h, h.st.n)
                return sync.nop()
            t_st3[qi] = ST(h, "sync", st3f, deps=[tln])

        for step in range(17):
            if step < 16:
                front(step)
            if step >= 1:
                back(step - 1)

    ST(h, "sync", lambda sync: sync.wait_ge(h.st.h, h.st.n) and None)



def emit_p4(h, io):
    ident = h.sb("t_ident", [128, 128])
    ones1 = h.sb("t_ones1", [1, 128])
    brow = h.sb("t_brow", [1, D])
    bbc = h.sb("t_bbc", [128, D])
    wrT = h.sb("t_wrT", [128, 4 * 512])
    wiT = h.sb("t_wiT", [128, 4 * 512])
    wiTn = h.sb("t_wiTn", [128, 4 * 512])
    abuf = h.sb("t_abuf", [128, 512])
    bbuf = h.sb("t_bbuf", [128, 512])
    tmpw = h.sb("t_tmpw", [128, 512])
    x3t = h.sb("t_x3t", [128, D])
    x3fm = h.sb("t_x3fm", [128, 8 * 128])
    yb = h.sb("t_yb", [128, D])
    stats = h.sb("t_stats", [128, 8])
    trash = h.sb("t_trash", [128, D])

    pR = h.ps("t_pR", [128, 512])
    pI = h.ps("t_pI", [128, 512])
    pT3 = h.ps("t_pT3", [128, 512])
    pT4 = h.ps("t_pT4", [128, 512])

    def loads(sync):
        sync.dma_start(out=ident[:], in_=io["ident"][:]).then_inc(h.ld.h, 16)
        sync.dma_start(out=brow[:], in_=io["bias_ri_row"][:]).then_inc(h.ld.h, 16)
        h.ld.bump(32)
        return None
    ST(h, "sync", loads)
    ST(h, "vector", lambda v: v.memset(ones1[:], 1.0))

    def bbc_mm(t):
        t.matmul(pR[:], ones1[:], brow[:, 0:512], start=True, stop=True)
        return t.matmul(pI[:], ones1[:], brow[:, 512:1024], start=True, stop=True)
    ST(h, "tensor", bbc_mm)

    def bbc_ev(v):
        v.tensor_copy(out=bbc[:, 0:512], in_=pR[:])
        return v.tensor_copy(out=bbc[:, 512:1024], in_=pI[:])
    ST(h, "vector", bbc_ev)

    for part, dst in [("R", wrT), ("I", wiT)]:
        for ch in range(4):
            for cc in range(4):
                def lab(sync, part=part, ch=ch, cc=cc):
                    sync.dma_start(out=abuf[:], in_=io[f"Abig{part}"][cc, ch * 128:(ch + 1) * 128, :]).then_inc(h.ld.h, 16)
                    sync.dma_start(out=bbuf[:], in_=io[f"Bbig{part}"][cc, ch * 128:(ch + 1) * 128, :]).then_inc(h.ld.h, 16)
                    h.ld.bump(32)
                    return None
                ST(h, "sync", lab)
                if cc == 0:
                    ST(h, "vector", lambda v, dst=dst, ch=ch: v.tensor_tensor(
                        out=dst[:, ch * 512:(ch + 1) * 512], in0=abuf[:], in1=bbuf[:], op=ALU.mult))
                else:
                    ST(h, "vector", lambda v: v.tensor_tensor(
                        out=tmpw[:], in0=abuf[:], in1=bbuf[:], op=ALU.mult))
                    ST(h, "vector", lambda v, dst=dst, ch=ch: v.tensor_tensor(
                        out=dst[:, ch * 512:(ch + 1) * 512], in0=dst[:, ch * 512:(ch + 1) * 512],
                        in1=tmpw[:], op=ALU.add))
    ST(h, "vector", lambda v: v.tensor_scalar(out=wiTn[:], in0=wiT[:], scalar1=-1.0,
                                              scalar2=None, op0=ALU.mult))

    for ti in range(NTOK // 128):
        t0 = ti * 128

        def ldx(sync, t0=t0):
            sync.dma_start(out=x3t[:], in_=io["s3"][t0:t0 + 128, :]).then_inc(h.ld.h, 16)
            h.ld.bump(16)
            return None
        ST(h, "sync", ldx)

        tm_to_fm(h, [x3t], x3fm, pT3, pT4, ident, fm_stride=128, sub0=0)

        def ttmm(t):
            last = None
            for i4 in range(4):
                t.matmul(pR[:], x3fm[:, i4 * 128:(i4 + 1) * 128], wrT[:, i4 * 512:(i4 + 1) * 512],
                         start=(i4 == 0), stop=False)
                t.matmul(pI[:], x3fm[:, i4 * 128:(i4 + 1) * 128], wiT[:, i4 * 512:(i4 + 1) * 512],
                         start=(i4 == 0), stop=False)
            for i4 in range(4):
                t.matmul(pR[:], x3fm[:, (4 + i4) * 128:(5 + i4) * 128], wiTn[:, i4 * 512:(i4 + 1) * 512],
                         start=False, stop=(i4 == 3))
                last = t.matmul(pI[:], x3fm[:, (4 + i4) * 128:(5 + i4) * 128], wrT[:, i4 * 512:(i4 + 1) * 512],
                                start=False, stop=(i4 == 3))
            return last
        ST(h, "tensor", ttmm)

        def yev(v):
            v.tensor_tensor(out=yb[:, 0:512], in0=pR[:], in1=bbc[:, 0:512], op=ALU.add)
            return v.tensor_tensor(out=yb[:, 512:1024], in0=pI[:], in1=bbc[:, 512:1024], op=ALU.add)
        ST(h, "vector", yev)
        ST(h, "vector", lambda v: v.tensor_tensor(out=yb[:], in0=yb[:], in1=x3t[:], op=ALU.add))
        emit_ln(h, yb[:], yb[:], trash[:], nfeat=D, eps=1e-5, stats=stats)

        def st4(sync, t0=t0):
            sync.dma_start(out=io["out"][t0:t0 + 128, :], in_=yb[:]).then_inc(h.st.h, 16)
            h.st.bump(16)
            sync.wait_ge(h.st.h, h.st.n)
            return sync.nop()
        ST(h, "sync", st4)

    ST(h, "sync", lambda sync: sync.wait_ge(h.st.h, h.st.n) and None)


def build_p4():
    nc = bass.Bass("TRN2", target_bir_lowering=False)
    with ExitStack() as ctx:
        io = {}
        io["s3"] = nc.dram_tensor("s3", [NTOK, D], F32, kind="ExternalInput").ap()
        for nm, shape in [("AbigR", [4, 512, 512]), ("BbigR", [4, 512, 512]),
                          ("AbigI", [4, 512, 512]), ("BbigI", [4, 512, 512]),
                          ("bias_ri_row", [1, D]), ("ident", [128, 128])]:
            io[nm] = nc.dram_tensor(nm, shape, F32, kind="ExternalInput").ap()
        io["out"] = nc.dram_tensor("out", [NTOK, D], F32, kind="ExternalOutput").ap()
        block = ctx.enter_context(nc.Block())
        h = H(nc, ctx, block)
        emit_p4(h, io)
    return nc



def build_p0a(variant):
    nc = bass.Bass("TRN2", target_bir_lowering=False)
    with ExitStack() as ctx:
        io = {}
        io["x0"] = nc.dram_tensor("x0", [NTOK, D], F32, kind="ExternalInput").ap()
        for nm, shape in [("wx1T", [D, 160]), ("w2nT", [5, 32, D]), ("mu_x", [D, 1]),
                          ("x_biasT", [D, 5]), ("ident", [128, 128])]:
            io[nm] = nc.dram_tensor(nm, shape, F32, kind="ExternalInput").ap()
        if variant == "A":
            for nm, shape in [("wrT", [D, 512]), ("wkT", [D, 512]), ("ww1T", [D, 64]),
                              ("ww2T", [64, 512]), ("bw2", [512, 1])]:
                io[nm] = nc.dram_tensor(nm, shape, F32, kind="ExternalInput").ap()
            for nm in ["rsc", "ksc", "wsc"]:
                io[nm] = nc.dram_tensor(nm, [512, NTOK], F32, kind="ExternalOutput").ap()
        else:
            for nm, shape in [("wvT", [D, D]), ("wgT", [D, D])]:
                io[nm] = nc.dram_tensor(nm, shape, F32, kind="ExternalInput").ap()
            for nm in ["vsc", "gsc"]:
                io[nm] = nc.dram_tensor(nm, [NTOK, D], F32, kind="ExternalOutput").ap()
        block = ctx.enter_context(nc.Block())
        h = H(nc, ctx, block)
        emit_p0a(h, io, variant)
    return nc


def build_p0b(debug=False):
    nc = bass.Bass("TRN2", target_bir_lowering=False)
    with ExitStack() as ctx:
        io = {}
        io["x0"] = nc.dram_tensor("x0", [NTOK, D], F32, kind="ExternalInput").ap()
        for nm, shape in [("woT", [D, D]), ("bonusT", [128, 4]), ("ident", [128, 128]),
                          ("mask64", [64, 64]), ("ln_w", [1, 256]), ("ln_b", [1, 256])]:
            io[nm] = nc.dram_tensor(nm, shape, F32, kind="ExternalInput").ap()
        for nm in ["rsc", "ksc", "wsc"]:
            io[nm] = nc.dram_tensor(nm, [512, NTOK], F32, kind="ExternalInput").ap()
        for nm in ["vsc", "gsc"]:
            io[nm] = nc.dram_tensor(nm, [NTOK, D], F32, kind="ExternalInput").ap()
        io["s1"] = nc.dram_tensor("s1", [NTOK, D], F32, kind="ExternalOutput").ap()
        if debug:
            io["d_buf"] = nc.dram_tensor("d_buf", [128, 1024], F32, kind="ExternalOutput").ap()
            io["d_otm"] = nc.dram_tensor("d_otm", [512, D], F32, kind="ExternalOutput").ap()
        block = ctx.enter_context(nc.Block())
        h = H(nc, ctx, block)
        emit_p0b(h, io, debug=debug)
    return nc


def build_p3():
    nc = bass.Bass("TRN2", target_bir_lowering=False)
    with ExitStack() as ctx:
        io = {}
        io["s2"] = nc.dram_tensor("s2", [NTOK, D], F32, kind="ExternalInput").ap()
        io["W1"] = nc.dram_tensor("W1", [D, D], F32, kind="ExternalInput").ap()
        io["ident"] = nc.dram_tensor("ident", [128, 128], F32, kind="ExternalInput").ap()
        io["s3"] = nc.dram_tensor("s3", [NTOK, D], F32, kind="ExternalOutput").ap()
        io["rs_scr"] = nc.dram_tensor("rs_scr", [8, 128], F32).ap()
        block = ctx.enter_context(nc.Block())
        h = H(nc, ctx, block)
        emit_p3(h, io)
    return nc


def build_p2():
    nc = bass.Bass("TRN2", target_bir_lowering=False)
    with ExitStack() as ctx:
        io = {}
        io["xa"] = nc.dram_tensor("xa", [NTOK, D], F32, kind="ExternalInput").ap()
        io["s1"] = nc.dram_tensor("s1", [NTOK, D], F32, kind="ExternalInput").ap()
        for nm, shape in [("ident", [128, 128]), ("fc_logits_row", [1, 8]), ("pj_logits_row", [1, 8]),
                          ("fc_mixB", [BANK, 32, 128, D]), ("pj_mixTB", [BANK, 32, 128, D]),
                          ("fc_s1f", [BANK * D, BLK]), ("fc_s2f", [BANK * HID, BLK]),
                          ("pj_s1f", [BANK * HID, BLK]), ("pj_s2f", [BANK * D, BLK])]:
            io[nm] = nc.dram_tensor(nm, shape, F32, kind="ExternalInput").ap()
        io["s2"] = nc.dram_tensor("s2", [NTOK, D], F32, kind="ExternalOutput").ap()
        io["fc_sel"] = nc.dram_tensor("fc_sel", [2 * 4096, D], F32).ap()
        io["pj_sel"] = nc.dram_tensor("pj_sel", [2 * 4096, D], F32).ap()
        block = ctx.enter_context(nc.Block())
        h = H(nc, ctx, block)
        emit_p2(h, io)
    return nc


def build_p1():
    nc = bass.Bass("TRN2", target_bir_lowering=False)
    with ExitStack() as ctx:
        io = {}
        io["s1"] = nc.dram_tensor("s1", [NTOK, D], F32, kind="ExternalInput").ap()
        for nm, shape in [("wqT2", [D, D]), ("wkT2", [D, 256]), ("wvT2", [D, 256]),
                          ("woT2", [D, D]), ("rope_cos", [T, 32]), ("rope_sin", [T, 32]),
                          ("q_gain_row", [1, H0]), ("k_gain_row", [1, HKV]),
                          ("ident", [128, 128]), ("tri_neg", [128, 128])]:
            io[nm] = nc.dram_tensor(nm, shape, F32, kind="ExternalInput").ap()
        io["xa"] = nc.dram_tensor("xa", [NTOK, D], F32, kind="ExternalOutput").ap()
        block = ctx.enter_context(nc.Block())
        h = H(nc, ctx, block)
        emit_p1(h, io)
    return nc



def build_fused():
    nc = bass.Bass("TRN2", target_bir_lowering=False)
    with ExitStack() as ctx:
        io = {}
        io["x0"] = nc.dram_tensor("x0", [NTOK, D], F32, kind="ExternalInput").ap()
        ext = [("wx1T", [D, 160]), ("w2nT", [5, 32, D]), ("mu_x", [D, 1]),
               ("x_biasT", [D, 5]), ("ident", [128, 128]),
               ("wrT", [D, 512]), ("wkT", [D, 512]), ("ww1T", [D, 64]),
               ("ww2T", [64, 512]), ("bw2", [512, 1]),
               ("wvT", [D, D]), ("wgT", [D, D]),
               ("woT", [D, D]), ("bonusT", [128, 4]), ("mask64", [64, 64]),
               ("ln_w", [1, 256]), ("ln_b", [1, 256]),
               ("wqT2", [D, D]), ("wkT2", [D, 256]), ("wvT2", [D, 256]),
               ("woT2", [D, D]), ("rope_cos", [T, 32]), ("rope_sin", [T, 32]),
               ("q_gain_row", [1, H0]), ("k_gain_row", [1, HKV]), ("tri_neg", [128, 128]),
               ("fc_logits_row", [1, 8]), ("pj_logits_row", [1, 8]),
               ("fc_mixB", [BANK, 32, 128, D]), ("pj_mixTB", [BANK, 32, 128, D]),
               ("fc_s1f", [BANK * D, BLK]), ("fc_s2f", [BANK * HID, BLK]),
               ("pj_s1f", [BANK * HID, BLK]), ("pj_s2f", [BANK * D, BLK]),
               ("W1", [D, D]),
               ("AbigR", [4, 512, 512]), ("BbigR", [4, 512, 512]),
               ("AbigI", [4, 512, 512]), ("BbigI", [4, 512, 512]),
               ("bias_ri_row", [1, D])]
        for nm, shape in ext:
            io[nm] = nc.dram_tensor(nm, shape, F32, kind="ExternalInput").ap()
        for nm, shape in [("rsc", [512, NTOK]), ("ksc", [512, NTOK]), ("wsc", [512, NTOK]),
                          ("vsc", [NTOK, D]), ("gsc", [NTOK, D]),
                          ("s1", [NTOK, D]), ("xa", [NTOK, D]), ("s2", [NTOK, D]),
                          ("s3", [NTOK, D]), ("fc_sel", [2 * 4096, D]),
                          ("pj_sel", [2 * 4096, D]), ("rs_scr", [8, 128])]:
            io[nm] = nc.dram_tensor(nm, shape, F32).ap()
        io["out"] = nc.dram_tensor("out", [NTOK, D], F32, kind="ExternalOutput").ap()

        block = ctx.enter_context(nc.Block())
        h = H(nc, ctx, block)
        phases = [("A_", lambda: emit_p0a(h, io, "A")),
                  ("B_", lambda: emit_p0a(h, io, "B")),
                  ("C_", lambda: emit_p0b(h, io)),
                  ("D_", lambda: emit_p1(h, io)),
                  ("E_", lambda: emit_p2(h, io)),
                  ("F_", lambda: emit_p3(h, io)),
                  ("G_", lambda: emit_p4(h, io))]
        outer_ctx = h.ctx
        for pfx, fn in phases:
            with ExitStack() as pctx:
                h.ctx = pctx
                h.pfx = pfx
                fn()
            h.ctx = outer_ctx
        h.pfx = ""
    return nc


_BUILD_CACHE = {}


def _get(name, builder):
    if name not in _BUILD_CACHE:
        _BUILD_CACHE[name] = builder()
    return _BUILD_CACHE[name]


def _kernel_unfused(**inputs):
    g = _prep(inputs)
    x0 = np.asarray(inputs["L0_attn_x"], dtype=np.float32)
    xs = [np.ascontiguousarray(x0[2 * c:2 * c + 2].reshape(NTOK, D)) for c in range(NCORES)]
    cores = list(range(NCORES))

    def run(nc, per_core):
        res = run_bass_kernel_spmd(nc, per_core, core_ids=cores)
        return res.results

    base = {"wx1T": g["wx1T"], "w2nT": g["w2nT"], "mu_x": g["mu_x"],
            "x_biasT": g["x_biasT"], "ident": g["ident"]}
    imA = [dict(base, x0=xs[c], wrT=g["wrT"], wkT=g["wkT"], ww1T=g["ww1T"],
                ww2T=g["ww2T"], bw2=g["bw2"]) for c in cores]
    rA = run(_get("p0aA", lambda: build_p0a("A")), imA)
    imB = [dict(base, x0=xs[c], wvT=g["wvT"], wgT=g["wgT"]) for c in cores]
    rB = run(_get("p0aB", lambda: build_p0a("B")), imB)
    imC = [{"x0": xs[c], "woT": g["woT"], "bonusT": g["bonusT"], "ident": g["ident"],
            "mask64": g["mask64"], "ln_w": g["ln_w"], "ln_b": g["ln_b"],
            "rsc": rA[c]["rsc"], "ksc": rA[c]["ksc"], "wsc": rA[c]["wsc"],
            "vsc": rB[c]["vsc"], "gsc": rB[c]["gsc"]} for c in cores]
    rC = run(_get("p0b", lambda: build_p0b(False)), imC)
    imD = [{"s1": rC[c]["s1"], "wqT2": g["wqT2"], "wkT2": g["wkT2"], "wvT2": g["wvT2"],
            "woT2": g["woT2"], "rope_cos": g["rope_cos"], "rope_sin": g["rope_sin"],
            "q_gain_row": g["q_gain_row"], "k_gain_row": g["k_gain_row"],
            "ident": g["ident"], "tri_neg": g["tri_neg"]} for c in cores]
    rD = run(_get("p1", build_p1), imD)
    imE = [{"xa": rD[c]["xa"], "s1": rC[c]["s1"], "ident": g["ident"], "fc_logits_row": g["fc_logits_row"],
            "pj_logits_row": g["pj_logits_row"], "fc_mixB": g["fc_mixB"],
            "pj_mixTB": g["pj_mixTB"], "fc_s1f": g["fc_s1f"], "fc_s2f": g["fc_s2f"],
            "pj_s1f": g["pj_s1f"], "pj_s2f": g["pj_s2f"]} for c in cores]
    rE = run(_get("p2", build_p2), imE)
    W1 = np.ascontiguousarray(np.asarray(inputs["L1_attn_W"], dtype=np.float32))
    imF = [{"s2": rE[c]["s2"], "W1": W1, "ident": g["ident"]} for c in cores]
    rF = run(_get("p3", build_p3), imF)
    imG = [{"s3": rF[c]["s3"], "AbigR": g["AbigR"], "BbigR": g["BbigR"],
            "AbigI": g["AbigI"], "BbigI": g["BbigI"],
            "bias_ri_row": g["bias_ri_row"], "ident": g["ident"]} for c in cores]
    rG = run(_get("p4", build_p4), imG)

    out = np.stack([np.asarray(rG[c]["out"]) for c in cores])
    return out.reshape(16, T, D).astype(np.float32)


def kernel(**inputs):
    g = _prep(inputs)
    x0 = np.asarray(inputs["L0_attn_x"], dtype=np.float32)
    xs = [np.ascontiguousarray(x0[2 * c:2 * c + 2].reshape(NTOK, D)) for c in range(NCORES)]
    W1 = np.ascontiguousarray(np.asarray(inputs["L1_attn_W"], dtype=np.float32))
    nc = _get("fused", build_fused)
    names = ["wx1T", "w2nT", "mu_x", "x_biasT", "ident", "wrT", "wkT", "ww1T",
             "ww2T", "bw2", "wvT", "wgT", "woT", "bonusT", "mask64", "ln_w", "ln_b",
             "wqT2", "wkT2", "wvT2", "woT2", "rope_cos", "rope_sin",
             "q_gain_row", "k_gain_row", "tri_neg", "fc_logits_row", "pj_logits_row",
             "fc_mixB", "pj_mixTB", "fc_s1f", "fc_s2f", "pj_s1f", "pj_s2f",
             "AbigR", "BbigR", "AbigI", "BbigI", "bias_ri_row"]
    base = {nm: g[nm] for nm in names}
    base["W1"] = W1
    in_maps = [dict(base, x0=xs[c]) for c in range(NCORES)]
    res = run_bass_kernel_spmd(nc, in_maps, core_ids=list(range(NCORES)))
    out = np.stack([np.asarray(res.results[c]["out"]) for c in range(NCORES)])
    return out.reshape(16, T, D).astype(np.float32)
